# version 02
# baseline (speedup 1.0000x reference)
import numpy as np
import ml_dtypes
from contextlib import ExitStack

import concourse.bass as bass
import concourse.mybir as mybir
from concourse.bass_utils import run_bass_kernel_spmd

F32 = mybir.dt.float32
BF16 = mybir.dt.bfloat16
AF = mybir.ActivationFunctionType
ALU = mybir.AluOpType
AX = mybir.AxisListType
BF = ml_dtypes.bfloat16

MAGIC = float(np.float32(1.5 * 2 ** 23))
N_CORES = 8
SEQ = 4096
C = 1024
NT = SEQ // 128
CCH = 8
H = 16
DH = 64
NI = 2816
NIC = NI // 128
EPS = 1e-5

COMPUTE_SEMS = {"TE", "VE", "SE", "GE"}


def merge(*ds):
    out = {}
    for d in ds:
        for k, v in (d or {}).items():
            if v:
                out[k] = max(out.get(k, 0), v)
    return out


class Prog:
    ENGS = ("sync", "tensor", "vector", "scalar", "gpsimd")

    def __init__(self, nc, es):
        self.nc = nc
        self.sems = {}
        self.cnt = {}
        self.streams = {e: [] for e in self.ENGS}
        self.lastwait = {e: {} for e in self.ENGS}
        self.eng_of_sem = {"SD": "sync", "GD": "gpsimd", "TE": "tensor",
                           "VE": "vector", "SE": "scalar", "GE": "gpsimd"}
        for name in self.eng_of_sem:
            self.sems[name] = es.enter_context(nc.semaphore(name))
            self.cnt[name] = 0

    def add(self, eng, fn, wait=None, inc=None, inc_amt=1, keep_self=False):
        w = {}
        for s, v in (wait or {}).items():
            if not v:
                continue
            assert v <= self.cnt[s], f"wait {s}>={v} but only {self.cnt[s]} emitted"
            if (not keep_self) and s in COMPUTE_SEMS and self.eng_of_sem[s] == eng:
                continue
            if self.lastwait[eng].get(s, -1) < v:
                w[s] = v
                self.lastwait[eng][s] = v
        if inc:
            self.cnt[inc] += inc_amt
        self.streams[eng].append((w, fn, inc, inc_amt))
        return self.cnt[inc] if inc else None

    def dma(self, eng, out, in_, wait=None):
        sem = "SD" if eng == "sync" else "GD"
        return self.add(eng, lambda e, o=out, i=in_: e.dma_start(out=o, in_=i),
                        wait=wait, inc=sem, inc_amt=16)

    def fence(self):
        cur = dict(self.cnt)
        for eng in self.ENGS:
            self.add(eng, lambda e: None, wait=dict(cur))

    def flush(self, blk):
        for eng in self.ENGS:
            ops = self.streams[eng]
            self.streams[eng] = []
            if not ops:
                continue
            sems = self.sems

            def mk(ops_):
                def run(e):
                    for w, fn, inc, amt in ops_:
                        for s, v in w.items():
                            e.wait_ge(sems[s], v)
                        ins = fn(e)
                        if inc is not None:
                            ins.then_inc(sems[inc], amt)
                return run
            getattr(blk, eng)(mk(ops))


def build(debug=()):
    nc = bass.Bass("TRN2", target_bir_lowering=False)

    d_x = nc.dram_tensor("x", [SEQ, C], F32, kind="ExternalInput").ap()
    d_wqkv = nc.dram_tensor("wqkv", [C, 3 * C], F32, kind="ExternalInput").ap()
    d_wf = nc.dram_tensor("wf", [DH, DH], F32, kind="ExternalInput").ap()
    d_bf = nc.dram_tensor("bfv", [1, DH], F32, kind="ExternalInput").ap()
    d_wout = nc.dram_tensor("wout", [C, C], F32, kind="ExternalInput").ap()
    d_bout = nc.dram_tensor("bout", [1, C], F32, kind="ExternalInput").ap()
    d_clipx = nc.dram_tensor("clipx", [128, 1], F32, kind="ExternalInput").ap()
    d_clipd = nc.dram_tensor("clipd", [128, 1], F32, kind="ExternalInput").ap()
    d_wg = nc.dram_tensor("wg", [NI, C], F32, kind="ExternalInput").ap()
    d_wu = nc.dram_tensor("wu", [NI, C], F32, kind="ExternalInput").ap()
    d_wd = nc.dram_tensor("wd", [C, NI], F32, kind="ExternalInput").ap()
    d_ident = nc.dram_tensor("ident", [128, 128], BF16, kind="ExternalInput").ap()
    d_ones_col = nc.dram_tensor("ones_col", [128, 1], BF16, kind="ExternalInput").ap()
    d_ones_row = nc.dram_tensor("ones_row", [1, 1024], BF16, kind="ExternalInput").ap()
    d_ones_f32 = nc.dram_tensor("ones_f32", [1, 128], F32, kind="ExternalInput").ap()
    d_eps_row = nc.dram_tensor("eps_row", [1, 16], BF16, kind="ExternalInput").ap()
    d_ind = nc.dram_tensor("ind", [8, 16, 128], BF16, kind="ExternalInput").ap()
    d_constf = nc.dram_tensor("constf", [128, 2], F32, kind="ExternalInput").ap()

    d_out = nc.dram_tensor("out", [SEQ, C], F32, kind="ExternalOutput").ap()

    d_x1 = nc.dram_tensor("x1_scr", [SEQ, C], F32, kind="Internal").ap()
    d_h = nc.dram_tensor("h_scr", [NIC, 128, SEQ], F32, kind="Internal").ap()
    d_wqT = nc.dram_tensor("wqT_scr", [2, NIC, CCH, 128, 128], BF16, kind="Internal").ap()
    d_wdT = nc.dram_tensor("wdT_scr", [NIC, CCH, 128, 128], BF16, kind="Internal").ap()
    d_cc1_in = nc.dram_tensor("cc1_in", [128, 1], F32, kind="Internal").ap()
    d_cc1_out = nc.dram_tensor("cc1_out", [128, 1], F32, kind="Internal", addr_space="Shared").ap()
    d_cc2_in = nc.dram_tensor("cc2_in", [128, 1], F32, kind="Internal").ap()
    d_cc2_out = nc.dram_tensor("cc2_out", [128, 1], F32, kind="Internal", addr_space="Shared").ap()
    d_s128a = nc.dram_tensor("s128a", [128, 1], F32, kind="Internal").ap()
    d_s128b = nc.dram_tensor("s128b", [128, 1], F32, kind="Internal").ap()

    dbg = {}
    for name in debug:
        shp = {"x1": ([SEQ, C], F32), "q_fm": ([128, CCH * SEQ], BF16),
               "kvx": ([128, 8 * 128], BF16), "ks": ([128, 8 * 16], BF16),
               "xq": ([128, CCH * SEQ], BF16), "h0": ([128, SEQ], F32),
               "s1": ([128, 1], F32), "s2": ([128, 1], F32),
               "wall": ([128, CCH * 3 * C], BF16), "xnfm": ([128, CCH * 128], BF16),
               "ktok": ([128, C], BF16), "vtok": ([128, C], BF16),
               "qtok": ([128, C], BF16), "stat0": ([128, 16], F32)}[name]
        dt_ = BF16 if name in ("q_fm", "kvx", "ks", "xq", "wall", "xnfm", "ktok",
                               "vtok", "qtok") else F32
        dbg[name] = nc.dram_tensor("dbg_" + name, shp[0], dt_, kind="ExternalOutput").ap()

    es_all = ExitStack()
    with es_all:
        es = es_all
        ident = es.enter_context(nc.sbuf_tensor("identb", [128, 128], BF16))
        ones_col = es.enter_context(nc.sbuf_tensor("onescol", [128, 1], BF16))
        ones_row = es.enter_context(nc.sbuf_tensor("onesrow", [1, 1024], BF16))
        ones_f32 = es.enter_context(nc.sbuf_tensor("onesf32", [1, 128], F32))
        eps_row = es.enter_context(nc.sbuf_tensor("epsrow", [1, 16], BF16))
        ind_sb = es.enter_context(nc.sbuf_tensor("indsb", [16, 8 * 128], BF16))
        bias_row = es.enter_context(nc.sbuf_tensor("biasrow", [1, 3 * C], BF16))
        bf_bf = es.enter_context(nc.sbuf_tensor("bfbf", [1, DH], BF16))
        bout_row = es.enter_context(nc.sbuf_tensor("boutrow", [1, C], BF16))
        clipx_sb = es.enter_context(nc.sbuf_tensor("clipxsb", [128, 1], F32))
        clipd_sb = es.enter_context(nc.sbuf_tensor("clipdsb", [128, 1], F32))
        r2_all = es.enter_context(nc.sbuf_tensor("r2all", [128, NT], F32))
        nmr2_all = es.enter_context(nc.sbuf_tensor("nmr2all", [128, NT], F32))
        sg_all = es.enter_context(nc.sbuf_tensor("sgall", [128, NIC], F32))
        su_all = es.enter_context(nc.sbuf_tensor("suall", [128, NIC], F32))
        s1_col = es.enter_context(nc.sbuf_tensor("s1col", [128, 1], F32))
        rs1_col = es.enter_context(nc.sbuf_tensor("rs1col", [128, 1], F32))
        s2_col = es.enter_context(nc.sbuf_tensor("s2col", [128, 1], F32))
        rs2_col = es.enter_context(nc.sbuf_tensor("rs2col", [128, 1], F32))
        uscale_all = es.enter_context(nc.sbuf_tensor("uscaleall", [128, NT], F32))
        ubias_all = es.enter_context(nc.sbuf_tensor("ubiasall", [128, NT], F32))
        s1sg_all = es.enter_context(nc.sbuf_tensor("s1sgall", [128, NIC], F32))
        s1su_all = es.enter_context(nc.sbuf_tensor("s1suall", [128, NIC], F32))
        grow = es.enter_context(nc.sbuf_tensor("growbuf", [1, 128], F32))
        gcol = es.enter_context(nc.sbuf_tensor("gcolbuf", [1, 1], F32))
        gbc = es.enter_context(nc.sbuf_tensor("gbcbuf", [128, 1], F32))
        sig_col = es.enter_context(nc.sbuf_tensor("sigcol", [128, 1], F32))
        kvx_sb = es.enter_context(nc.sbuf_tensor("kvxsb", [128, 8 * 128], BF16))
        ks_sb = es.enter_context(nc.sbuf_tensor("kssb", [128, 8 * 16], BF16))
        xb = [es.enter_context(nc.sbuf_tensor(f"xb{i}", [128, C], F32)) for i in range(2)]
        x1b = [es.enter_context(nc.sbuf_tensor(f"x1b{i}", [128, C], F32)) for i in range(2)]
        sq = es.enter_context(nc.sbuf_tensor("sqscr", [128, C], F32))
        stat = es.enter_context(nc.sbuf_tensor("statscr", [128, 16], F32))
        constf = es.enter_context(nc.sbuf_tensor("constf_sb", [128, 2], F32))
        esBig = ExitStack()
        big = esBig.enter_context(nc.sbuf_tensor("big", [128, CCH * SEQ], BF16))

        blk = es.enter_context(nc.Block())
        p = Prog(nc, es)
        bigv = big[:].rearrange("p (c n) -> p c n", n=SEQ)

        xb_free = [{}, {}]
        x1b_free = [{}, {}]

        p.dma("sync", ident[:], d_ident)
        p.dma("sync", ones_col[:], d_ones_col)
        p.dma("sync", ones_row[:], d_ones_row)
        p.dma("sync", ones_f32[:], d_ones_f32)
        p.dma("sync", eps_row[:], d_eps_row)
        p.dma("sync", clipx_sb[:], d_clipx)
        p.dma("sync", clipd_sb[:], d_clipd)
        p.dma("sync", constf[:], d_constf)
        for g in range(8):
            p.dma("sync", ind_sb[:, g * 128:(g + 1) * 128], d_ind[g])
        sd_consts = p.cnt["SD"]
        p.dma("gpsimd", bf_bf[:], d_bf)
        p.dma("gpsimd", bout_row[:], d_bout)
        gd_consts = p.cnt["GD"]

        p.add("vector", lambda e: e.memset(bias_row[:], 0.0), inc="VE")
        for j in range(16):
            p.add("vector", lambda e, j=j: e.tensor_copy(
                bias_row[:, j * 64:(j + 1) * 64], bf_bf[:]),
                wait={"GD": gd_consts}, inc="VE")
            p.add("vector", lambda e, j=j: e.tensor_copy(
                bias_row[:, 2048 + j * 64: 2048 + (j + 1) * 64], bf_bf[:]), inc="VE")
        p.add("vector", lambda e: e.memset(stat[:, 10:14], 0.0), inc="VE")
        ve_consts = p.cnt["VE"]

        esA = ExitStack()
        if True:
            Wall = esA.enter_context(nc.sbuf_tensor("Wall", [128, CCH * 3 * C], BF16))
            wf_sb = esA.enter_context(nc.sbuf_tensor("wf_sb", [DH, DH], BF16))
            wTb = esA.enter_context(nc.sbuf_tensor("wTbuf", [64, 1024], BF16))
            xnfm0 = esA.enter_context(nc.sbuf_tensor("xnfm0", [128, CCH * 128], BF16))
            xnfm1 = esA.enter_context(nc.sbuf_tensor("xnfm1", [128, CCH * 128], BF16))
            xntok = esA.enter_context(nc.sbuf_tensor("xntok", [128, C], BF16))
            ktok0 = esA.enter_context(nc.sbuf_tensor("ktok0", [128, C], BF16))
            ktok1 = esA.enter_context(nc.sbuf_tensor("ktok1", [128, C], BF16))
            vtok0 = esA.enter_context(nc.sbuf_tensor("vtok0", [128, C], BF16))
            vtok1 = esA.enter_context(nc.sbuf_tensor("vtok1", [128, C], BF16))
            qtok0 = esA.enter_context(nc.sbuf_tensor("qtok0", [128, C], BF16))
            qtok1 = esA.enter_context(nc.sbuf_tensor("qtok1", [128, C], BF16))
            ptr6 = esA.enter_context(nc.psum_tensor("ptr6", [128, 1024], BF16))
            ptr7 = esA.enter_context(nc.psum_tensor("ptr7", [128, 1024], BF16))
            pkv = esA.enter_context(nc.psum_tensor("pkv", [128, 512], F32))
            pks = esA.enter_context(nc.psum_tensor("pks", [128, 512], F32))
            pa0 = esA.enter_context(nc.psum_tensor("pa0", [128, 512], F32))
            pa1 = esA.enter_context(nc.psum_tensor("pa1", [128, 512], F32))
            pa2 = esA.enter_context(nc.psum_tensor("pa2", [128, 512], F32))
            pa3 = esA.enter_context(nc.psum_tensor("pa3", [128, 512], F32))
            Wall_v = Wall[:].rearrange("p (c n) -> p c n", n=3 * C)
            pabank = [pa0, pa1, pa2, pa3]

            p.dma("gpsimd", wf_sb[:], d_wf)
            gd_wf = p.cnt["GD"]
            for cc in range(CCH):
                p.dma("gpsimd", Wall_v[:, cc, 1024:2048],
                      d_wqkv[cc * 128:(cc + 1) * 128, 2048:3072])
            gd_v = p.cnt["GD"]

            te_fold_tr_last = 0
            ve_fold = 0
            esW = ExitStack()
            if True:
                wstage = esW.enter_context(nc.sbuf_tensor("wstage", [128, CCH * 2048], BF16))
                wstage_v = wstage[:].rearrange("p (c n) -> p c n", n=2048)
                for cc in range(CCH):
                    p.dma("gpsimd", wstage_v[:, cc, 0:1024],
                          d_wqkv[cc * 128:(cc + 1) * 128, 0:1024])
                    p.dma("gpsimd", wstage_v[:, cc, 1024:2048],
                          d_wqkv[cc * 128:(cc + 1) * 128, 1024:2048])
                gd_stage = p.cnt["GD"]

                tr6_ve = 0
                mm_ve = [0, 0]
                wT_te = 0
                gidx = 0
                for src_off, dst_off in ((1024, 0), (0, 2048)):
                    for hh in range(H):
                        bk = gidx % 2
                        gidx += 1
                        te_tr = None
                        for cc in range(CCH):
                            te_tr = p.add("tensor", lambda e, cc=cc, so=src_off, hh=hh:
                                          e.transpose(ptr6[0:64, cc * 128:(cc + 1) * 128],
                                                      wstage_v[:, cc, so + hh * 64: so + (hh + 1) * 64],
                                                      ident[:]),
                                          wait={"GD": gd_stage, "SD": sd_consts,
                                                "VE": tr6_ve} if cc == 0 else None,
                                          inc="TE")
                        te_fold_tr_last = te_tr
                        v_wt = p.add("vector", lambda e: e.tensor_copy(
                            wTb[:], ptr6[0:64, :]),
                            wait={"TE": max(te_tr, wT_te)}, inc="VE")
                        tr6_ve = v_wt
                        te_mm = None
                        for cc in range(CCH):
                            te_mm = p.add("tensor", lambda e, cc=cc, bk=bk:
                                          e.matmul(pabank[bk][:, cc * 64:(cc + 1) * 64],
                                                   wTb[:, cc * 128:(cc + 1) * 128],
                                                   wf_sb[:], start=True, stop=True),
                                          wait={"VE": max(v_wt, mm_ve[bk]),
                                                "GD": gd_wf} if cc == 0 else None,
                                          inc="TE")
                        wT_te = te_mm
                        mm_ve[bk] = p.add(
                            "vector",
                            lambda e, bk=bk, do=dst_off, hh=hh:
                            e.tensor_copy(
                                Wall_v[:, :, do + hh * 64: do + (hh + 1) * 64],
                                pabank[bk][:, 0:512].rearrange("p (c n) -> p c n", n=64)),
                            wait={"TE": te_mm}, inc="VE")
                ve_fold = p.cnt["VE"]
                if "wall" in dbg:
                    p.dma("sync", dbg["wall"], Wall[:], wait={"VE": ve_fold, "GD": gd_v})
            esW.close()

            if True:
                wfb = esA.enter_context(nc.sbuf_tensor("wfb", [128, NI], F32))
                wub = esA.enter_context(nc.sbuf_tensor("wub", [128, NI // 2], F32))
                wt1b = esA.enter_context(nc.sbuf_tensor("wt1b", [128, NI // 2], F32))
                whatb = esA.enter_context(nc.sbuf_tensor("whatb", [128, NI], BF16))
                wstg0 = esA.enter_context(nc.sbuf_tensor("wstg0", [128, NI], BF16))
                wstat = esA.enter_context(nc.sbuf_tensor("wstat", [128, 4], F32))
                chunks = [("g", i) for i in range(NIC)] + [("u", i) for i in range(NIC)] + \
                         [("d", j) for j in range(CCH)]
                cp = {"ptr": 0, "wfb_free": {"TE": te_fold_tr_last}, "wub_free": {},
                      "what_free": {}, "wstg_free": {}, "wstat_free": {}}

                def emit_prep_chunk(tr_psum, tr_free):
                    if cp["ptr"] >= len(chunks):
                        return
                    kind, idx = chunks[cp["ptr"]]
                    cp["ptr"] += 1
                    n = C if kind in ("g", "u") else NI
                    nblk = n // 128
                    src = {"g": d_wg, "u": d_wu, "d": d_wd}[kind]
                    gdv = p.dma("gpsimd", wfb[:, 0:n],
                                src[idx * 128:(idx + 1) * 128, :], wait=cp["wfb_free"])
                    v1 = p.add("vector", lambda e, n=n: e.reduce_max(
                        wstat[:, 0:1], wfb[:, 0:n], AX.X, apply_absolute_value=True),
                        wait=merge({"GD": gdv}, cp["wstat_free"]), inc="VE")
                    v2 = p.add("vector", lambda e: e.tensor_scalar(
                        wstat[:, 1:2], wstat[:, 0:1], 1.0 / 127.0, None, ALU.mult),
                        wait={"VE": v1}, keep_self=True, inc="VE")
                    if kind == "g":
                        p.add("vector", lambda e, idx=idx: e.tensor_copy(
                            sg_all[:, idx:idx + 1], wstat[:, 1:2]),
                            wait={"VE": v2}, keep_self=True, inc="VE")
                    elif kind == "u":
                        p.add("vector", lambda e, idx=idx: e.tensor_copy(
                            su_all[:, idx:idx + 1], wstat[:, 1:2]),
                            wait={"VE": v2}, keep_self=True, inc="VE")
                    v3 = p.add("vector", lambda e: e.reciprocal(
                        wstat[:, 2:3], wstat[:, 1:2]),
                        wait={"VE": v2}, keep_self=True, inc="VE")
                    v6 = None
                    s1 = None
                    for p0 in range(0, n, NI // 2):
                        p1 = min(n, p0 + NI // 2)
                        w = p1 - p0
                        s1 = p.add("scalar", lambda e, p0=p0, p1=p1: e.activation(
                            wub[:, 0:p1 - p0], wfb[:, p0:p1], AF.Identity, bias=constf[:, 1:2],
                            scale=wstat[:, 2:3]),
                            wait=merge({"VE": v3}, cp["wub_free"]), inc="SE")
                        v4 = p.add("vector", lambda e, w=w: e.tensor_scalar(
                            wt1b[:, 0:w], wub[:, 0:w], MAGIC, 127.0, ALU.subtract, ALU.min),
                            wait={"SE": s1}, inc="VE")
                        cp["wub_free"] = {"VE": v4}
                        if kind == "d":
                            p.add("vector", lambda e, w=w: e.tensor_scalar(
                                wub[:, 0:w], wt1b[:, 0:w], -128.0, None, ALU.max), inc="VE")
                            v6 = p.add("vector", lambda e, p0=p0, w=w: e.tensor_scalar(
                                whatb[:, p0:p0 + w], wub[:, 0:w], wstat[:, 1:2], None,
                                ALU.mult), wait=cp["what_free"], inc="VE")
                            cp["wub_free"] = {"VE": v6}
                        else:
                            v6 = p.add("vector", lambda e, p0=p0, w=w: e.tensor_scalar(
                                whatb[:, p0:p0 + w], wt1b[:, 0:w], -128.0, None, ALU.max),
                                wait=cp["what_free"], inc="VE")
                    cp["wfb_free"] = {"SE": s1}
                    cp["wstat_free"] = {"SE": s1}
                    te_last = None
                    gdo = None
                    for r0 in range(0, nblk, 8):
                        rblks = min(8, nblk - r0)
                        for j in range(rblks):
                            te_last = p.add("tensor", lambda e, r0=r0, j=j:
                                            e.transpose(tr_psum[:, j * 128:(j + 1) * 128],
                                                        whatb[:, (r0 + j) * 128:(r0 + j + 1) * 128],
                                                        ident[:]),
                                            wait=merge({"VE": v6}, {"VE": tr_free[0]})
                                            if j == 0 else None, inc="TE")
                        ve_evac = p.add("vector", lambda e, r0=r0, rblks=rblks:
                                        e.tensor_copy(wstg0[:, r0 * 128:(r0 + rblks) * 128],
                                                      tr_psum[:, 0:rblks * 128]),
                                        wait=merge({"TE": te_last}, cp["wstg_free"]),
                                        inc="VE")
                        tr_free[0] = ve_evac
                    cp["what_free"] = {"TE": te_last}
                    if kind in ("g", "u"):
                        mi = 0 if kind == "g" else 1
                        gdo = p.dma("gpsimd", d_wqT[mi, idx].transpose([1, 0, 2]),
                                    wstg0[:, 0:C].rearrange("p (c b) -> p c b", b=128),
                                    wait={"VE": tr_free[0]})
                    else:
                        for ci in range(NIC):
                            gdo = p.dma("gpsimd", d_wdT[ci, idx],
                                        wstg0[:, ci * 128:(ci + 1) * 128],
                                        wait={"VE": tr_free[0]})
                    cp["wstg_free"] = {"GD": gdo}

                v_pkv0 = p.add("vector", lambda e: e.memset(pkv[:], 0.0), inc="VE")
                v_pks0 = p.add("vector", lambda e: e.memset(pks[:], 0.0), inc="VE")
                xnfm = [xnfm0, xnfm1]
                xnfm_v = [xnfm0[:].rearrange("p (c n) -> p c n", n=128),
                          xnfm1[:].rearrange("p (c n) -> p c n", n=128)]
                ktok = [ktok0, ktok1]
                vtok = [vtok0, vtok1]
                qtok = [qtok0, qtok1]
                xnfm_free = [0, 0]
                ktok_free = [0, 0]
                vtok_free = [0, 0]
                qtok_free = [0, 0]
                bank_free = [0, 0, 0, 0]
                tr6_free = [ve_fold]
                tr7_free = [0]
                xntok_free = [0]
                prev_v_var = 0
                kv_te_last = 0
                sd_x1_pb = [0] * NT

                for tt in range(NT):
                    bi = tt % 2
                    sdx = p.dma("sync", xb[bi][:], d_x[tt * 128:(tt + 1) * 128, :],
                                wait=xb_free[bi])
                    v_sum = p.add("vector", lambda e, bi=bi: e.reduce_sum(
                        stat[:, 0:1], xb[bi][:], AX.X), wait={"SD": sdx}, inc="VE")
                    s_sq = p.add("scalar", lambda e, bi=bi: e.activation(
                        sq[:], xb[bi][:], AF.Square, accum_out=stat[:, 1:2]),
                        wait={"SD": sdx, "VE": prev_v_var}, inc="SE")
                    v_mean = p.add("vector", lambda e: e.tensor_scalar(
                        stat[:, 2:3], stat[:, 0:1], 1.0 / C, None, ALU.mult),
                        wait={"SE": s_sq, "VE": v_sum}, keep_self=True, inc="VE")
                    v_m2 = p.add("vector", lambda e: e.tensor_scalar(
                        stat[:, 3:4], stat[:, 2:3], stat[:, 2:3], None, ALU.mult),
                        wait={"VE": v_mean}, keep_self=True, inc="VE")
                    v_var = p.add("vector", lambda e: e.tensor_scalar(
                        stat[:, 4:5], stat[:, 1:2], 1.0 / C, stat[:, 3:4],
                        ALU.mult, ALU.subtract),
                        wait={"VE": v_m2}, keep_self=True, inc="VE")
                    prev_v_var = v_var
                    s_sd = p.add("scalar", lambda e: e.activation(
                        stat[:, 5:6], stat[:, 4:5], AF.Sqrt, bias=constf[:, 0:1]),
                        wait={"VE": v_var}, inc="SE")
                    v_r = p.add("vector", lambda e: e.reciprocal(
                        stat[:, 6:7], stat[:, 5:6]), wait={"SE": s_sd}, inc="VE")
                    v_nmr = p.add("vector", lambda e: e.tensor_scalar(
                        stat[:, 7:8], stat[:, 2:3], stat[:, 6:7], -1.0,
                        ALU.mult, ALU.mult),
                        wait={"VE": v_r}, keep_self=True, inc="VE")
                    s_xn = p.add("scalar", lambda e, bi=bi: e.activation(
                        xntok[:], xb[bi][:], AF.Identity, bias=stat[:, 7:8], scale=stat[:, 6:7]),
                        wait={"VE": v_nmr, "TE": xntok_free[0]}, inc="SE")
                    xb_free[bi] = {"SE": s_xn}

                    te_tr = None
                    for j in range(8):
                        te_tr = p.add("tensor", lambda e, j=j: e.transpose(
                            ptr6[:, j * 128:(j + 1) * 128], xntok[:, j * 128:(j + 1) * 128],
                            ident[:]),
                            wait={"SE": s_xn, "VE": tr6_free[0]}
                            if j == 0 else None, inc="TE")
                    xntok_free[0] = te_tr
                    v_xnfm = p.add("vector", lambda e, bi=bi: e.tensor_copy(
                        xnfm[bi][:], ptr6[:, 0:1024]),
                        wait={"TE": max(te_tr, xnfm_free[bi])}, inc="VE")
                    tr6_free[0] = v_xnfm

                    grp_last_te = 0
                    for grp, bank, kind in ((0, 0, "k"), (1, 1, "k"), (2, 2, "v"),
                                            (3, 3, "v"), (4, 0, "q"), (5, 1, "q")):
                        te_g = None
                        for c in range(CCH):
                            last = (c == CCH - 1)
                            te_g = p.add(
                                "tensor",
                                lambda e, grp=grp, bank=bank, c=c, bi=bi, kind=kind:
                                e.matmul(pabank[bank][:, 0:512], xnfm_v[bi][:, c, :],
                                         Wall_v[:, c, grp * 512:(grp + 1) * 512],
                                         start=(c == 0),
                                         stop=(last and kind == "v")),
                                wait=merge({"VE": max(v_xnfm, ve_fold)},
                                           {"SE": bank_free[bank]},
                                           {"GD": gd_v}) if c == 0 else None,
                                inc="TE")
                        if kind != "v":
                            te_g = p.add("tensor", lambda e, grp=grp, bank=bank:
                                         e.matmul(pabank[bank][:, 0:512],
                                                  ones_row[0:1, 0:128],
                                                  bias_row[0:1, grp * 512:(grp + 1) * 512],
                                                  start=False, stop=True),
                                         wait={"VE": ve_consts}, inc="TE")
                        grp_last_te = te_g
                        if kind == "k":
                            se_e = p.add("scalar", lambda e, grp=grp, bank=bank, bi=bi:
                                         e.activation(ktok[bi][:, grp * 512:(grp + 1) * 512],
                                                      pabank[bank][:, 0:512], AF.Relu),
                                         wait={"TE": max(te_g, ktok_free[bi])}, inc="SE")
                        elif kind == "v":
                            se_e = p.add("scalar", lambda e, grp=grp, bank=bank, bi=bi:
                                         e.activation(vtok[bi][:, (grp - 2) * 512:(grp - 1) * 512],
                                                      pabank[bank][:, 0:512], AF.Identity),
                                         wait={"TE": max(te_g, vtok_free[bi])}, inc="SE")
                        else:
                            se_e = p.add("scalar", lambda e, grp=grp, bank=bank, bi=bi:
                                         e.activation(qtok[bi][:, (grp - 4) * 512:(grp - 3) * 512],
                                                      pabank[bank][:, 0:512], AF.Relu),
                                         wait={"TE": max(te_g, qtok_free[bi])}, inc="SE")
                        bank_free[bank] = se_e
                    se_kv_ready = max(bank_free)

                    te_kv = None
                    for hh in range(H):
                        pr = hh // 2
                        po = (hh % 2) * 64
                        te_kv = p.add("tensor", lambda e, hh=hh, pr=pr, po=po, bi=bi, tt=tt:
                                      e.matmul(pkv[po:po + 64, pr * 64:(pr + 1) * 64],
                                               ktok[bi][:, hh * 64:(hh + 1) * 64],
                                               vtok[bi][:, hh * 64:(hh + 1) * 64],
                                               start=False,
                                               stop=(tt == NT - 1 and hh == H - 1),
                                               skip_group_check=True),
                                      wait={"SE": se_kv_ready, "VE": v_pkv0}
                                      if hh == 0 else None, inc="TE")
                        te_kv = p.add("tensor", lambda e, hh=hh, po=po, bi=bi, tt=tt:
                                      e.matmul(pks[po:po + 64, hh:hh + 1],
                                               ktok[bi][:, hh * 64:(hh + 1) * 64],
                                               ones_col[:],
                                               start=False,
                                               stop=(tt == NT - 1 and hh == H - 1),
                                               skip_group_check=True),
                                      inc="TE")
                    ktok_free[bi] = te_kv
                    vtok_free[bi] = te_kv
                    kv_te_last = te_kv

                    te_qt = None
                    for j in range(8):
                        te_qt = p.add("tensor", lambda e, j=j, bi=bi: e.transpose(
                            ptr7[:, j * 128:(j + 1) * 128], qtok[bi][:, j * 128:(j + 1) * 128],
                            ident[:]),
                            wait={"SE": bank_free[1], "VE": tr7_free[0]}
                            if j == 0 else None, inc="TE")
                    qtok_free[bi] = te_qt
                    v_qfm = p.add("vector", lambda e, tt=tt: e.tensor_copy(
                        bigv[:, :, tt * 128:(tt + 1) * 128], ptr7[:, 0:1024]),
                        wait={"TE": te_qt}, inc="VE")
                    tr7_free[0] = v_qfm
                    xnfm_free[bi] = grp_last_te

                    if tt == 0:
                        if "xnfm" in dbg:
                            p.dma("sync", dbg["xnfm"], xnfm0[:], wait={"VE": v_xnfm})
                        if "stat0" in dbg:
                            p.dma("sync", dbg["stat0"], stat[:], wait={"VE": v_nmr})
                        if "ktok" in dbg:
                            p.dma("sync", dbg["ktok"], ktok0[:], wait={"SE": bank_free[1]})
                        if "vtok" in dbg:
                            p.dma("sync", dbg["vtok"], vtok0[:], wait={"SE": bank_free[3]})
                        if "qtok" in dbg:
                            p.dma("sync", dbg["qtok"], qtok0[:], wait={"SE": bank_free[1]})

                    emit_prep_chunk(ptr7, tr7_free)

                while cp["ptr"] < len(chunks):
                    emit_prep_chunk(ptr7, tr7_free)
                ve_prep_done = p.cnt["VE"]
                gd_prep_done = p.cnt["GD"]

                p.add("vector", lambda e: e.memset(kvx_sb[:], 0.0),
                      wait={"TE": kv_te_last}, inc="VE")
                p.add("vector", lambda e: e.memset(ks_sb[:], 0.0), inc="VE")
                for g in range(8):
                    p.add("vector", lambda e, g=g: e.tensor_copy(
                        kvx_sb[0:64, g * 128: g * 128 + 64],
                        pkv[0:64, g * 64:(g + 1) * 64]), inc="VE")
                    p.add("vector", lambda e, g=g: e.tensor_copy(
                        kvx_sb[64:128, g * 128 + 64: g * 128 + 128],
                        pkv[64:128, g * 64:(g + 1) * 64]), inc="VE")
                    p.add("vector", lambda e, g=g: e.tensor_copy(
                        ks_sb[0:64, g * 16 + 2 * g: g * 16 + 2 * g + 1],
                        pks[0:64, 2 * g:2 * g + 1]), inc="VE")
                    p.add("vector", lambda e, g=g: e.tensor_copy(
                        ks_sb[64:128, g * 16 + 2 * g + 1: g * 16 + 2 * g + 2],
                        pks[64:128, 2 * g + 1:2 * g + 2]), inc="VE")
                ve_kvx = p.cnt["VE"]
        esA.close()

        p.fence()

        esB = ExitStack()
        if True:
            WoutB = esB.enter_context(nc.sbuf_tensor("WoutB", [128, CCH * C], BF16))
            Awin0 = esB.enter_context(nc.sbuf_tensor("Awin0", [128, CCH * 512], BF16))
            Awin1 = esB.enter_context(nc.sbuf_tensor("Awin1", [128, CCH * 512], BF16))
            rdb0 = esB.enter_context(nc.sbuf_tensor("rdb0", [16, 512], BF16))
            rdb1 = esB.enter_context(nc.sbuf_tensor("rdb1", [16, 512], BF16))
            bcb0 = esB.enter_context(nc.sbuf_tensor("bcb0", [128, 512], F32))
            bcb1 = esB.enter_context(nc.sbuf_tensor("bcb1", [128, 512], F32))
            pnrm = esB.enter_context(nc.psum_tensor("pnrm", [128, 512], F32))
            pbc0 = esB.enter_context(nc.psum_tensor("pbc0", [128, 512], F32))
            pat0 = esB.enter_context(nc.psum_tensor("pat0", [128, 512], F32))
            pbc1 = esB.enter_context(nc.psum_tensor("pbc1", [128, 512], F32))
            pat1 = esB.enter_context(nc.psum_tensor("pat1", [128, 512], F32))
            po0 = esB.enter_context(nc.psum_tensor("po0", [128, 512], F32))
            po1 = esB.enter_context(nc.psum_tensor("po1", [128, 512], F32))
            Wout_v = WoutB[:].rearrange("p (c n) -> p c n", n=C)
            for cc in range(CCH):
                p.dma("gpsimd", Wout_v[:, cc, :], d_wout[cc * 128:(cc + 1) * 128, :])
            gd_wout = p.cnt["GD"]

            Awin = [Awin0, Awin1]
            Awin_v = [Awin0[:].rearrange("p (c n) -> p c n", n=512),
                      Awin1[:].rearrange("p (c n) -> p c n", n=512)]
            rd = [rdb0, rdb1]
            bc = [bcb0, bcb1]
            pbc = [pbc0, pbc1]
            pat = [pat0, pat1]
            Awin_free = [0, 0]
            rd_free = [0, 0]
            bc_free = [0, 0]
            pnrm_free = 0
            pat_free = [0, 0]
            pbc_free = [0, 0]
            po_free = [0, 0]
            prev_v_am = 0

            for ns in range(8):
                wi = ns % 2
                te_n = None
                for g in range(8):
                    te_n = p.add("tensor", lambda e, g=g, ns=ns: e.matmul(
                        pnrm[0:16, 0:512], ks_sb[:, g * 16:(g + 1) * 16],
                        bigv[:, g, ns * 512:(ns + 1) * 512],
                        start=(g == 0), stop=False),
                        wait={"VE": max(ve_kvx, pnrm_free)} if g == 0 else None,
                        inc="TE")
                te_n = p.add("tensor", lambda e: e.matmul(
                    pnrm[0:16, 0:512], eps_row[:], ones_row[0:1, 0:512],
                    start=False, stop=True), inc="TE")
                v_rd = p.add("vector", lambda e, wi=wi: e.reciprocal(
                    rd[wi][:], pnrm[0:16, 0:512]),
                    wait={"TE": max(te_n, rd_free[wi])}, inc="VE")
                pnrm_free = v_rd

                for g in range(8):
                    pi = g % 2
                    te_bc = p.add("tensor", lambda e, g=g, pi=pi, wi=wi: e.matmul(
                        pbc[pi][:, 0:512], ind_sb[:, g * 128:(g + 1) * 128], rd[wi][:],
                        start=True, stop=True),
                        wait={"VE": v_rd, "SE": pbc_free[pi]}, inc="TE")
                    rd_free[wi] = te_bc
                    te_at = p.add("tensor", lambda e, g=g, pi=pi, ns=ns: e.matmul(
                        pat[pi][:, 0:512], kvx_sb[:, g * 128:(g + 1) * 128],
                        bigv[:, g, ns * 512:(ns + 1) * 512],
                        start=True, stop=True),
                        wait={"VE": pat_free[pi]}, inc="TE")
                    se_bc = p.add("scalar", lambda e, pi=pi: e.activation(
                        bc[pi][:], pbc[pi][:, 0:512], AF.Identity),
                        wait={"TE": te_bc, "VE": bc_free[pi]}, inc="SE")
                    pbc_free[pi] = se_bc
                    v_a = p.add("vector", lambda e, g=g, pi=pi, wi=wi: e.tensor_tensor(
                        Awin_v[wi][:, g, :], pat[pi][:, 0:512], bc[pi][:], ALU.mult),
                        wait={"TE": max(te_at, Awin_free[wi]), "SE": se_bc}, inc="VE")
                    pat_free[pi] = v_a
                    bc_free[pi] = v_a
                ve_awin = p.cnt["VE"]

                for tl in range(4):
                    tt = ns * 4 + tl
                    bi = tt % 2
                    sdx = p.dma("sync", xb[bi][:], d_x[tt * 128:(tt + 1) * 128, :],
                                wait=xb_free[bi])
                    te_o = None
                    for c in range(CCH):
                        p.add("tensor", lambda e, c=c, tl=tl, wi=wi: e.matmul(
                            po0[:, 0:512], Awin_v[wi][:, c, tl * 128:(tl + 1) * 128],
                            Wout_v[:, c, 0:512], start=(c == 0), stop=False),
                            wait={"VE": max(ve_awin, po_free[0]), "GD": gd_wout}
                            if c == 0 else None, inc="TE")
                        te_o = p.add("tensor", lambda e, c=c, tl=tl, wi=wi: e.matmul(
                            po1[:, 0:512], Awin_v[wi][:, c, tl * 128:(tl + 1) * 128],
                            Wout_v[:, c, 512:1024], start=(c == 0), stop=False),
                            wait={"VE": po_free[1]} if c == 0 else None, inc="TE")
                    p.add("tensor", lambda e: e.matmul(
                        po0[:, 0:512], ones_row[0:1, 0:128], bout_row[0:1, 0:512],
                        start=False, stop=True), wait={"GD": gd_consts}, inc="TE")
                    te_o = p.add("tensor", lambda e: e.matmul(
                        po1[:, 0:512], ones_row[0:1, 0:128], bout_row[0:1, 512:1024],
                        start=False, stop=True), inc="TE")
                    if tl == 3:
                        Awin_free[wi] = te_o
                    v_x1a = p.add("vector", lambda e, bi=bi: e.scalar_tensor_tensor(
                        x1b[bi][:, 0:512], po0[:, 0:512], 1.0, xb[bi][:, 0:512],
                        ALU.mult, ALU.add),
                        wait=merge({"TE": te_o, "SD": sdx}, x1b_free[bi]), inc="VE")
                    v_x1b = p.add("vector", lambda e, bi=bi: e.scalar_tensor_tensor(
                        x1b[bi][:, 512:1024], po1[:, 0:512], 1.0, xb[bi][:, 512:1024],
                        ALU.mult, ALU.add), inc="VE")
                    po_free[0] = v_x1b
                    po_free[1] = v_x1b
                    xb_free[bi] = {"VE": v_x1b}
                    sd_x1 = p.dma("sync", d_x1[tt * 128:(tt + 1) * 128, :], x1b[bi][:],
                                  wait={"VE": v_x1b})
                    sd_x1_pb[tt] = sd_x1
                    x1b_free[bi] = {"SD": sd_x1}
                    p.add("vector", lambda e, bi=bi: e.reduce_sum(
                        stat[:, 0:1], x1b[bi][:], AX.X), wait={"VE": v_x1b}, inc="VE")
                    s_sq2 = p.add("scalar", lambda e, bi=bi: e.activation(
                        sq[:], x1b[bi][:], AF.Square, accum_out=stat[:, 1:2]),
                        wait={"VE": max(v_x1b, prev_v_am)}, inc="SE")
                    v_sum2 = p.cnt["VE"]
                    v_mean = p.add("vector", lambda e: e.tensor_scalar(
                        stat[:, 2:3], stat[:, 0:1], 1.0 / C, None, ALU.mult),
                        wait={"SE": s_sq2, "VE": v_sum2}, keep_self=True, inc="VE")
                    v_m2 = p.add("vector", lambda e: e.tensor_scalar(
                        stat[:, 3:4], stat[:, 2:3], stat[:, 2:3], None, ALU.mult),
                        wait={"VE": v_mean}, keep_self=True, inc="VE")
                    v_var = p.add("vector", lambda e: e.tensor_scalar(
                        stat[:, 4:5], stat[:, 1:2], 1.0 / C, stat[:, 3:4],
                        ALU.mult, ALU.subtract),
                        wait={"VE": v_m2}, keep_self=True, inc="VE")
                    s_sd2 = p.add("scalar", lambda e: e.activation(
                        stat[:, 5:6], stat[:, 4:5], AF.Sqrt, bias=constf[:, 0:1]),
                        wait={"VE": v_var}, inc="SE")
                    v_r2 = p.add("vector", lambda e, tt=tt: e.reciprocal(
                        r2_all[:, tt:tt + 1], stat[:, 5:6]), wait={"SE": s_sd2}, inc="VE")
                    v_nmr2 = p.add("vector", lambda e, tt=tt: e.tensor_scalar(
                        nmr2_all[:, tt:tt + 1], stat[:, 2:3], r2_all[:, tt:tt + 1], -1.0,
                        ALU.mult, ALU.mult),
                        wait={"VE": v_r2}, keep_self=True, inc="VE")
                    s_u = p.add("scalar", lambda e, bi=bi, tt=tt: e.activation(
                        sq[:], x1b[bi][:], AF.Identity, bias=nmr2_all[:, tt:tt + 1],
                        scale=r2_all[:, tt:tt + 1]),
                        wait={"VE": v_nmr2}, inc="SE")
                    v_am = p.add("vector", lambda e: e.reduce_max(
                        stat[:, 8:9], sq[:], AX.X, apply_absolute_value=True),
                        wait={"SE": s_u}, inc="VE")
                    amo, amn = 10 + (tt % 2), 10 + ((tt + 1) % 2)
                    prev_v_am = p.add("vector", lambda e, amo=amo, amn=amn: e.tensor_scalar(
                        stat[:, amn:amn + 1], stat[:, 8:9], stat[:, amo:amo + 1], None,
                        ALU.max), wait={"VE": v_am}, keep_self=True, inc="VE")
            ve_pb = p.cnt["VE"]
            sd_pb = p.cnt["SD"]

            if "x1" in dbg:
                for t2 in range(NT):
                    p.dma("sync", dbg["x1"][t2 * 128:(t2 + 1) * 128, :],
                          d_x1[t2 * 128:(t2 + 1) * 128, :], wait={"SD": sd_pb})
            if "q_fm" in dbg:
                p.dma("sync", dbg["q_fm"], big[:], wait={"VE": ve_pb})
            if "kvx" in dbg:
                p.dma("sync", dbg["kvx"], kvx_sb[:], wait={"VE": ve_pb})
            if "ks" in dbg:
                p.dma("sync", dbg["ks"], ks_sb[:], wait={"VE": ve_pb})
        esB.close()

        def barrier(acc_ap, cc_in, cc_out, scr, s_col, rs_col, clip_sb, wait_d):
            sd1 = p.dma("sync", cc_in, acc_ap, wait=wait_d)
            ge = p.add("gpsimd", lambda e, ci=cc_in, co=cc_out: e.collective_compute(
                "AllReduce", ALU.max, replica_groups=[list(range(N_CORES))],
                ins=[ci], outs=[co]),
                wait={"SD": sd1, "GD": p.cnt["GD"]}, inc="GE")
            sd2 = p.dma("sync", grow[:], cc_out.rearrange("a one -> one a"),
                        wait={"GE": ge})
            v1 = p.add("vector", lambda e: e.reduce_max(gcol[:], grow[:], AX.X),
                       wait={"SD": sd2}, inc="VE")
            v2 = p.add("vector", lambda e: e.tensor_scalar(
                grow[:], ones_f32[:], gcol[0:1, 0:1], None, ALU.mult),
                wait={"VE": v1}, keep_self=True, inc="VE")
            sd3 = p.dma("sync", scr.rearrange("a one -> one a"), grow[:], wait={"VE": v2})
            sd4 = p.dma("sync", gbc[:], scr, wait={"SD": sd3})
            s_sig = p.add("scalar", lambda e, cs=clip_sb: e.activation(
                sig_col[:], cs[:], AF.Sigmoid), wait={"SD": max(sd4, sd_consts)}, inc="SE")
            v3 = p.add("vector", lambda e, s=s_col: e.scalar_tensor_tensor(
                s[:], sig_col[:], 1.0 / 127.0, gbc[:], ALU.mult, ALU.mult),
                wait={"SE": s_sig, "SD": sd4}, inc="VE")
            v4 = p.add("vector", lambda e, s=s_col, r=rs_col: e.reciprocal(r[:], s[:]),
                       wait={"VE": v3}, keep_self=True, inc="VE")
            return v4

        v_b1 = barrier(stat[:, 10:11], d_cc1_in, d_cc1_out, d_s128a, s1_col, rs1_col,
                       clipx_sb, {"VE": ve_pb})
        p.add("vector", lambda e: e.tensor_scalar(
            uscale_all[:], r2_all[:], rs1_col[:], None, ALU.mult),
            wait={"VE": v_b1}, keep_self=True, inc="VE")
        p.add("vector", lambda e: e.tensor_scalar(
            ubias_all[:], nmr2_all[:], rs1_col[:], None, ALU.mult), inc="VE")
        p.add("vector", lambda e: e.tensor_scalar(
            s1sg_all[:], sg_all[:], s1_col[:], None, ALU.mult),
            wait={"VE": ve_prep_done}, inc="VE")
        v_b1e = p.add("vector", lambda e: e.tensor_scalar(
            s1su_all[:], su_all[:], s1_col[:], None, ALU.mult), inc="VE")
        if "s1" in dbg:
            p.dma("sync", dbg["s1"], s1_col[:], wait={"VE": v_b1e})

        p.fence()

        esC = ExitStack()
        if True:
            wqTsb = esC.enter_context(nc.sbuf_tensor("wqTsb", [128, 2 * NIC * C], BF16))
            ubuf = esC.enter_context(nc.sbuf_tensor("ubuf", [128, C], F32))
            t1buf = esC.enter_context(nc.sbuf_tensor("t1buf", [128, C], F32))
            xqtok = esC.enter_context(nc.sbuf_tensor("xqtok", [128, C], BF16))
            sgb0 = esC.enter_context(nc.sbuf_tensor("sgb0", [128, 512], BF16))
            sgb1 = esC.enter_context(nc.sbuf_tensor("sgb1", [128, 512], BF16))
            hb0 = esC.enter_context(nc.sbuf_tensor("hb0", [128, 512], F32))
            hb1 = esC.enter_context(nc.sbuf_tensor("hb1", [128, 512], F32))
            hb2 = esC.enter_context(nc.sbuf_tensor("hb2", [128, 512], F32))
            pg0 = esC.enter_context(nc.psum_tensor("pg0", [128, 512], F32))
            pu0 = esC.enter_context(nc.psum_tensor("pu0", [128, 512], F32))
            pg1 = esC.enter_context(nc.psum_tensor("pg1", [128, 512], F32))
            pu1 = esC.enter_context(nc.psum_tensor("pu1", [128, 512], F32))
            pg2 = esC.enter_context(nc.psum_tensor("pg2", [128, 512], F32))
            pu2 = esC.enter_context(nc.psum_tensor("pu2", [128, 512], F32))
            ptrC = esC.enter_context(nc.psum_tensor("ptrC", [128, 1024], BF16))
            wqT_v = wqTsb[:].rearrange("p (m i c) -> p m i c", m=2, i=NIC)
            for mi in range(2):
                for ii in range(NIC):
                    p.dma("gpsimd", wqT_v[:, mi, ii, :].rearrange("p (c b) -> p c b", b=128),
                          d_wqT[mi, ii].transpose([1, 0, 2]),
                          wait={"GD": gd_prep_done})
            gd_wqT = p.cnt["GD"]

            xq_evac_ve = [0] * NT
            x1_free = [dict(x1b_free[0]), dict(x1b_free[1])]
            trC_free = [0]
            ubuf_free = [0]
            xqtok_free2 = [0]

            def emit_xq_tile(tt):
                bi = tt % 2
                sdx = p.dma("sync", x1b[bi][:], d_x1[tt * 128:(tt + 1) * 128, :],
                            wait=merge(x1_free[bi], {"SD": sd_x1_pb[tt]}))
                s_u = p.add("scalar", lambda e, bi=bi, tt=tt: e.activation(
                    ubuf[:], x1b[bi][:], AF.Identity, bias=ubias_all[:, tt:tt + 1],
                    scale=uscale_all[:, tt:tt + 1]),
                    wait={"SD": sdx, "VE": max(v_b1e, ubuf_free[0])}, inc="SE")
                v_t1 = p.add("vector", lambda e: e.tensor_scalar(
                    t1buf[:], ubuf[:], MAGIC, MAGIC, ALU.add, ALU.subtract),
                    wait={"SE": s_u}, inc="VE")
                ubuf_free[0] = v_t1
                v_xq = p.add("vector", lambda e: e.tensor_scalar(
                    xqtok[:], t1buf[:], 127.0, -128.0, ALU.min, ALU.max),
                    wait={"TE": xqtok_free2[0]}, inc="VE")
                x1_free[bi] = {"VE": v_xq}
                te_tr = None
                for j in range(8):
                    te_tr = p.add("tensor", lambda e, j=j: e.transpose(
                        ptrC[:, j * 128:(j + 1) * 128], xqtok[:, j * 128:(j + 1) * 128],
                        ident[:]),
                        wait={"VE": max(v_xq, trC_free[0])} if j == 0 else None, inc="TE")
                xqtok_free2[0] = te_tr
                v_ev = p.add("vector", lambda e, tt=tt: e.tensor_copy(
                    bigv[:, :, tt * 128:(tt + 1) * 128], ptrC[:, 0:1024]),
                    wait={"TE": te_tr}, inc="VE")
                trC_free[0] = v_ev
                xq_evac_ve[tt] = v_ev

            for tt in range(8):
                emit_xq_tile(tt)

            pg = [pg0, pg1, pg2]
            pu = [pu0, pu1, pu2]
            sgb = [sgb0, sgb1]
            hb = [hb0, hb1, hb2]
            pg_free = [0, 0, 0]
            pu_free = [0, 0, 0]
            sg_free = [0, 0]
            hb_free = [{}, {}, {}]
            prev_hm = 0
            hi = 0

            for ns in range(8):
                for t3 in range(8 + ns * 3, min(NT, 8 + (ns + 1) * 3)):
                    emit_xq_tile(t3)
                need_ve = xq_evac_ve[4 * ns + 3]
                for ii in range(NIC):
                    pi = (ns * NIC + ii) % 3
                    te_g = None
                    for hbk in range(CCH):
                        te_g = p.add("tensor", lambda e, ii=ii, hbk=hbk, ns=ns, pi=pi:
                                     e.matmul(pg[pi][:, 0:512],
                                              wqT_v[:, 0, ii, hbk * 128:(hbk + 1) * 128],
                                              bigv[:, hbk, ns * 512:(ns + 1) * 512],
                                              start=(hbk == 0), stop=(hbk == CCH - 1)),
                                     wait={"VE": need_ve, "GD": gd_wqT, "SE": pg_free[pi]}
                                     if hbk == 0 else None, inc="TE")
                    te_u = None
                    for hbk in range(CCH):
                        te_u = p.add("tensor", lambda e, ii=ii, hbk=hbk, ns=ns, pi=pi:
                                     e.matmul(pu[pi][:, 0:512],
                                              wqT_v[:, 1, ii, hbk * 128:(hbk + 1) * 128],
                                              bigv[:, hbk, ns * 512:(ns + 1) * 512],
                                              start=(hbk == 0), stop=(hbk == CCH - 1)),
                                     wait={"VE": pu_free[pi]} if hbk == 0 else None,
                                     inc="TE")
                    si = (ns * NIC + ii) % 2
                    s_sg = p.add("scalar", lambda e, ii=ii, pi=pi, si=si: e.activation(
                        sgb[si][:], pg[pi][:, 0:512], AF.Silu,
                        scale=s1sg_all[:, ii:ii + 1]),
                        wait={"TE": te_g, "VE": max(v_b1e, sg_free[si])}, inc="SE")
                    pg_free[pi] = s_sg
                    hbi = hi % 3
                    hi += 1
                    v_h = p.add("vector", lambda e, ii=ii, pi=pi, si=si, hbi=hbi:
                                e.scalar_tensor_tensor(hb[hbi][:], pu[pi][:, 0:512],
                                                       s1su_all[:, ii:ii + 1], sgb[si][:],
                                                       ALU.mult, ALU.mult),
                                wait=merge({"TE": te_u, "SE": s_sg}, hb_free[hbi]),
                                inc="VE")
                    pu_free[pi] = v_h
                    sg_free[si] = v_h
                    v_hm = p.add("vector", lambda e, hbi=hbi: e.reduce_max(
                        stat[:, 9:10], hb[hbi][:], AX.X, apply_absolute_value=True),
                        wait={"VE": prev_hm}, inc="VE")
                    kk = ns * NIC + ii
                    hmo, hmn = 12 + (kk % 2), 12 + ((kk + 1) % 2)
                    prev_hm = p.add("vector", lambda e, hmo=hmo, hmn=hmn: e.tensor_scalar(
                        stat[:, hmn:hmn + 1], stat[:, 9:10], stat[:, hmo:hmo + 1], None,
                        ALU.max), wait={"VE": v_hm}, keep_self=True, inc="VE")
                    sd_h = p.dma("sync", d_h[ii][:, ns * 512:(ns + 1) * 512], hb[hbi][:],
                                 wait={"VE": v_h})
                    hb_free[hbi] = {"SD": sd_h}
            ve_pc = p.cnt["VE"]
            sd_pc = p.cnt["SD"]
            if "xq" in dbg:
                p.dma("sync", dbg["xq"], big[:], wait={"VE": ve_pc})
            if "h0" in dbg:
                p.dma("sync", dbg["h0"], d_h[0], wait={"SD": sd_pc})
        esC.close()

        v_b2 = barrier(stat[:, 12:13], d_cc2_in, d_cc2_out, d_s128b, s2_col, rs2_col,
                       clipd_sb, {"VE": ve_pc})
        if "s2" in dbg:
            p.dma("sync", dbg["s2"], s2_col[:], wait={"VE": v_b2})

        esBig.close()
        p.fence()

        esD = ExitStack()
        if True:
            wdTsb = esD.enter_context(nc.sbuf_tensor("wdTsb", [128, NIC * C], BF16))
            hin0 = esD.enter_context(nc.sbuf_tensor("hin0", [128, NIC * 128], F32))
            hin1 = esD.enter_context(nc.sbuf_tensor("hin1", [128, NIC * 128], F32))
            ubuf2 = esD.enter_context(nc.sbuf_tensor("ubuf2", [128, NI], F32))
            t2buf = esD.enter_context(nc.sbuf_tensor("t2buf", [128, NI], F32))
            hq0 = esD.enter_context(nc.sbuf_tensor("hq0", [128, NI], BF16))
            hq1 = esD.enter_context(nc.sbuf_tensor("hq1", [128, NI], BF16))
            outb0 = esD.enter_context(nc.sbuf_tensor("outb0", [128, C], F32))
            outb1 = esD.enter_context(nc.sbuf_tensor("outb1", [128, C], F32))
            pd0 = esD.enter_context(nc.psum_tensor("pd0", [128, 512], F32))
            pd1 = esD.enter_context(nc.psum_tensor("pd1", [128, 512], F32))
            pd2 = esD.enter_context(nc.psum_tensor("pd2", [128, 512], F32))
            pd3 = esD.enter_context(nc.psum_tensor("pd3", [128, 512], F32))
            wdT_v = wdTsb[:].rearrange("p (i c) -> p i c", i=NIC)
            for ci in range(NIC):
                p.dma("gpsimd", wdT_v[:, ci, :].rearrange("p (c b) -> p c b", b=128),
                      d_wdT[ci].transpose([1, 0, 2]),
                      wait={"GD": gd_prep_done})
            gd_wdT = p.cnt["GD"]

            hin = [hin0, hin1]
            hinv = [hin0[:].rearrange("p (i n) -> p i n", n=128),
                    hin1[:].rearrange("p (i n) -> p i n", n=128)]
            hq = [hq0, hq1]
            outb = [outb0, outb1]
            pdk = [(pd0, pd1), (pd2, pd3)]
            hin_free = [0, 0]
            hq_free = [0, 0]
            pd_free = [0, 0]
            outb_free = [0, 0]
            ub2_free = 0
            t2_free = 0
            sd_out_last = None
            for tt in range(NT):
                bi = tt % 2
                sdh = p.dma("sync", hinv[bi],
                            d_h[:, :, tt * 128:(tt + 1) * 128].transpose([1, 0, 2]),
                            wait={"SE": hin_free[bi]})
                sdx = p.dma("gpsimd", x1b[bi][:], d_x1[tt * 128:(tt + 1) * 128, :],
                            wait=x1_free[bi])
                s_u = p.add("scalar", lambda e, bi=bi: e.activation(
                    ubuf2[:], hin[bi][:], AF.Identity, bias=constf[:, 1:2], scale=rs2_col[:]),
                    wait={"SD": sdh, "VE": max(v_b2, ub2_free)}, inc="SE")
                hin_free[bi] = s_u
                v_t1 = p.add("vector", lambda e: e.tensor_scalar(
                    t2buf[:], ubuf2[:], MAGIC, 127.0, ALU.subtract, ALU.min),
                    wait={"SE": s_u, "VE": t2_free}, inc="VE")
                ub2_free = v_t1
                v_hq = p.add("vector", lambda e, bi=bi: e.tensor_scalar(
                    hq[bi][:], t2buf[:], -128.0, None, ALU.max),
                    wait={"TE": hq_free[bi]}, inc="VE")
                t2_free = v_hq
                pdi = tt % 2
                pda, pdb = pdk[pdi]
                te_d = None
                for ci in range(NIC):
                    p.add("tensor", lambda e, ci=ci, bi=bi, pda=pda: e.matmul(
                        pda[:, 0:512], hq[bi][:, ci * 128:(ci + 1) * 128],
                        wdT_v[:, ci, 0:512], start=(ci == 0), stop=(ci == NIC - 1)),
                        wait={"VE": max(v_hq, pd_free[pdi]), "GD": gd_wdT}
                        if ci == 0 else None, inc="TE")
                    te_d = p.add("tensor", lambda e, ci=ci, bi=bi, pdb=pdb: e.matmul(
                        pdb[:, 0:512], hq[bi][:, ci * 128:(ci + 1) * 128],
                        wdT_v[:, ci, 512:1024], start=(ci == 0), stop=(ci == NIC - 1)),
                        inc="TE")
                hq_free[bi] = te_d
                v_o1 = p.add("vector", lambda e, bi=bi, pda=pda: e.scalar_tensor_tensor(
                    outb[bi][:, 0:512], pda[:, 0:512], s2_col[:], x1b[bi][:, 0:512],
                    ALU.mult, ALU.add),
                    wait=merge({"TE": te_d, "GD": sdx}, {"SD": outb_free[bi]}), inc="VE")
                v_o2 = p.add("vector", lambda e, bi=bi, pdb=pdb: e.scalar_tensor_tensor(
                    outb[bi][:, 512:1024], pdb[:, 0:512], s2_col[:], x1b[bi][:, 512:1024],
                    ALU.mult, ALU.add), inc="VE")
                pd_free[pdi] = v_o2
                x1_free[bi] = {"VE": v_o2}
                sd_o = p.dma("sync", d_out[tt * 128:(tt + 1) * 128, :], outb[bi][:],
                             wait={"VE": v_o2})
                outb_free[bi] = sd_o
                sd_out_last = sd_o

            p.add("sync", lambda e: None, wait={"SD": sd_out_last})
        esD.close()

        with nc.allow_low_precision(reason="bf16 matmul operands / quantized codes by design; fp32 accumulation everywhere"):
            p.flush(blk)
    return nc


_NC_CACHE = {}


def _consts():
    ident = np.eye(128, dtype=np.float32).astype(BF)
    ones_col = np.ones((128, 1), np.float32).astype(BF)
    ones_row = np.ones((1, 1024), np.float32).astype(BF)
    ones_f32 = np.ones((1, 128), np.float32)
    eps_row = np.full((1, 16), 1e-6, np.float32).astype(BF)
    ind = np.zeros((8, 16, 128), np.float32)
    for g in range(8):
        ind[g, 2 * g, 0:64] = 1.0
        ind[g, 2 * g + 1, 64:128] = 1.0
    return ident, ones_col, ones_row, ones_f32, eps_row, ind.astype(BF)


def kernel(L0_attn_x, L0_attn_W_qkv, L0_attn_W_f, L0_attn_b_f, L0_attn_W_out,
           L0_attn_b_out, L0_ffn_clip_x, L0_ffn_clip_down, L0_ffn_w_gate,
           L0_ffn_w_up, L0_ffn_w_down):
    x = np.ascontiguousarray(np.asarray(L0_attn_x, dtype=np.float32))
    ident, ones_col, ones_row, ones_f32, eps_row, ind = _consts()
    shared = {
        "wqkv": np.ascontiguousarray(np.asarray(L0_attn_W_qkv, np.float32)),
        "wf": np.ascontiguousarray(np.asarray(L0_attn_W_f, np.float32)),
        "bfv": np.ascontiguousarray(np.asarray(L0_attn_b_f, np.float32).reshape(1, DH)),
        "wout": np.ascontiguousarray(np.asarray(L0_attn_W_out, np.float32)),
        "bout": np.ascontiguousarray(np.asarray(L0_attn_b_out, np.float32).reshape(1, C)),
        "clipx": np.ascontiguousarray(np.broadcast_to(
            np.asarray(L0_ffn_clip_x, np.float32).reshape(1, 1), (128, 1))),
        "clipd": np.ascontiguousarray(np.broadcast_to(
            np.asarray(L0_ffn_clip_down, np.float32).reshape(1, 1), (128, 1))),
        "wg": np.ascontiguousarray(np.asarray(L0_ffn_w_gate, np.float32)),
        "wu": np.ascontiguousarray(np.asarray(L0_ffn_w_up, np.float32)),
        "wd": np.ascontiguousarray(np.asarray(L0_ffn_w_down, np.float32)),
        "ident": ident, "ones_col": ones_col, "ones_row": ones_row,
        "ones_f32": ones_f32, "eps_row": eps_row, "ind": ind,
        "constf": np.ascontiguousarray(np.broadcast_to(
            np.array([[EPS, MAGIC]], np.float32), (128, 2))),
    }
    if "nc" not in _NC_CACHE:
        _NC_CACHE["nc"] = build()
    nc = _NC_CACHE["nc"]
    in_maps = [dict(shared, x=np.ascontiguousarray(x[b])) for b in range(N_CORES)]
    res = run_bass_kernel_spmd(nc, in_maps, core_ids=list(range(N_CORES)))
    out = np.stack([np.asarray(res.results[b]["out"]) for b in range(N_CORES)], axis=0)
    return out


# version 10
# speedup vs baseline: 1.2543x; 1.2543x over previous
import numpy as np
import ml_dtypes
from contextlib import ExitStack

import concourse.bass as bass
import concourse.mybir as mybir
from concourse.bass_utils import run_bass_kernel_spmd

F32 = mybir.dt.float32
BF16 = mybir.dt.bfloat16
AF = mybir.ActivationFunctionType
ALU = mybir.AluOpType
AX = mybir.AxisListType
BF = ml_dtypes.bfloat16

MAGIC = float(np.float32(1.5 * 2 ** 23))
N_CORES = 8
SEQ = 4096
C = 1024
NT = SEQ // 128
CCH = 8
H = 16
DH = 64
NI = 2816
NIC = NI // 128
EPS = 1e-5

COMPUTE_SEMS = {"TE", "VE", "SE", "GE"}


def merge(*ds):
    out = {}
    for d in ds:
        for k, v in (d or {}).items():
            if v:
                out[k] = max(out.get(k, 0), v)
    return out


class Prog:
    ENGS = ("sync", "tensor", "vector", "scalar", "gpsimd")

    def __init__(self, nc, es):
        self.nc = nc
        self.es = es
        self.sems = {}
        self.cnt = {}
        self.streams = {e: [] for e in self.ENGS}
        self.lastwait = {e: {} for e in self.ENGS}
        self.eng_of_sem = {"TE": "tensor", "VE": "vector", "SE": "scalar",
                           "GE": "gpsimd"}
        for name in self.eng_of_sem:
            self.sems[name] = es.enter_context(nc.semaphore(name))
            self.cnt[name] = 0

    def sem(self, name):
        if name not in self.sems:
            self.sems[name] = self.es.enter_context(self.nc.semaphore(name))
            self.cnt[name] = 0
        return name

    SERIAL = False

    def add(self, eng, fn, wait=None, inc=None, inc_amt=1, keep_self=False):
        if Prog.SERIAL and inc is not None:
            wait = merge(wait, getattr(self, "_last", None))
            keep_self = True
        w = {}
        for s, v in (wait or {}).items():
            if not v:
                continue
            assert v <= self.cnt[s], f"wait {s}>={v} but only {self.cnt[s]} emitted"
            if (not keep_self) and s in COMPUTE_SEMS and self.eng_of_sem[s] == eng:
                continue
            if self.lastwait[eng].get(s, -1) < v:
                w[s] = v
                self.lastwait[eng][s] = v
        if inc:
            self.cnt[inc] += inc_amt
        self.streams[eng].append((w, fn, inc, inc_amt))
        if Prog.SERIAL and inc is not None:
            self._last = {inc: self.cnt[inc]}
        return self.cnt[inc] if inc else None

    def dma(self, eng, out, in_, wait=None, sem=None):
        assert sem is not None, "dma needs an explicit stream semaphore"
        self.sem(sem)
        return self.add(eng, lambda e, o=out, i=in_: e.dma_start(out=o, in_=i),
                        wait=wait, inc=sem, inc_amt=16)

    def fence(self):
        cur = dict(self.cnt)
        for eng in self.ENGS:
            self.add(eng, lambda e: None, wait=dict(cur))

    def flush(self, blk):
        for eng in self.ENGS:
            ops = self.streams[eng]
            self.streams[eng] = []
            if not ops:
                continue
            sems = self.sems

            def mk(ops_):
                def run(e):
                    for w, fn, inc, amt in ops_:
                        for s, v in w.items():
                            e.wait_ge(sems[s], v)
                        ins = fn(e)
                        if inc is not None:
                            ins.then_inc(sems[inc], amt)
                return run
            getattr(blk, eng)(mk(ops))


def build(debug=()):
    nc = bass.Bass("TRN2", target_bir_lowering=False)

    d_x = nc.dram_tensor("x", [SEQ, C], F32, kind="ExternalInput").ap()
    d_wqkv = nc.dram_tensor("wqkv", [C, 3 * C], F32, kind="ExternalInput").ap()
    d_wf = nc.dram_tensor("wf", [DH, DH], F32, kind="ExternalInput").ap()
    d_bf = nc.dram_tensor("bfv", [1, DH], F32, kind="ExternalInput").ap()
    d_wout = nc.dram_tensor("wout", [C, C], F32, kind="ExternalInput").ap()
    d_bout = nc.dram_tensor("bout", [1, C], F32, kind="ExternalInput").ap()
    d_clipx = nc.dram_tensor("clipx", [128, 1], F32, kind="ExternalInput").ap()
    d_clipd = nc.dram_tensor("clipd", [128, 1], F32, kind="ExternalInput").ap()
    d_wg = nc.dram_tensor("wg", [NI, C], F32, kind="ExternalInput").ap()
    d_wu = nc.dram_tensor("wu", [NI, C], F32, kind="ExternalInput").ap()
    d_wd = nc.dram_tensor("wd", [C, NI], F32, kind="ExternalInput").ap()
    d_ident = nc.dram_tensor("ident", [128, 128], BF16, kind="ExternalInput").ap()
    d_ones_col = nc.dram_tensor("ones_col", [128, 1], BF16, kind="ExternalInput").ap()
    d_ones_row = nc.dram_tensor("ones_row", [1, 1024], BF16, kind="ExternalInput").ap()
    d_ones_f32 = nc.dram_tensor("ones_f32", [1, 128], F32, kind="ExternalInput").ap()
    d_eps_row = nc.dram_tensor("eps_row", [1, 16], BF16, kind="ExternalInput").ap()
    d_ind = nc.dram_tensor("ind", [8, 16, 128], BF16, kind="ExternalInput").ap()
    d_constf = nc.dram_tensor("constf", [128, 2], F32, kind="ExternalInput").ap()

    d_out = nc.dram_tensor("out", [SEQ, C], F32, kind="ExternalOutput").ap()

    d_x1 = nc.dram_tensor("x1_scr", [SEQ, C], F32, kind="Internal").ap()
    d_h = nc.dram_tensor("h_scr", [NIC, 128, SEQ], F32, kind="Internal").ap()
    d_wqT = nc.dram_tensor("wqT_scr", [2, NIC, 128, CCH * 128], BF16, kind="Internal").ap()
    d_wdT = nc.dram_tensor("wdT_scr", [NIC, CCH, 128, 128], BF16, kind="Internal").ap()
    d_cc1_in = nc.dram_tensor("cc1_in", [128, 1], F32, kind="Internal").ap()
    d_cc1_out = nc.dram_tensor("cc1_out", [128, 1], F32, kind="Internal", addr_space="Shared").ap()
    d_cc2_in = nc.dram_tensor("cc2_in", [128, 1], F32, kind="Internal").ap()
    d_cc2_out = nc.dram_tensor("cc2_out", [128, 1], F32, kind="Internal", addr_space="Shared").ap()
    d_s128a = nc.dram_tensor("s128a", [128, 1], F32, kind="Internal").ap()
    d_s128b = nc.dram_tensor("s128b", [128, 1], F32, kind="Internal").ap()

    dbg = {}
    for name in debug:
        shp = {"x1": ([SEQ, C], F32), "q_fm": ([128, CCH * SEQ], BF16),
               "kvx": ([128, 8 * 128], BF16), "ks": ([128, 8 * 16], BF16),
               "xq": ([128, CCH * SEQ], BF16), "h0": ([128, SEQ], F32),
               "s1": ([128, 1], F32), "s2": ([128, 1], F32),
               "wall": ([128, CCH * 3 * C], BF16), "xnfm": ([128, CCH * 128], BF16),
               "wdts": ([128, NIC * C], BF16), "hqs": ([128, NIC * 512], BF16),
               "hmax": ([128, 1], F32), "hall": ([NIC * 128, SEQ], F32),
               "pdu": ([128, 3 * 512], F32),
               "ktok": ([128, C], BF16), "vtok": ([128, C], BF16),
               "qtok": ([128, C], BF16), "stat0": ([128, 16], F32)}[name]
        dt_ = BF16 if name in ("q_fm", "kvx", "ks", "xq", "wall", "xnfm", "ktok",
                               "vtok", "qtok", "wdts", "hqs") else F32
        dbg[name] = nc.dram_tensor("dbg_" + name, shp[0], dt_, kind="ExternalOutput").ap()

    es_all = ExitStack()
    with es_all:
        es = es_all
        ident = es.enter_context(nc.sbuf_tensor("identb", [128, 128], BF16))
        ones_col = es.enter_context(nc.sbuf_tensor("onescol", [128, 1], BF16))
        ones_row = es.enter_context(nc.sbuf_tensor("onesrow", [1, 1024], BF16))
        ones_f32 = es.enter_context(nc.sbuf_tensor("onesf32", [1, 128], F32))
        eps_row = es.enter_context(nc.sbuf_tensor("epsrow", [1, 16], BF16))
        ind_sb = es.enter_context(nc.sbuf_tensor("indsb", [16, 8 * 128], BF16))
        bias_row = es.enter_context(nc.sbuf_tensor("biasrow", [1, 3 * C], BF16))
        bf_bf = es.enter_context(nc.sbuf_tensor("bfbf", [1, DH], BF16))
        bout_row = es.enter_context(nc.sbuf_tensor("boutrow", [1, C], BF16))
        clipx_sb = es.enter_context(nc.sbuf_tensor("clipxsb", [128, 1], F32))
        clipd_sb = es.enter_context(nc.sbuf_tensor("clipdsb", [128, 1], F32))
        r2_all = es.enter_context(nc.sbuf_tensor("r2all", [128, NT], F32))
        nmr2_all = es.enter_context(nc.sbuf_tensor("nmr2all", [128, NT], F32))
        sg_all = es.enter_context(nc.sbuf_tensor("sgall", [128, NIC], F32))
        su_all = es.enter_context(nc.sbuf_tensor("suall", [128, NIC], F32))
        s1_col = es.enter_context(nc.sbuf_tensor("s1col", [128, 1], F32))
        rs1_col = es.enter_context(nc.sbuf_tensor("rs1col", [128, 1], F32))
        s2_col = es.enter_context(nc.sbuf_tensor("s2col", [128, 1], F32))
        rs2_col = es.enter_context(nc.sbuf_tensor("rs2col", [128, 1], F32))
        uscale_all = es.enter_context(nc.sbuf_tensor("uscaleall", [128, NT], F32))
        ubias_all = es.enter_context(nc.sbuf_tensor("ubiasall", [128, NT], F32))
        s1sg_all = es.enter_context(nc.sbuf_tensor("s1sgall", [128, NIC], F32))
        s1su_all = es.enter_context(nc.sbuf_tensor("s1suall", [128, NIC], F32))
        grow = es.enter_context(nc.sbuf_tensor("growbuf", [1, 128], F32))
        gcol = es.enter_context(nc.sbuf_tensor("gcolbuf", [1, 1], F32))
        gbc = es.enter_context(nc.sbuf_tensor("gbcbuf", [128, 1], F32))
        sig_col = es.enter_context(nc.sbuf_tensor("sigcol", [128, 1], F32))
        kvx_sb = es.enter_context(nc.sbuf_tensor("kvxsb", [128, 8 * 128], BF16))
        ks_sb = es.enter_context(nc.sbuf_tensor("kssb", [128, 8 * 16], BF16))
        xb = [es.enter_context(nc.sbuf_tensor(f"xb{i}", [128, C], F32)) for i in range(2)]
        x1b = [es.enter_context(nc.sbuf_tensor(f"x1b{i}", [128, C], F32)) for i in range(2)]
        sq = es.enter_context(nc.sbuf_tensor("sqscr", [128, C], F32))
        stat = es.enter_context(nc.sbuf_tensor("statscr", [128, 16], F32))
        constf = es.enter_context(nc.sbuf_tensor("constf_sb", [128, 2], F32))
        esBig = ExitStack()
        big = esBig.enter_context(nc.sbuf_tensor("big", [128, CCH * SEQ], BF16))

        blk = es.enter_context(nc.Block())
        p = Prog(nc, es)
        bigv = big[:].rearrange("p (c n) -> p c n", n=SEQ)

        xb_free = [{}, {}]
        x1b_free = [{}, {}]

        for o, i in ((ident[:], d_ident), (ones_col[:], d_ones_col),
                     (ones_row[:], d_ones_row), (ones_f32[:], d_ones_f32),
                     (eps_row[:], d_eps_row), (clipx_sb[:], d_clipx),
                     (clipd_sb[:], d_clipd), (constf[:], d_constf)):
            p.dma("sync", o, i, sem="sCONST")
        for g in range(8):
            p.dma("sync", ind_sb[:, g * 128:(g + 1) * 128], d_ind[g], sem="sCONST")
        sd_consts = p.cnt["sCONST"]
        p.dma("gpsimd", bf_bf[:], d_bf, sem="sCONST2")
        p.dma("gpsimd", bout_row[:], d_bout, sem="sCONST2")
        gd_consts = p.cnt["sCONST2"]

        p.add("vector", lambda e: e.memset(bias_row[:], 0.0), inc="VE")
        for j in range(16):
            p.add("vector", lambda e, j=j: e.tensor_copy(
                bias_row[:, j * 64:(j + 1) * 64], bf_bf[:]),
                wait={"sCONST2": gd_consts}, inc="VE")
            p.add("vector", lambda e, j=j: e.tensor_copy(
                bias_row[:, 2048 + j * 64: 2048 + (j + 1) * 64], bf_bf[:]), inc="VE")
        p.add("vector", lambda e: e.memset(stat[:, 10:14], 0.0), inc="VE")
        ve_consts = p.cnt["VE"]

        esA = ExitStack()
        if True:
            Wall = esA.enter_context(nc.sbuf_tensor("Wall", [128, CCH * 3 * C], BF16))
            wf_sb = esA.enter_context(nc.sbuf_tensor("wf_sb", [DH, DH], BF16))
            wTb = esA.enter_context(nc.sbuf_tensor("wTbuf", [64, 1024], BF16))
            xnfm0 = esA.enter_context(nc.sbuf_tensor("xnfm0", [128, CCH * 128], BF16))
            xnfm1 = esA.enter_context(nc.sbuf_tensor("xnfm1", [128, CCH * 128], BF16))
            xntok = esA.enter_context(nc.sbuf_tensor("xntok", [128, C], BF16))
            ktok0 = esA.enter_context(nc.sbuf_tensor("ktok0", [128, C], BF16))
            ktok1 = esA.enter_context(nc.sbuf_tensor("ktok1", [128, C], BF16))
            vtok0 = esA.enter_context(nc.sbuf_tensor("vtok0", [128, C], BF16))
            vtok1 = esA.enter_context(nc.sbuf_tensor("vtok1", [128, C], BF16))
            qtok0 = esA.enter_context(nc.sbuf_tensor("qtok0", [128, C], BF16))
            qtok1 = esA.enter_context(nc.sbuf_tensor("qtok1", [128, C], BF16))
            ptr6 = esA.enter_context(nc.psum_tensor("ptr6", [128, 1024], BF16))
            ptr7 = esA.enter_context(nc.psum_tensor("ptr7", [128, 1024], BF16))
            pkv = esA.enter_context(nc.psum_tensor("pkv", [128, 512], F32))
            pks = esA.enter_context(nc.psum_tensor("pks", [128, 512], F32))
            pa0 = esA.enter_context(nc.psum_tensor("pa0", [128, 512], F32))
            pa1 = esA.enter_context(nc.psum_tensor("pa1", [128, 512], F32))
            pa2 = esA.enter_context(nc.psum_tensor("pa2", [128, 512], F32))
            pa3 = esA.enter_context(nc.psum_tensor("pa3", [128, 512], F32))
            Wall_v = Wall[:].rearrange("p (c n) -> p c n", n=3 * C)
            pabank = [pa0, pa1, pa2, pa3]

            p.dma("gpsimd", wf_sb[:], d_wf, sem="sWF")
            gd_wf = p.cnt["sWF"]
            for cc in range(CCH):
                p.dma("gpsimd", Wall_v[:, cc, 1024:2048],
                      d_wqkv[cc * 128:(cc + 1) * 128, 2048:3072], sem="sWALL")
            gd_v = p.cnt["sWALL"]

            te_fold_tr_last = 0
            ve_fold = 0
            esW = ExitStack()
            if True:
                wstage = esW.enter_context(nc.sbuf_tensor("wstage", [128, CCH * 2048], BF16))
                wstage_v = wstage[:].rearrange("p (c n) -> p c n", n=2048)
                for cc in range(CCH):
                    p.dma("gpsimd", wstage_v[:, cc, 0:1024],
                          d_wqkv[cc * 128:(cc + 1) * 128, 0:1024], sem="sSTG")
                    p.dma("gpsimd", wstage_v[:, cc, 1024:2048],
                          d_wqkv[cc * 128:(cc + 1) * 128, 1024:2048], sem="sSTG")
                gd_stage = p.cnt["sSTG"]

                tr6_ve = 0
                mm_ve = [0, 0]
                wT_te = 0
                gidx = 0
                for src_off, dst_off in ((1024, 0), (0, 2048)):
                    for hh in range(H):
                        bk = gidx % 2
                        gidx += 1
                        te_tr = None
                        for cc in range(CCH):
                            te_tr = p.add("tensor", lambda e, cc=cc, so=src_off, hh=hh:
                                          e.transpose(ptr6[0:64, cc * 128:(cc + 1) * 128],
                                                      wstage_v[:, cc, so + hh * 64: so + (hh + 1) * 64],
                                                      ident[:]),
                                          wait={"sSTG": gd_stage, "sCONST": sd_consts,
                                                "VE": tr6_ve} if cc == 0 else None,
                                          inc="TE")
                        te_fold_tr_last = te_tr
                        v_wt = p.add("vector", lambda e: e.tensor_copy(
                            wTb[:], ptr6[0:64, :]),
                            wait={"TE": max(te_tr, wT_te)}, inc="VE")
                        tr6_ve = v_wt
                        te_mm = None
                        for cc in range(CCH):
                            te_mm = p.add("tensor", lambda e, cc=cc, bk=bk:
                                          e.matmul(pabank[bk][:, cc * 64:(cc + 1) * 64],
                                                   wTb[:, cc * 128:(cc + 1) * 128],
                                                   wf_sb[:], start=True, stop=True),
                                          wait={"VE": max(v_wt, mm_ve[bk]),
                                                "sWF": gd_wf} if cc == 0 else None,
                                          inc="TE")
                        wT_te = te_mm
                        mm_ve[bk] = p.add(
                            "vector",
                            lambda e, bk=bk, do=dst_off, hh=hh:
                            e.tensor_copy(
                                Wall_v[:, :, do + hh * 64: do + (hh + 1) * 64],
                                pabank[bk][:, 0:512].rearrange("p (c n) -> p c n", n=64)),
                            wait={"TE": te_mm}, inc="VE")
                ve_fold = p.cnt["VE"]
                if "wall" in dbg:
                    p.dma("sync", dbg["wall"], Wall[:], wait={"VE": ve_fold, "sWALL": gd_v}, sem="sDBG")
            esW.close()

            if True:
                wfbA = esA.enter_context(nc.sbuf_tensor("wfbA", [128, C], F32))
                wfbB = esA.enter_context(nc.sbuf_tensor("wfbB", [128, C], F32))
                wub = esA.enter_context(nc.sbuf_tensor("wub", [128, C], F32))
                wt1b = esA.enter_context(nc.sbuf_tensor("wt1b", [128, C], F32))
                whatbA = esA.enter_context(nc.sbuf_tensor("whatbA", [128, C], BF16))
                whatbB = esA.enter_context(nc.sbuf_tensor("whatbB", [128, C], BF16))
                wstgA = esA.enter_context(nc.sbuf_tensor("wstgA", [128, C], BF16))
                wstgB = esA.enter_context(nc.sbuf_tensor("wstgB", [128, C], BF16))
                wstat = esA.enter_context(nc.sbuf_tensor("wstat", [128, 8], F32))
                chunks = [("g", i) for i in range(NIC)] + [("u", i) for i in range(NIC)]
                wfb = [wfbA, wfbB]
                whatb = [whatbA, whatbB]
                wstg = [wstgA, wstgB]
                cp = {"s1": 0, "s2": 0,
                      "wfb_free": [{"TE": te_fold_tr_last}, {"TE": te_fold_tr_last}],
                      "what_free": [{}, {}], "wstg_free": [{}, {}],
                      "wstat_free": {}, "wub_free": {}, "what_v6": {}, "scales": {}}

                def prep_stage1():
                    if cp["s1"] >= len(chunks):
                        return
                    k = cp["s1"]
                    cp["s1"] += 1
                    kind, idx = chunks[k]
                    kb = k % 2
                    src_t = d_wg if kind == "g" else d_wu
                    gdv = p.dma("gpsimd", wfb[kb][:], src_t[idx * 128:(idx + 1) * 128, :],
                                wait=cp["wfb_free"][kb], sem=f"sWFB{kb}")
                    v1 = p.add("vector", lambda e, kb=kb: e.reduce_max(
                        wstat[:, 0:1], wfb[kb][:], AX.X, apply_absolute_value=True),
                        wait=merge({f"sWFB{kb}": gdv}, cp["wstat_free"]), inc="VE")
                    v2 = p.add("vector", lambda e: e.tensor_scalar(
                        wstat[:, 1:2], wstat[:, 0:1], 1.0 / 127.0, None, ALU.mult),
                        wait={"VE": v1}, keep_self=True, inc="VE")
                    dst = sg_all if kind == "g" else su_all
                    p.add("vector", lambda e, dst=dst, idx=idx: e.tensor_copy(
                        dst[:, idx:idx + 1], wstat[:, 1:2]),
                        wait={"VE": v2}, keep_self=True, inc="VE")
                    v3 = p.add("vector", lambda e: e.reciprocal(
                        wstat[:, 2:3], wstat[:, 1:2]),
                        wait={"VE": v2}, keep_self=True, inc="VE")
                    s1 = p.add("scalar", lambda e, kb=kb: e.activation(
                        wub[:], wfb[kb][:], AF.Identity, bias=constf[:, 1:2],
                        scale=wstat[:, 2:3]),
                        wait=merge({"VE": v3}, cp["wub_free"]), inc="SE")
                    cp["wfb_free"][kb] = {"SE": s1}
                    cp["wstat_free"] = {"SE": s1}
                    v4 = p.add("vector", lambda e: e.tensor_scalar(
                        wt1b[:], wub[:], MAGIC, 127.0, ALU.subtract, ALU.min),
                        wait={"SE": s1}, inc="VE")
                    cp["wub_free"] = {"VE": v4}
                    v6 = p.add("vector", lambda e, kb=kb: e.tensor_scalar(
                        whatb[kb][:], wt1b[:], -128.0, None, ALU.max),
                        wait=cp["what_free"][kb], inc="VE")
                    cp["what_v6"][k] = v6

                def prep_stage2(tr_psum, tr_free):
                    if cp["s2"] >= cp["s1"]:
                        return
                    k = cp["s2"]
                    cp["s2"] += 1
                    kind, idx = chunks[k]
                    kb = k % 2
                    v6 = cp["what_v6"].pop(k)
                    te_last = None
                    for j in range(8):
                        te_last = p.add("tensor", lambda e, j=j, kb=kb:
                                        e.transpose(tr_psum[:, j * 128:(j + 1) * 128],
                                                    whatb[kb][:, j * 128:(j + 1) * 128],
                                                    ident[:]),
                                        wait=merge({"VE": v6}, tr_free[0])
                                        if j == 0 else None, inc="TE")
                    cp["what_free"][kb] = {"TE": te_last}
                    se_ev = p.add("scalar", lambda e, kb=kb: e.activation(
                        wstg[kb][:], tr_psum[:, 0:1024], AF.Identity),
                        wait=merge({"TE": te_last}, cp["wstg_free"][kb]), inc="SE")
                    tr_free[0] = {"SE": se_ev}
                    mi = 0 if kind == "g" else 1
                    gdo = p.dma("gpsimd", d_wqT[mi, idx], wstg[kb][:],
                                wait={"SE": se_ev}, sem=f"sWSG{kb}")
                    cp["wstg_free"][kb] = {f"sWSG{kb}": gdo}

                v_pkv0 = p.add("vector", lambda e: e.memset(pkv[:], 0.0), inc="VE")
                v_pks0 = p.add("vector", lambda e: e.memset(pks[:], 0.0), inc="VE")
                xnfm = [xnfm0, xnfm1]
                xnfm_v = [xnfm0[:].rearrange("p (c n) -> p c n", n=128),
                          xnfm1[:].rearrange("p (c n) -> p c n", n=128)]
                ktok = [ktok0, ktok1]
                vtok = [vtok0, vtok1]
                qtok = [qtok0, qtok1]
                xnfm_free = [0, 0]
                ktok_free = [0, 0]
                vtok_free = [0, 0]
                qtok_free = [0, 0]
                bank_free = [0, 0, 0, 0]
                tr6_free = [ve_fold]
                tr7_free = [{}]
                xntok_free = [0]
                prev_v_var = 0
                kv_te_last = 0
                sd_x1_pb = [0] * NT

                def pa_front(tt):
                    bi = tt % 2
                    sdx = p.dma("sync", xb[bi][:], d_x[tt * 128:(tt + 1) * 128, :],
                                wait=xb_free[bi], sem=f"sXB{bi}")
                    s_sum = p.add("scalar", lambda e, bi=bi: e.activation(
                        sq[:], xb[bi][:], AF.Identity, accum_out=stat[:, 0:1]),
                        wait={f"sXB{bi}": sdx, "SE": pa_state["prev_s_var"]}, inc="SE")
                    s_sq = p.add("scalar", lambda e, bi=bi: e.activation(
                        sq[:], xb[bi][:], AF.Square, accum_out=stat[:, 1:2]),
                        inc="SE")
                    s_mean = p.add("scalar", lambda e: e.activation(
                        stat[:, 2:3], stat[:, 0:1], AF.Identity, scale=1.0 / C),
                        wait={"SE": s_sq}, keep_self=True, inc="SE")
                    s_m2 = p.add("scalar", lambda e: e.activation(
                        stat[:, 3:4], stat[:, 2:3], AF.Square),
                        wait={"SE": s_mean}, keep_self=True, inc="SE")
                    s_nm2 = p.add("scalar", lambda e: e.activation(
                        stat[:, 4:5], stat[:, 3:4], AF.Identity, scale=-1.0),
                        wait={"SE": s_m2}, keep_self=True, inc="SE")
                    s_var = p.add("scalar", lambda e: e.activation(
                        stat[:, 5:6], stat[:, 1:2], AF.Identity, scale=1.0 / C,
                        bias=stat[:, 4:5]),
                        wait={"SE": s_nm2}, keep_self=True, inc="SE")
                    s_sd = p.add("scalar", lambda e: e.activation(
                        stat[:, 6:7], stat[:, 5:6], AF.Sqrt, bias=constf[:, 0:1]),
                        wait={"SE": s_var}, keep_self=True, inc="SE")
                    pa_state["prev_s_var"] = s_sd
                    v_r = p.add("vector", lambda e: e.reciprocal(
                        stat[:, 7:8], stat[:, 6:7]), wait={"SE": s_sd}, inc="VE")
                    s_mr = p.add("scalar", lambda e: e.activation(
                        stat[:, 8:9], stat[:, 2:3], AF.Identity, scale=stat[:, 7:8]),
                        wait={"VE": v_r}, inc="SE")
                    s_xn = p.add("vector", lambda e, bi=bi: e.tensor_scalar(
                        xntok[:], xb[bi][:], stat[:, 7:8], stat[:, 8:9],
                        ALU.mult, ALU.subtract),
                        wait={"SE": s_mr, "TE": xntok_free[0]}, inc="VE")
                    xb_free[bi] = {"VE": s_xn}

                    te_tr = None
                    for j in range(8):
                        te_tr = p.add("tensor", lambda e, j=j: e.transpose(
                            ptr6[:, j * 128:(j + 1) * 128], xntok[:, j * 128:(j + 1) * 128],
                            ident[:]),
                            wait={"VE": max(s_xn, tr6_free[0])}
                            if j == 0 else None, inc="TE")
                    xntok_free[0] = te_tr
                    v_xnfm = p.add("vector", lambda e, bi=bi: e.tensor_copy(
                        xnfm[bi][:], ptr6[:, 0:1024]),
                        wait={"TE": max(te_tr, xnfm_free[bi])}, inc="VE")
                    tr6_free[0] = v_xnfm
                    pa_state["v_xnfm"][tt] = v_xnfm

                def pa_back(tt):
                    bi = tt % 2
                    v_xnfm = pa_state["v_xnfm"][tt]
                    grp_last_te = 0
                    for grp, bank, kind in ((0, 0, "k"), (1, 1, "k"), (2, 2, "v"),
                                            (3, 3, "v"), (4, 0, "q"), (5, 1, "q")):
                        te_g = None
                        for c in range(CCH):
                            last = (c == CCH - 1)
                            te_g = p.add(
                                "tensor",
                                lambda e, grp=grp, bank=bank, c=c, bi=bi, kind=kind:
                                e.matmul(pabank[bank][:, 0:512], xnfm_v[bi][:, c, :],
                                         Wall_v[:, c, grp * 512:(grp + 1) * 512],
                                         start=(c == 0),
                                         stop=(last and kind == "v")),
                                wait=merge({"VE": max(v_xnfm, ve_fold)},
                                           {"SE": bank_free[bank]},
                                           {"sWALL": gd_v}) if c == 0 else None,
                                inc="TE")
                        if kind != "v":
                            te_g = p.add("tensor", lambda e, grp=grp, bank=bank:
                                         e.matmul(pabank[bank][:, 0:512],
                                                  ones_row[0:1, 0:128],
                                                  bias_row[0:1, grp * 512:(grp + 1) * 512],
                                                  start=False, stop=True),
                                         wait={"VE": ve_consts}, inc="TE")
                        grp_last_te = te_g
                        if kind == "k":
                            se_e = p.add("scalar", lambda e, grp=grp, bank=bank, bi=bi:
                                         e.activation(ktok[bi][:, grp * 512:(grp + 1) * 512],
                                                      pabank[bank][:, 0:512], AF.Relu),
                                         wait={"TE": max(te_g, ktok_free[bi])}, inc="SE")
                        elif kind == "v":
                            se_e = p.add("scalar", lambda e, grp=grp, bank=bank, bi=bi:
                                         e.activation(vtok[bi][:, (grp - 2) * 512:(grp - 1) * 512],
                                                      pabank[bank][:, 0:512], AF.Identity),
                                         wait={"TE": max(te_g, vtok_free[bi])}, inc="SE")
                        else:
                            se_e = p.add("scalar", lambda e, grp=grp, bank=bank, bi=bi:
                                         e.activation(qtok[bi][:, (grp - 4) * 512:(grp - 3) * 512],
                                                      pabank[bank][:, 0:512], AF.Relu),
                                         wait={"TE": max(te_g, qtok_free[bi])}, inc="SE")
                        bank_free[bank] = se_e
                    se_kv_ready = max(bank_free)

                    te_kv = None
                    for hh in range(H):
                        pr = hh // 2
                        po = (hh % 2) * 64
                        te_kv = p.add("tensor", lambda e, hh=hh, pr=pr, po=po, bi=bi, tt=tt:
                                      e.matmul(pkv[po:po + 64, pr * 64:(pr + 1) * 64],
                                               ktok[bi][:, hh * 64:(hh + 1) * 64],
                                               vtok[bi][:, hh * 64:(hh + 1) * 64],
                                               start=False,
                                               stop=(tt == NT - 1 and hh == H - 1),
                                               skip_group_check=True),
                                      wait={"SE": se_kv_ready, "VE": v_pkv0}
                                      if hh == 0 else None, inc="TE")
                        te_kv = p.add("tensor", lambda e, hh=hh, po=po, bi=bi, tt=tt:
                                      e.matmul(pks[po:po + 64, hh:hh + 1],
                                               ktok[bi][:, hh * 64:(hh + 1) * 64],
                                               ones_col[:],
                                               start=False,
                                               stop=(tt == NT - 1 and hh == H - 1),
                                               skip_group_check=True),
                                      inc="TE")
                    ktok_free[bi] = te_kv
                    vtok_free[bi] = te_kv
                    pa_state["kv_te_last"] = te_kv

                    te_qt = None
                    for j in range(8):
                        te_qt = p.add("tensor", lambda e, j=j, bi=bi: e.transpose(
                            ptr7[:, j * 128:(j + 1) * 128], qtok[bi][:, j * 128:(j + 1) * 128],
                            ident[:]),
                            wait=merge({"SE": bank_free[1]}, tr7_free[0])
                            if j == 0 else None, inc="TE")
                    qtok_free[bi] = te_qt
                    v_qfm = p.add("vector", lambda e, tt=tt: e.tensor_copy(
                        bigv[:, :, tt * 128:(tt + 1) * 128], ptr7[:, 0:1024]),
                        wait={"TE": te_qt}, inc="VE")
                    tr7_free[0] = {"VE": v_qfm}
                    xnfm_free[bi] = grp_last_te

                    if tt == 0:
                        if "xnfm" in dbg:
                            p.dma("sync", dbg["xnfm"], xnfm0[:], wait={"VE": v_xnfm}, sem="sDBG")
                        if "stat0" in dbg:
                            p.dma("sync", dbg["stat0"], stat[:], wait={"VE": v_xnfm}, sem="sDBG")
                        if "ktok" in dbg:
                            p.dma("sync", dbg["ktok"], ktok0[:], wait={"SE": bank_free[1]}, sem="sDBG")
                        if "vtok" in dbg:
                            p.dma("sync", dbg["vtok"], vtok0[:], wait={"SE": bank_free[3]}, sem="sDBG")
                        if "qtok" in dbg:
                            p.dma("sync", dbg["qtok"], qtok0[:], wait={"SE": bank_free[1]}, sem="sDBG")

                pa_state = {"prev_s_var": 0, "v_xnfm": {}, "kv_te_last": 0}
                pa_front(0)
                for tt in range(NT):
                    if tt + 1 < NT:
                        pa_front(tt + 1)
                    want2 = min(cp["s1"], max(0, 2 * tt))
                    while cp["s2"] < want2:
                        prep_stage2(ptr7, tr7_free)
                    want1 = min(len(chunks), 2 * (tt + 1))
                    while cp["s1"] < want1:
                        prep_stage1()
                    pa_back(tt)
                kv_te_last = pa_state["kv_te_last"]
                while cp["s2"] < len(chunks):
                    prep_stage2(ptr7, tr7_free)
                ve_prep_pa = p.cnt["VE"]
                gd_prep_pa = (p.cnt.get("sWSG0", 0), p.cnt.get("sWSG1", 0))
                se_prep_pa = p.cnt["SE"]

                p.add("vector", lambda e: e.memset(kvx_sb[:], 0.0),
                      wait={"TE": kv_te_last}, inc="VE")
                p.add("vector", lambda e: e.memset(ks_sb[:], 0.0), inc="VE")
                for g in range(8):
                    p.add("vector", lambda e, g=g: e.tensor_copy(
                        kvx_sb[0:64, g * 128: g * 128 + 64],
                        pkv[0:64, g * 64:(g + 1) * 64]), inc="VE")
                    p.add("vector", lambda e, g=g: e.tensor_copy(
                        kvx_sb[64:128, g * 128 + 64: g * 128 + 128],
                        pkv[64:128, g * 64:(g + 1) * 64]), inc="VE")
                    p.add("vector", lambda e, g=g: e.tensor_copy(
                        ks_sb[0:64, g * 16 + 2 * g: g * 16 + 2 * g + 1],
                        pks[0:64, 2 * g:2 * g + 1]), inc="VE")
                    p.add("vector", lambda e, g=g: e.tensor_copy(
                        ks_sb[64:128, g * 16 + 2 * g + 1: g * 16 + 2 * g + 2],
                        pks[64:128, 2 * g + 1:2 * g + 2]), inc="VE")
                ve_kvx = p.cnt["VE"]
        esA.close()

        p.fence()

        esB = ExitStack()
        if True:
            WoutB = esB.enter_context(nc.sbuf_tensor("WoutB", [128, CCH * C], BF16))
            Awin0 = esB.enter_context(nc.sbuf_tensor("Awin0", [128, CCH * 512], BF16))
            Awin1 = esB.enter_context(nc.sbuf_tensor("Awin1", [128, CCH * 512], BF16))
            rdb0 = esB.enter_context(nc.sbuf_tensor("rdb0", [16, 512], BF16))
            rdb1 = esB.enter_context(nc.sbuf_tensor("rdb1", [16, 512], BF16))
            bcb0 = esB.enter_context(nc.sbuf_tensor("bcb0", [128, 512], F32))
            bcb1 = esB.enter_context(nc.sbuf_tensor("bcb1", [128, 512], F32))
            pnrm = esB.enter_context(nc.psum_tensor("pnrm", [128, 512], F32))
            pbc0 = esB.enter_context(nc.psum_tensor("pbc0", [128, 512], F32))
            pat0 = esB.enter_context(nc.psum_tensor("pat0", [128, 512], F32))
            pbc1 = esB.enter_context(nc.psum_tensor("pbc1", [128, 512], F32))
            pat1 = esB.enter_context(nc.psum_tensor("pat1", [128, 512], F32))
            po0 = esB.enter_context(nc.psum_tensor("po0", [128, 512], F32))
            po1 = esB.enter_context(nc.psum_tensor("po1", [128, 512], F32))
            ptrB = esB.enter_context(nc.psum_tensor("ptrB", [128, 1024], BF16))
            wfb_d = esB.enter_context(nc.sbuf_tensor("wfb_d", [128, NI], F32))
            wub_d = esB.enter_context(nc.sbuf_tensor("wub_d", [128, NI // 2], F32))
            wt1_d = esB.enter_context(nc.sbuf_tensor("wt1_d", [128, NI // 2], F32))
            what_d0 = esB.enter_context(nc.sbuf_tensor("what_d0", [128, NI], BF16))
            what_d1 = esB.enter_context(nc.sbuf_tensor("what_d1", [128, NI], BF16))
            wstg_d0 = esB.enter_context(nc.sbuf_tensor("wstg_d0", [128, NI], BF16))
            wstg_d1 = esB.enter_context(nc.sbuf_tensor("wstg_d1", [128, NI], BF16))
            wstat_d = esB.enter_context(nc.sbuf_tensor("wstat_d", [128, 4], F32))
            Wout_v = WoutB[:].rearrange("p (c n) -> p c n", n=C)
            for cc in range(CCH):
                p.dma("gpsimd", Wout_v[:, cc, :], d_wout[cc * 128:(cc + 1) * 128, :],
                      sem="sWOUT")
            gd_wout = p.cnt["sWOUT"]

            Awin = [Awin0, Awin1]
            Awin_v = [Awin0[:].rearrange("p (c n) -> p c n", n=512),
                      Awin1[:].rearrange("p (c n) -> p c n", n=512)]
            rd = [rdb0, rdb1]
            bc = [bcb0, bcb1]
            pbc = [pbc0, pbc1]
            pat = [pat0, pat1]
            Awin_free = [0, 0]
            rd_free = [0, 0]
            bc_free = [0, 0]
            pat_free = [0, 0]
            pbc_free = [0, 0]
            po_free = [0, 0]

            pb_state = {"ve_awin": {}, "prev_v_am": 0}

            def pb_front(ns):
                wi = ns % 2
                te_n = None
                for g in range(8):
                    te_n = p.add("tensor", lambda e, g=g, ns=ns: e.matmul(
                        pnrm[0:16, 0:512], ks_sb[:, g * 16:(g + 1) * 16],
                        bigv[:, g, ns * 512:(ns + 1) * 512],
                        start=(g == 0), stop=False),
                        wait={"VE": max(ve_kvx, pb_state.get("pnrm_free", 0))}
                        if g == 0 else None, inc="TE")
                te_n = p.add("tensor", lambda e: e.matmul(
                    pnrm[0:16, 0:512], eps_row[:], ones_row[0:1, 0:512],
                    start=False, stop=True), inc="TE")
                v_rd = p.add("vector", lambda e, wi=wi: e.reciprocal(
                    rd[wi][:], pnrm[0:16, 0:512]),
                    wait={"TE": max(te_n, rd_free[wi])}, inc="VE")
                pb_state["pnrm_free"] = v_rd

                for g in range(8):
                    pi = g % 2
                    te_bc = p.add("tensor", lambda e, g=g, pi=pi, wi=wi: e.matmul(
                        pbc[pi][:, 0:512], ind_sb[:, g * 128:(g + 1) * 128], rd[wi][:],
                        start=True, stop=True),
                        wait={"VE": v_rd, "SE": pbc_free[pi]}, inc="TE")
                    rd_free[wi] = te_bc
                    te_at = p.add("tensor", lambda e, g=g, pi=pi, ns=ns: e.matmul(
                        pat[pi][:, 0:512], kvx_sb[:, g * 128:(g + 1) * 128],
                        bigv[:, g, ns * 512:(ns + 1) * 512],
                        start=True, stop=True),
                        wait={"VE": pat_free[pi]}, inc="TE")
                    se_bc = p.add("scalar", lambda e, pi=pi: e.activation(
                        bc[pi][:], pbc[pi][:, 0:512], AF.Identity),
                        wait={"TE": te_bc, "VE": bc_free[pi]}, inc="SE")
                    pbc_free[pi] = se_bc
                    v_a = p.add("vector", lambda e, g=g, pi=pi, wi=wi: e.tensor_tensor(
                        Awin_v[wi][:, g, :], pat[pi][:, 0:512], bc[pi][:], ALU.mult),
                        wait={"TE": max(te_at, Awin_free[wi]), "SE": se_bc}, inc="VE")
                    pat_free[pi] = v_a
                    bc_free[pi] = v_a
                pb_state["ve_awin"][ns] = p.cnt["VE"]

            def pb_back(ns):
                wi = ns % 2
                ve_awin = pb_state["ve_awin"][ns]
                for tl in range(4):
                    tt = ns * 4 + tl
                    bi = tt % 2
                    sdx = p.dma("sync", xb[bi][:], d_x[tt * 128:(tt + 1) * 128, :],
                                wait=xb_free[bi], sem=f"sXB{bi}")
                    te_o = None
                    for c in range(CCH):
                        p.add("tensor", lambda e, c=c, tl=tl, wi=wi: e.matmul(
                            po0[:, 0:512], Awin_v[wi][:, c, tl * 128:(tl + 1) * 128],
                            Wout_v[:, c, 0:512], start=(c == 0), stop=False),
                            wait={"VE": max(ve_awin, po_free[0]), "sWOUT": gd_wout}
                            if c == 0 else None, inc="TE")
                        te_o = p.add("tensor", lambda e, c=c, tl=tl, wi=wi: e.matmul(
                            po1[:, 0:512], Awin_v[wi][:, c, tl * 128:(tl + 1) * 128],
                            Wout_v[:, c, 512:1024], start=(c == 0), stop=False),
                            wait={"VE": po_free[1]} if c == 0 else None, inc="TE")
                    p.add("tensor", lambda e: e.matmul(
                        po0[:, 0:512], ones_row[0:1, 0:128], bout_row[0:1, 0:512],
                        start=False, stop=True), wait={"sCONST2": gd_consts}, inc="TE")
                    te_o = p.add("tensor", lambda e: e.matmul(
                        po1[:, 0:512], ones_row[0:1, 0:128], bout_row[0:1, 512:1024],
                        start=False, stop=True), inc="TE")
                    if tl == 3:
                        Awin_free[wi] = te_o
                    v_x1a = p.add("vector", lambda e, bi=bi: e.scalar_tensor_tensor(
                        x1b[bi][:, 0:512], po0[:, 0:512], 1.0, xb[bi][:, 0:512],
                        ALU.mult, ALU.add),
                        wait=merge({"TE": te_o, f"sXB{bi}": sdx}, x1b_free[bi]), inc="VE")
                    v_x1b = p.add("vector", lambda e, bi=bi: e.scalar_tensor_tensor(
                        x1b[bi][:, 512:1024], po1[:, 0:512], 1.0, xb[bi][:, 512:1024],
                        ALU.mult, ALU.add), inc="VE")
                    po_free[0] = v_x1b
                    po_free[1] = v_x1b
                    xb_free[bi] = {"VE": v_x1b}
                    sd_x1 = p.dma("sync", d_x1[tt * 128:(tt + 1) * 128, :], x1b[bi][:],
                                  wait={"VE": v_x1b}, sem=f"sX1W{bi}")
                    sd_x1_pb[tt] = {f"sX1W{bi}": sd_x1}
                    x1b_free[bi] = {f"sX1W{bi}": sd_x1}
                    s_sum = p.add("scalar", lambda e, bi=bi: e.activation(
                        sq[:], x1b[bi][:], AF.Identity, accum_out=stat[:, 0:1]),
                        wait={"VE": max(v_x1b, pb_state["prev_v_am"])}, inc="SE")
                    s_sq2 = p.add("scalar", lambda e, bi=bi: e.activation(
                        sq[:], x1b[bi][:], AF.Square, accum_out=stat[:, 1:2]), inc="SE")
                    s_mean = p.add("scalar", lambda e: e.activation(
                        stat[:, 2:3], stat[:, 0:1], AF.Identity, scale=1.0 / C),
                        wait={"SE": s_sq2}, keep_self=True, inc="SE")
                    s_m2 = p.add("scalar", lambda e: e.activation(
                        stat[:, 3:4], stat[:, 2:3], AF.Square),
                        wait={"SE": s_mean}, keep_self=True, inc="SE")
                    s_nm2 = p.add("scalar", lambda e: e.activation(
                        stat[:, 4:5], stat[:, 3:4], AF.Identity, scale=-1.0),
                        wait={"SE": s_m2}, keep_self=True, inc="SE")
                    s_var = p.add("scalar", lambda e: e.activation(
                        stat[:, 5:6], stat[:, 1:2], AF.Identity, scale=1.0 / C,
                        bias=stat[:, 4:5]),
                        wait={"SE": s_nm2}, keep_self=True, inc="SE")
                    s_sd2 = p.add("scalar", lambda e: e.activation(
                        stat[:, 6:7], stat[:, 5:6], AF.Sqrt, bias=constf[:, 0:1]),
                        wait={"SE": s_var}, keep_self=True, inc="SE")
                    v_r2 = p.add("vector", lambda e, tt=tt: e.reciprocal(
                        r2_all[:, tt:tt + 1], stat[:, 6:7]), wait={"SE": s_sd2}, inc="VE")
                    v_nmr2 = p.add("vector", lambda e, tt=tt: e.tensor_scalar(
                        nmr2_all[:, tt:tt + 1], stat[:, 2:3], r2_all[:, tt:tt + 1], -1.0,
                        ALU.mult, ALU.mult),
                        wait={"VE": v_r2}, keep_self=True, inc="VE")
                    s_u = p.add("scalar", lambda e, bi=bi, tt=tt: e.activation(
                        sq[:], x1b[bi][:], AF.Identity, bias=nmr2_all[:, tt:tt + 1],
                        scale=r2_all[:, tt:tt + 1]),
                        wait={"VE": v_nmr2}, inc="SE")
                    v_am = p.add("vector", lambda e: e.reduce_max(
                        stat[:, 8:9], sq[:], AX.X, apply_absolute_value=True),
                        wait={"SE": s_u}, inc="VE")
                    amo, amn = 10 + (tt % 2), 10 + ((tt + 1) % 2)
                    pb_state["prev_v_am"] = p.add(
                        "vector", lambda e, amo=amo, amn=amn: e.tensor_scalar(
                            stat[:, amn:amn + 1], stat[:, 8:9], stat[:, amo:amo + 1], None,
                            ALU.max), wait={"VE": v_am}, keep_self=True, inc="VE")

            what_d = [what_d0, what_d1]
            wstg_d = [wstg_d0, wstg_d1]
            dp = {"s1": 0, "s2": 0, "wfb_free": {}, "wub_free": {}, "wstat_free": {},
                  "what_free": [{}, {}], "wstg_free": [{}, {}], "v6": {},
                  "trB_free": {}}

            def dprep_stage1():
                if dp["s1"] >= CCH:
                    return
                j = dp["s1"]
                dp["s1"] += 1
                kb = j % 2
                gdv = p.dma("gpsimd", wfb_d[:], d_wd[j * 128:(j + 1) * 128, :],
                            wait=dp["wfb_free"], sem="sDL")
                v1 = p.add("vector", lambda e: e.reduce_max(
                    wstat_d[:, 0:1], wfb_d[:], AX.X, apply_absolute_value=True),
                    wait=merge({"sDL": gdv}, dp["wstat_free"]), inc="VE")
                v2 = p.add("vector", lambda e: e.tensor_scalar(
                    wstat_d[:, 1:2], wstat_d[:, 0:1], 1.0 / 127.0, None, ALU.mult),
                    wait={"VE": v1}, keep_self=True, inc="VE")
                v3 = p.add("vector", lambda e: e.reciprocal(
                    wstat_d[:, 2:3], wstat_d[:, 1:2]),
                    wait={"VE": v2}, keep_self=True, inc="VE")
                s1v = None
                v6 = None
                for p0 in range(0, NI, NI // 2):
                    w = NI // 2
                    s1v = p.add("scalar", lambda e, p0=p0, w=w: e.activation(
                        wub_d[:, 0:w], wfb_d[:, p0:p0 + w], AF.Identity,
                        bias=constf[:, 1:2], scale=wstat_d[:, 2:3]),
                        wait=merge({"VE": v3}, dp["wub_free"]), inc="SE")
                    v4 = p.add("vector", lambda e, w=w: e.tensor_scalar(
                        wt1_d[:, 0:w], wub_d[:, 0:w], MAGIC, 127.0,
                        ALU.subtract, ALU.min), wait={"SE": s1v}, inc="VE")
                    p.add("vector", lambda e, w=w: e.tensor_scalar(
                        wub_d[:, 0:w], wt1_d[:, 0:w], -128.0, None, ALU.max), inc="VE")
                    v6 = p.add("vector", lambda e, kb=kb, p0=p0, w=w: e.tensor_scalar(
                        what_d[kb][:, p0:p0 + w], wub_d[:, 0:w], wstat_d[:, 1:2], None,
                        ALU.mult), wait=dp["what_free"][kb], inc="VE")
                    dp["wub_free"] = {"VE": v6}
                dp["wfb_free"] = {"SE": s1v}
                dp["wstat_free"] = {"SE": s1v}
                dp["v6"][j] = v6

            def dprep_stage2():
                if dp["s2"] >= dp["s1"]:
                    return
                j = dp["s2"]
                dp["s2"] += 1
                kb = j % 2
                v6 = dp["v6"].pop(j)
                te_last = None
                for r0 in range(0, NIC, 8):
                    rblks = min(8, NIC - r0)
                    for jj in range(rblks):
                        te_last = p.add("tensor", lambda e, r0=r0, jj=jj, kb=kb:
                                        e.transpose(ptrB[:, jj * 128:(jj + 1) * 128],
                                                    what_d[kb][:, (r0 + jj) * 128:(r0 + jj + 1) * 128],
                                                    ident[:]),
                                        wait=merge({"VE": v6}, dp["trB_free"])
                                        if jj == 0 else None, inc="TE")
                    se_ev = p.add("scalar", lambda e, r0=r0, rblks=rblks, kb=kb:
                                  e.activation(wstg_d[kb][:, r0 * 128:(r0 + rblks) * 128],
                                               ptrB[:, 0:rblks * 128], AF.Identity),
                                  wait=merge({"TE": te_last}, dp["wstg_free"][kb]),
                                  inc="SE")
                    dp["trB_free"] = {"SE": se_ev}
                dp["what_free"][kb] = {"TE": te_last}
                gdo = p.dma("gpsimd", d_wdT[:, j].transpose([1, 0, 2]),
                            wstg_d[kb][:].rearrange("p (i n) -> p i n", n=128),
                            wait=dp["trB_free"], sem=f"sDS{kb}")
                dp["wstg_free"][kb] = {f"sDS{kb}": gdo}

            pb_front(0)
            for ns in range(8):
                if ns + 1 < 8:
                    pb_front(ns + 1)
                dprep_stage1()
                pb_back(ns)
                dprep_stage2()
            while dp["s2"] < CCH:
                dprep_stage2()
            gd_prep_pb = (p.cnt.get("sDS0", 0), p.cnt.get("sDS1", 0))
            ve_pb = p.cnt["VE"]
            sd_pb = dict(p.cnt)

            if "x1" in dbg:
                for t2 in range(NT):
                    p.dma("sync", dbg["x1"][t2 * 128:(t2 + 1) * 128, :],
                          d_x1[t2 * 128:(t2 + 1) * 128, :],
                          wait={"sX1W0": p.cnt.get("sX1W0", 0),
                                "sX1W1": p.cnt.get("sX1W1", 0)}, sem="sDBG")
            if "q_fm" in dbg:
                p.dma("sync", dbg["q_fm"], big[:], wait={"VE": ve_pb}, sem="sDBG")
            if "kvx" in dbg:
                p.dma("sync", dbg["kvx"], kvx_sb[:], wait={"VE": ve_pb}, sem="sDBG")
            if "ks" in dbg:
                p.dma("sync", dbg["ks"], ks_sb[:], wait={"VE": ve_pb}, sem="sDBG")
        esB.close()

        def barrier(acc_ap, cc_in, cc_out, scr, s_col, rs_col, clip_sb, wait_d):
            sd1 = p.dma("sync", cc_in, acc_ap, wait=wait_d, sem="sBAR")
            ge = p.add("gpsimd", lambda e, ci=cc_in, co=cc_out: e.collective_compute(
                "AllReduce", ALU.max, replica_groups=[list(range(N_CORES))],
                ins=[ci], outs=[co]),
                wait={"sBAR": sd1}, inc="GE")
            sd2 = p.dma("sync", grow[:], cc_out.rearrange("a one -> one a"),
                        wait={"GE": ge}, sem="sBAR")
            v1 = p.add("vector", lambda e: e.reduce_max(gcol[:], grow[:], AX.X),
                       wait={"sBAR": sd2}, inc="VE")
            v2 = p.add("vector", lambda e: e.tensor_scalar(
                grow[:], ones_f32[:], gcol[0:1, 0:1], None, ALU.mult),
                wait={"VE": v1}, keep_self=True, inc="VE")
            sd3 = p.dma("sync", scr.rearrange("a one -> one a"), grow[:],
                        wait={"VE": v2}, sem="sBAR")
            sd4 = p.dma("sync", gbc[:], scr, wait={"sBAR": sd3}, sem="sBAR")
            s_sig = p.add("scalar", lambda e, cs=clip_sb: e.activation(
                sig_col[:], cs[:], AF.Sigmoid),
                wait={"sBAR": sd4, "sCONST": sd_consts}, inc="SE")
            v3 = p.add("vector", lambda e, s=s_col: e.scalar_tensor_tensor(
                s[:], sig_col[:], 1.0 / 127.0, gbc[:], ALU.mult, ALU.mult),
                wait={"SE": s_sig, "sBAR": sd4}, inc="VE")
            v4 = p.add("vector", lambda e, s=s_col, r=rs_col: e.reciprocal(r[:], s[:]),
                       wait={"VE": v3}, keep_self=True, inc="VE")
            return v4

        v_b1 = barrier(stat[:, 10:11], d_cc1_in, d_cc1_out, d_s128a, s1_col, rs1_col,
                       clipx_sb, {"VE": ve_pb})
        p.add("vector", lambda e: e.tensor_scalar(
            uscale_all[:], r2_all[:], rs1_col[:], None, ALU.mult),
            wait={"VE": v_b1}, keep_self=True, inc="VE")
        p.add("vector", lambda e: e.tensor_scalar(
            ubias_all[:], nmr2_all[:], rs1_col[:], None, ALU.mult), inc="VE")
        p.add("vector", lambda e: e.tensor_scalar(
            s1sg_all[:], sg_all[:], s1_col[:], None, ALU.mult),
            wait={"VE": ve_prep_pa}, inc="VE")
        v_b1e = p.add("vector", lambda e: e.tensor_scalar(
            s1su_all[:], su_all[:], s1_col[:], None, ALU.mult), inc="VE")
        if "s1" in dbg:
            p.dma("sync", dbg["s1"], s1_col[:], wait={"VE": v_b1e}, sem="sDBG")

        p.fence()

        esC = ExitStack()
        if True:
            wqTsb = esC.enter_context(nc.sbuf_tensor("wqTsb", [128, 2 * NIC * C], BF16))
            ubuf = esC.enter_context(nc.sbuf_tensor("ubuf", [128, C], F32))
            t1buf = esC.enter_context(nc.sbuf_tensor("t1buf", [128, C], F32))
            xqtok = esC.enter_context(nc.sbuf_tensor("xqtok", [128, C], BF16))
            sgb0 = esC.enter_context(nc.sbuf_tensor("sgb0", [128, 512], BF16))
            sgb1 = esC.enter_context(nc.sbuf_tensor("sgb1", [128, 512], BF16))
            hb0 = esC.enter_context(nc.sbuf_tensor("hb0", [128, 512], F32))
            hb1 = esC.enter_context(nc.sbuf_tensor("hb1", [128, 512], F32))
            hb2 = esC.enter_context(nc.sbuf_tensor("hb2", [128, 512], F32))
            pg0 = esC.enter_context(nc.psum_tensor("pg0", [128, 512], F32))
            pu0 = esC.enter_context(nc.psum_tensor("pu0", [128, 512], F32))
            pg1 = esC.enter_context(nc.psum_tensor("pg1", [128, 512], F32))
            pu1 = esC.enter_context(nc.psum_tensor("pu1", [128, 512], F32))
            pg2 = esC.enter_context(nc.psum_tensor("pg2", [128, 512], F32))
            pu2 = esC.enter_context(nc.psum_tensor("pu2", [128, 512], F32))
            ptrC = esC.enter_context(nc.psum_tensor("ptrC", [128, 1024], BF16))
            wqT_v = wqTsb[:].rearrange("p (m i c) -> p m i c", m=2, i=NIC)
            for mi in range(2):
                for ii in range(NIC):
                    p.dma("gpsimd", wqT_v[:, mi, ii, :], d_wqT[mi, ii],
                          wait={"sWSG0": gd_prep_pa[0], "sWSG1": gd_prep_pa[1]},
                          sem="sWQT")
            gd_wqT = p.cnt["sWQT"]

            xq_evac_ve = [0] * NT
            x1_free = [dict(x1b_free[0]), dict(x1b_free[1])]
            trC_free = [0]
            ubuf_free = [0]
            xqtok_free2 = [0]

            def emit_xq_tile(tt):
                bi = tt % 2
                sdx = p.dma("sync", x1b[bi][:], d_x1[tt * 128:(tt + 1) * 128, :],
                            wait=merge(x1_free[bi], sd_x1_pb[tt]), sem=f"sX1R{bi}")
                s_u = p.add("scalar", lambda e, bi=bi, tt=tt: e.activation(
                    ubuf[:], x1b[bi][:], AF.Identity, bias=ubias_all[:, tt:tt + 1],
                    scale=uscale_all[:, tt:tt + 1]),
                    wait={f"sX1R{bi}": sdx, "VE": max(v_b1e, ubuf_free[0])}, inc="SE")
                v_t1 = p.add("vector", lambda e: e.tensor_scalar(
                    t1buf[:], ubuf[:], MAGIC, MAGIC, ALU.add, ALU.subtract),
                    wait={"SE": s_u}, inc="VE")
                ubuf_free[0] = v_t1
                v_xq = p.add("vector", lambda e: e.tensor_scalar(
                    xqtok[:], t1buf[:], 127.0, -128.0, ALU.min, ALU.max),
                    wait={"TE": xqtok_free2[0]}, inc="VE")
                x1_free[bi] = {"VE": v_xq}
                te_tr = None
                for j in range(8):
                    te_tr = p.add("tensor", lambda e, j=j: e.transpose(
                        ptrC[:, j * 128:(j + 1) * 128], xqtok[:, j * 128:(j + 1) * 128],
                        ident[:]),
                        wait={"VE": max(v_xq, trC_free[0])} if j == 0 else None, inc="TE")
                xqtok_free2[0] = te_tr
                v_ev = p.add("vector", lambda e, tt=tt: e.tensor_copy(
                    bigv[:, :, tt * 128:(tt + 1) * 128], ptrC[:, 0:1024]),
                    wait={"TE": te_tr}, inc="VE")
                trC_free[0] = v_ev
                xq_evac_ve[tt] = v_ev

            for tt in range(8):
                emit_xq_tile(tt)

            pg = [pg0, pg1, pg2]
            pu = [pu0, pu1, pu2]
            sgb = [sgb0, sgb1]
            hb = [hb0, hb1, hb2]
            pg_free = [0, 0, 0]
            pu_free = [0, 0, 0]
            sg_free = [0, 0]
            hb_free = [{}, {}, {}]
            prev_hm = 0
            hi = 0

            for ns in range(8):
                for t3 in range(8 + ns * 3, min(NT, 8 + (ns + 1) * 3)):
                    emit_xq_tile(t3)
                need_ve = xq_evac_ve[4 * ns + 3]
                for ii in range(NIC):
                    pi = (ns * NIC + ii) % 3
                    te_g = None
                    for hbk in range(CCH):
                        te_g = p.add("tensor", lambda e, ii=ii, hbk=hbk, ns=ns, pi=pi:
                                     e.matmul(pg[pi][:, 0:512],
                                              wqT_v[:, 0, ii, hbk * 128:(hbk + 1) * 128],
                                              bigv[:, hbk, ns * 512:(ns + 1) * 512],
                                              start=(hbk == 0), stop=(hbk == CCH - 1)),
                                     wait={"VE": need_ve, "sWQT": gd_wqT, "SE": pg_free[pi]}
                                     if hbk == 0 else None, inc="TE")
                    te_u = None
                    for hbk in range(CCH):
                        te_u = p.add("tensor", lambda e, ii=ii, hbk=hbk, ns=ns, pi=pi:
                                     e.matmul(pu[pi][:, 0:512],
                                              wqT_v[:, 1, ii, hbk * 128:(hbk + 1) * 128],
                                              bigv[:, hbk, ns * 512:(ns + 1) * 512],
                                              start=(hbk == 0), stop=(hbk == CCH - 1)),
                                     wait={"VE": pu_free[pi]} if hbk == 0 else None,
                                     inc="TE")
                    si = (ns * NIC + ii) % 2
                    s_sg = p.add("scalar", lambda e, ii=ii, pi=pi, si=si: e.activation(
                        sgb[si][:], pg[pi][:, 0:512], AF.Silu,
                        scale=s1sg_all[:, ii:ii + 1]),
                        wait={"TE": te_g, "VE": max(v_b1e, sg_free[si])}, inc="SE")
                    pg_free[pi] = s_sg
                    hbi = hi % 3
                    hi += 1
                    v_h = p.add("vector", lambda e, ii=ii, pi=pi, si=si, hbi=hbi:
                                e.scalar_tensor_tensor(hb[hbi][:], pu[pi][:, 0:512],
                                                       s1su_all[:, ii:ii + 1], sgb[si][:],
                                                       ALU.mult, ALU.mult),
                                wait=merge({"TE": te_u, "SE": s_sg}, hb_free[hbi]),
                                inc="VE")
                    pu_free[pi] = v_h
                    sg_free[si] = v_h
                    v_hm = p.add("vector", lambda e, hbi=hbi: e.reduce_max(
                        stat[:, 9:10], hb[hbi][:], AX.X, apply_absolute_value=True),
                        wait={"VE": prev_hm}, inc="VE")
                    kk = ns * NIC + ii
                    hmo, hmn = 12 + (kk % 2), 12 + ((kk + 1) % 2)
                    prev_hm = p.add("vector", lambda e, hmo=hmo, hmn=hmn: e.tensor_scalar(
                        stat[:, hmn:hmn + 1], stat[:, 9:10], stat[:, hmo:hmo + 1], None,
                        ALU.max), wait={"VE": v_hm}, keep_self=True, inc="VE")
                    sd_h = p.dma("sync", d_h[ii][:, ns * 512:(ns + 1) * 512], hb[hbi][:],
                                 wait={"VE": v_h}, sem=f"sHW{hbi}")
                    hb_free[hbi] = {f"sHW{hbi}": sd_h}
            ve_pc = p.cnt["VE"]
            sd_pc = {f"sHW{k}": p.cnt.get(f"sHW{k}", 0) for k in range(3)}
            if "xq" in dbg:
                p.dma("sync", dbg["xq"], big[:], wait={"VE": ve_pc}, sem="sDBG")
            if "h0" in dbg:
                p.dma("sync", dbg["h0"], d_h[0], wait=sd_pc, sem="sDBG")
        esC.close()

        if "hmax" in dbg:
            p.dma("sync", dbg["hmax"], stat[:, 12:13], wait={"VE": ve_pc}, sem="sDBG")
        if "hall" in dbg:
            for ci2 in range(NIC):
                p.dma("sync", dbg["hall"][ci2 * 128:(ci2 + 1) * 128, :], d_h[ci2],
                      wait=sd_pc, sem="sDBG")
        v_b2 = barrier(stat[:, 12:13], d_cc2_in, d_cc2_out, d_s128b, s2_col, rs2_col,
                       clipd_sb, {"VE": ve_pc})
        if "s2" in dbg:
            p.dma("sync", dbg["s2"], s2_col[:], wait={"VE": v_b2}, sem="sDBG")

        esBig.close()
        p.fence()

        esD = ExitStack()
        if True:
            wdTsb = esD.enter_context(nc.sbuf_tensor("wdTsb", [128, NIC * C], BF16))
            hqs0 = esD.enter_context(nc.sbuf_tensor("hqs0", [128, NIC * 512], BF16))
            hqs1 = esD.enter_context(nc.sbuf_tensor("hqs1", [128, NIC * 512], BF16))
            hch0 = esD.enter_context(nc.sbuf_tensor("hch0", [128, 512], F32))
            hch1 = esD.enter_context(nc.sbuf_tensor("hch1", [128, 512], F32))
            hch2 = esD.enter_context(nc.sbuf_tensor("hch2", [128, 512], F32))
            uch0 = esD.enter_context(nc.sbuf_tensor("uch0", [128, 512], F32))
            uch1 = esD.enter_context(nc.sbuf_tensor("uch1", [128, 512], F32))
            tch0 = esD.enter_context(nc.sbuf_tensor("tch0", [128, 512], F32))
            tch1 = esD.enter_context(nc.sbuf_tensor("tch1", [128, 512], F32))
            outb0 = esD.enter_context(nc.sbuf_tensor("outb0", [128, C], F32))
            outb1 = esD.enter_context(nc.sbuf_tensor("outb1", [128, C], F32))
            pd0 = esD.enter_context(nc.psum_tensor("pd0", [128, 512], F32))
            pd1 = esD.enter_context(nc.psum_tensor("pd1", [128, 512], F32))
            pd2 = esD.enter_context(nc.psum_tensor("pd2", [128, 512], F32))
            pd3 = esD.enter_context(nc.psum_tensor("pd3", [128, 512], F32))

            wdT_v = wdTsb[:].rearrange("p (i c) -> p i c", i=NIC)
            wdT_ld = []
            for ci in range(NIC):
                v = p.dma("gpsimd", wdT_v[:, ci, :].rearrange("p (c b) -> p c b", b=128),
                          d_wdT[ci].transpose([1, 0, 2]),
                          wait={"sDS0": gd_prep_pb[0], "sDS1": gd_prep_pb[1]},
                          sem=f"sWDT{ci}")
                wdT_ld.append(v)

            hqs = [hqs0, hqs1]
            hqsv = [h[:].rearrange("p (i n) -> p i n", n=512) for h in hqs]
            hch = [hch0, hch1, hch2]
            uch = [uch0, uch1]
            tch = [tch0, tch1]
            outb = [outb0, outb1]
            pdk = [(pd0, pd1), (pd2, pd3)]
            hch_free = [{}, {}, {}]
            uch_free = [{}, {}]
            tch_free = [{}, {}]
            hqs_free = [{}, {}]
            pd_free = [0, 0]
            outb_free = [0, 0]
            sd_out_last = None
            v_hq_last_arr = [0] * 8

            def pd_quant(ns):
                si = ns % 2
                v_hq_last = None
                for ci in range(NIC):
                    k = ns * NIC + ci
                    h3 = k % 3
                    u2 = k % 2
                    sdh = p.dma("sync", hch[h3][:],
                                d_h[ci][:, ns * 512:(ns + 1) * 512],
                                wait=merge(hch_free[h3], sd_pc if k < 3 else None),
                                sem=f"sHR{h3}")
                    if k == 0 and "pdu" in dbg:
                        sd_d1 = p.dma("sync", dbg["pdu"][:, 0:512], hch[h3][:],
                                      wait={f"sHR{h3}": sdh}, sem="sDBG")
                        hch_free[h3] = merge(hch_free[h3], {"sDBG": sd_d1})
                    s_u = p.add("scalar", lambda e, h3=h3, u2=u2: e.activation(
                        uch[u2][:], hch[h3][:], AF.Identity, bias=constf[:, 1:2],
                        scale=rs2_col[:]),
                        wait=merge({f"sHR{h3}": sdh, "VE": v_b2}, uch_free[u2]), inc="SE")
                    hch_free[h3] = merge(hch_free[h3], {"SE": s_u})
                    if k == 0 and "pdu" in dbg:
                        sd_d2 = p.dma("sync", dbg["pdu"][:, 512:1024], uch[u2][:],
                                      wait={"SE": s_u}, sem="sDBG")
                        uch_free[u2] = merge(uch_free[u2], {"sDBG": sd_d2})
                    v_t1 = p.add("vector", lambda e, u2=u2: e.tensor_scalar(
                        tch[u2][:], uch[u2][:], MAGIC, 127.0, ALU.subtract, ALU.min),
                        wait=merge({"SE": s_u}, tch_free[u2]), inc="VE")
                    uch_free[u2] = {"VE": v_t1}
                    if k == 0 and "pdu" in dbg:
                        sd_d3 = p.dma("sync", dbg["pdu"][:, 1024:1536], tch[u2][:],
                                      wait={"VE": v_t1}, sem="sDBG")
                        tch_free[u2] = merge(tch_free[u2], {"sDBG": sd_d3})
                    v_hq = p.add("vector", lambda e, si=si, ci=ci, u2=u2:
                        e.tensor_scalar(hqsv[si][:, ci, :], tch[u2][:], -128.0, None,
                                        ALU.max),
                        wait=merge({"VE": v_t1}, hqs_free[si] if ci == 0 else None),
                        keep_self=True, inc="VE")
                    tch_free[u2] = {"VE": v_hq}
                    v_hq_last = v_hq
                v_hq_last_arr[ns] = v_hq_last
                if ns == 0 and "hqs" in dbg:
                    sd_hqd = p.dma("sync", dbg["hqs"], hqs0[:], wait={"VE": v_hq_last},
                                   sem="sDBG")
                    hqs_free[0] = merge(hqs_free[0], {"sDBG": sd_hqd})

            def pd_mms(ns):
                si = ns % 2
                v_hq_last = v_hq_last_arr[ns]
                if ns == 0 and "wdts" in dbg:
                    p.dma("sync", dbg["wdts"], wdTsb[:],
                          wait={f"sWDT{c2}": wdT_ld[c2] for c2 in range(NIC)}, sem="sDBG")
                for tl in range(4):
                    tt = ns * 4 + tl
                    bi = tt % 2
                    sdx = p.dma("gpsimd", x1b[bi][:], d_x1[tt * 128:(tt + 1) * 128, :],
                                wait=merge(x1_free[bi], sd_x1_pb[tt]), sem=f"sX1R{bi}")
                    pdi = tt % 2
                    pda, pdb = pdk[pdi]
                    te_d = None
                    for ci in range(NIC):
                        p.add("tensor", lambda e, ci=ci, si=si, tl=tl, pda=pda: e.matmul(
                            pda[:, 0:512], hqsv[si][:, ci, tl * 128:(tl + 1) * 128],
                            wdT_v[:, ci, 0:512], start=(ci == 0), stop=(ci == NIC - 1)),
                            wait=merge({"VE": max(v_hq_last, pd_free[pdi])}
                                       if ci == 0 else None,
                                       {f"sWDT{ci}": wdT_ld[ci]}), inc="TE")
                        te_d = p.add("tensor", lambda e, ci=ci, si=si, tl=tl, pdb=pdb: e.matmul(
                            pdb[:, 0:512], hqsv[si][:, ci, tl * 128:(tl + 1) * 128],
                            wdT_v[:, ci, 512:1024], start=(ci == 0), stop=(ci == NIC - 1)),
                            inc="TE")
                    if tl == 3:
                        hqs_free[si] = merge({}, {"TE": te_d})
                    v_o1 = p.add("vector", lambda e, bi=bi, pda=pda: e.scalar_tensor_tensor(
                        outb[bi][:, 0:512], pda[:, 0:512], s2_col[:], x1b[bi][:, 0:512],
                        ALU.mult, ALU.add),
                        wait=merge({"TE": te_d, f"sX1R{bi}": sdx},
                                   {f"sOUT{bi}": outb_free[bi]}), inc="VE")
                    v_o2 = p.add("vector", lambda e, bi=bi, pdb=pdb: e.scalar_tensor_tensor(
                        outb[bi][:, 512:1024], pdb[:, 0:512], x1b[bi]
                        if False else s2_col[:], x1b[bi][:, 512:1024],
                        ALU.mult, ALU.add), inc="VE")
                    pd_free[pdi] = v_o2
                    x1_free[bi] = {"VE": v_o2}
                    sd_o = p.dma("sync", d_out[tt * 128:(tt + 1) * 128, :], outb[bi][:],
                                 wait={"VE": v_o2}, sem=f"sOUT{bi}")
                    outb_free[bi] = sd_o

            pd_quant(0)
            for ns in range(8):
                if ns + 1 < 8:
                    pd_quant(ns + 1)
                pd_mms(ns)

            p.add("sync", lambda e: None,
                  wait={"sOUT0": p.cnt["sOUT0"], "sOUT1": p.cnt["sOUT1"]})
        esD.close()

        with nc.allow_low_precision(reason="bf16 matmul operands / quantized codes by design; fp32 accumulation everywhere"):
            p.flush(blk)
    return nc


_NC_CACHE = {}


def _consts():
    ident = np.eye(128, dtype=np.float32).astype(BF)
    ones_col = np.ones((128, 1), np.float32).astype(BF)
    ones_row = np.ones((1, 1024), np.float32).astype(BF)
    ones_f32 = np.ones((1, 128), np.float32)
    eps_row = np.full((1, 16), 1e-6, np.float32).astype(BF)
    ind = np.zeros((8, 16, 128), np.float32)
    for g in range(8):
        ind[g, 2 * g, 0:64] = 1.0
        ind[g, 2 * g + 1, 64:128] = 1.0
    return ident, ones_col, ones_row, ones_f32, eps_row, ind.astype(BF)


def kernel(L0_attn_x, L0_attn_W_qkv, L0_attn_W_f, L0_attn_b_f, L0_attn_W_out,
           L0_attn_b_out, L0_ffn_clip_x, L0_ffn_clip_down, L0_ffn_w_gate,
           L0_ffn_w_up, L0_ffn_w_down):
    x = np.ascontiguousarray(np.asarray(L0_attn_x, dtype=np.float32))
    ident, ones_col, ones_row, ones_f32, eps_row, ind = _consts()
    shared = {
        "wqkv": np.ascontiguousarray(np.asarray(L0_attn_W_qkv, np.float32)),
        "wf": np.ascontiguousarray(np.asarray(L0_attn_W_f, np.float32)),
        "bfv": np.ascontiguousarray(np.asarray(L0_attn_b_f, np.float32).reshape(1, DH)),
        "wout": np.ascontiguousarray(np.asarray(L0_attn_W_out, np.float32)),
        "bout": np.ascontiguousarray(np.asarray(L0_attn_b_out, np.float32).reshape(1, C)),
        "clipx": np.ascontiguousarray(np.broadcast_to(
            np.asarray(L0_ffn_clip_x, np.float32).reshape(1, 1), (128, 1))),
        "clipd": np.ascontiguousarray(np.broadcast_to(
            np.asarray(L0_ffn_clip_down, np.float32).reshape(1, 1), (128, 1))),
        "wg": np.ascontiguousarray(np.asarray(L0_ffn_w_gate, np.float32)),
        "wu": np.ascontiguousarray(np.asarray(L0_ffn_w_up, np.float32)),
        "wd": np.ascontiguousarray(np.asarray(L0_ffn_w_down, np.float32)),
        "ident": ident, "ones_col": ones_col, "ones_row": ones_row,
        "ones_f32": ones_f32, "eps_row": eps_row, "ind": ind,
        "constf": np.ascontiguousarray(np.broadcast_to(
            np.array([[EPS, MAGIC]], np.float32), (128, 2))),
    }
    if "nc" not in _NC_CACHE:
        _NC_CACHE["nc"] = build()
    nc = _NC_CACHE["nc"]
    in_maps = [dict(shared, x=np.ascontiguousarray(x[b])) for b in range(N_CORES)]
    res = run_bass_kernel_spmd(nc, in_maps, core_ids=list(range(N_CORES)))
    out = np.stack([np.asarray(res.results[b]["out"]) for b in range(N_CORES)], axis=0)
    return out


# version 11
# speedup vs baseline: 1.3028x; 1.0387x over previous
import numpy as np
import ml_dtypes
from contextlib import ExitStack

import concourse.bass as bass
import concourse.mybir as mybir
from concourse.bass_utils import run_bass_kernel_spmd

F32 = mybir.dt.float32
BF16 = mybir.dt.bfloat16
AF = mybir.ActivationFunctionType
ALU = mybir.AluOpType
AX = mybir.AxisListType
BF = ml_dtypes.bfloat16

MAGIC = float(np.float32(1.5 * 2 ** 23))
N_CORES = 8
SEQ = 4096
C = 1024
NT = SEQ // 128
CCH = 8
H = 16
DH = 64
NI = 2816
NIC = NI // 128
EPS = 1e-5

COMPUTE_SEMS = {"TE", "VE", "SE", "GE"}


def merge(*ds):
    out = {}
    for d in ds:
        for k, v in (d or {}).items():
            if v:
                out[k] = max(out.get(k, 0), v)
    return out


class Prog:
    ENGS = ("sync", "tensor", "vector", "scalar", "gpsimd")

    def __init__(self, nc, es):
        self.nc = nc
        self.es = es
        self.sems = {}
        self.cnt = {}
        self.streams = {e: [] for e in self.ENGS}
        self.lastwait = {e: {} for e in self.ENGS}
        self.eng_of_sem = {"TE": "tensor", "VE": "vector", "SE": "scalar",
                           "GE": "gpsimd"}
        for name in self.eng_of_sem:
            self.sems[name] = es.enter_context(nc.semaphore(name))
            self.cnt[name] = 0

    def sem(self, name):
        if name not in self.sems:
            self.sems[name] = self.es.enter_context(self.nc.semaphore(name))
            self.cnt[name] = 0
        return name

    SERIAL = False

    def add(self, eng, fn, wait=None, inc=None, inc_amt=1, keep_self=False):
        if Prog.SERIAL and inc is not None:
            wait = merge(wait, getattr(self, "_last", None))
            keep_self = True
        w = {}
        for s, v in (wait or {}).items():
            if not v:
                continue
            assert v <= self.cnt[s], f"wait {s}>={v} but only {self.cnt[s]} emitted"
            if (not keep_self) and s in COMPUTE_SEMS and self.eng_of_sem[s] == eng:
                continue
            if self.lastwait[eng].get(s, -1) < v:
                w[s] = v
                self.lastwait[eng][s] = v
        if inc:
            self.cnt[inc] += inc_amt
        self.streams[eng].append((w, fn, inc, inc_amt))
        if Prog.SERIAL and inc is not None:
            self._last = {inc: self.cnt[inc]}
        return self.cnt[inc] if inc else None

    def dma(self, eng, out, in_, wait=None, sem=None):
        assert sem is not None, "dma needs an explicit stream semaphore"
        self.sem(sem)
        return self.add(eng, lambda e, o=out, i=in_: e.dma_start(out=o, in_=i),
                        wait=wait, inc=sem, inc_amt=16)

    def fence(self):
        cur = dict(self.cnt)
        for eng in self.ENGS:
            self.add(eng, lambda e: None, wait=dict(cur))

    def flush(self, blk):
        for eng in self.ENGS:
            ops = self.streams[eng]
            self.streams[eng] = []
            if not ops:
                continue
            sems = self.sems

            def mk(ops_):
                def run(e):
                    for w, fn, inc, amt in ops_:
                        for s, v in w.items():
                            e.wait_ge(sems[s], v)
                        ins = fn(e)
                        if inc is not None:
                            ins.then_inc(sems[inc], amt)
                return run
            getattr(blk, eng)(mk(ops))


def build(debug=()):
    nc = bass.Bass("TRN2", target_bir_lowering=False)

    d_x = nc.dram_tensor("x", [SEQ, C], F32, kind="ExternalInput").ap()
    d_wqkv = nc.dram_tensor("wqkv", [C, 3 * C], F32, kind="ExternalInput").ap()
    d_wf = nc.dram_tensor("wf", [DH, DH], F32, kind="ExternalInput").ap()
    d_bf = nc.dram_tensor("bfv", [1, DH], F32, kind="ExternalInput").ap()
    d_wout = nc.dram_tensor("wout", [C, C], F32, kind="ExternalInput").ap()
    d_bout = nc.dram_tensor("bout", [1, C], F32, kind="ExternalInput").ap()
    d_clipx = nc.dram_tensor("clipx", [128, 1], F32, kind="ExternalInput").ap()
    d_clipd = nc.dram_tensor("clipd", [128, 1], F32, kind="ExternalInput").ap()
    d_wg = nc.dram_tensor("wg", [NI, C], F32, kind="ExternalInput").ap()
    d_wu = nc.dram_tensor("wu", [NI, C], F32, kind="ExternalInput").ap()
    d_wd = nc.dram_tensor("wd", [C, NI], F32, kind="ExternalInput").ap()
    d_ident = nc.dram_tensor("ident", [128, 128], BF16, kind="ExternalInput").ap()
    d_ones_col = nc.dram_tensor("ones_col", [128, 1], BF16, kind="ExternalInput").ap()
    d_ones_row = nc.dram_tensor("ones_row", [1, 1024], BF16, kind="ExternalInput").ap()
    d_ones_f32 = nc.dram_tensor("ones_f32", [1, 128], F32, kind="ExternalInput").ap()
    d_eps_row = nc.dram_tensor("eps_row", [1, 16], BF16, kind="ExternalInput").ap()
    d_ind = nc.dram_tensor("ind", [8, 16, 128], BF16, kind="ExternalInput").ap()
    d_constf = nc.dram_tensor("constf", [128, 2], F32, kind="ExternalInput").ap()

    d_out = nc.dram_tensor("out", [SEQ, C], F32, kind="ExternalOutput").ap()

    d_x1 = nc.dram_tensor("x1_scr", [SEQ, C], F32, kind="Internal").ap()
    d_h = nc.dram_tensor("h_scr", [NIC, 128, SEQ], F32, kind="Internal").ap()
    d_wqT = nc.dram_tensor("wqT_scr", [2, NIC, 128, CCH * 128], BF16, kind="Internal").ap()
    d_wdT = nc.dram_tensor("wdT_scr", [NIC, CCH, 128, 128], BF16, kind="Internal").ap()
    d_cc1_in = nc.dram_tensor("cc1_in", [128, 1], F32, kind="Internal").ap()
    d_cc1_out = nc.dram_tensor("cc1_out", [128, 1], F32, kind="Internal", addr_space="Shared").ap()
    d_cc2_in = nc.dram_tensor("cc2_in", [128, 1], F32, kind="Internal").ap()
    d_cc2_out = nc.dram_tensor("cc2_out", [128, 1], F32, kind="Internal", addr_space="Shared").ap()
    d_s128a = nc.dram_tensor("s128a", [128, 1], F32, kind="Internal").ap()
    d_s128b = nc.dram_tensor("s128b", [128, 1], F32, kind="Internal").ap()

    dbg = {}
    for name in debug:
        shp = {"x1": ([SEQ, C], F32), "q_fm": ([128, CCH * SEQ], BF16),
               "kvx": ([128, 8 * 128], BF16), "ks": ([128, 8 * 16], BF16),
               "xq": ([128, CCH * SEQ], BF16), "h0": ([128, SEQ], F32),
               "s1": ([128, 1], F32), "s2": ([128, 1], F32),
               "wall": ([128, CCH * 3 * C], BF16), "xnfm": ([128, CCH * 128], BF16),
               "wdts": ([128, NIC * C], BF16), "hqs": ([128, NIC * 512], BF16),
               "hmax": ([128, 1], F32), "hall": ([NIC * 128, SEQ], F32),
               "pdu": ([128, 3 * 512], F32),
               "ktok": ([128, C], BF16), "vtok": ([128, C], BF16),
               "qtok": ([128, C], BF16), "stat0": ([128, 16], F32)}[name]
        dt_ = BF16 if name in ("q_fm", "kvx", "ks", "xq", "wall", "xnfm", "ktok",
                               "vtok", "qtok", "wdts", "hqs") else F32
        dbg[name] = nc.dram_tensor("dbg_" + name, shp[0], dt_, kind="ExternalOutput").ap()

    es_all = ExitStack()
    with es_all:
        es = es_all
        ident = es.enter_context(nc.sbuf_tensor("identb", [128, 128], BF16))
        ones_col = es.enter_context(nc.sbuf_tensor("onescol", [128, 1], BF16))
        ones_row = es.enter_context(nc.sbuf_tensor("onesrow", [1, 1024], BF16))
        ones_f32 = es.enter_context(nc.sbuf_tensor("onesf32", [1, 128], F32))
        eps_row = es.enter_context(nc.sbuf_tensor("epsrow", [1, 16], BF16))
        ind_sb = es.enter_context(nc.sbuf_tensor("indsb", [16, 8 * 128], BF16))
        bias_row = es.enter_context(nc.sbuf_tensor("biasrow", [1, 3 * C], BF16))
        bf_bf = es.enter_context(nc.sbuf_tensor("bfbf", [1, DH], BF16))
        bout_row = es.enter_context(nc.sbuf_tensor("boutrow", [1, C], BF16))
        clipx_sb = es.enter_context(nc.sbuf_tensor("clipxsb", [128, 1], F32))
        clipd_sb = es.enter_context(nc.sbuf_tensor("clipdsb", [128, 1], F32))
        r2_all = es.enter_context(nc.sbuf_tensor("r2all", [128, NT], F32))
        nmr2_all = es.enter_context(nc.sbuf_tensor("nmr2all", [128, NT], F32))
        sg_all = es.enter_context(nc.sbuf_tensor("sgall", [128, NIC], F32))
        su_all = es.enter_context(nc.sbuf_tensor("suall", [128, NIC], F32))
        s1_col = es.enter_context(nc.sbuf_tensor("s1col", [128, 1], F32))
        rs1_col = es.enter_context(nc.sbuf_tensor("rs1col", [128, 1], F32))
        s2_col = es.enter_context(nc.sbuf_tensor("s2col", [128, 1], F32))
        rs2_col = es.enter_context(nc.sbuf_tensor("rs2col", [128, 1], F32))
        uscale_all = es.enter_context(nc.sbuf_tensor("uscaleall", [128, NT], F32))
        ubias_all = es.enter_context(nc.sbuf_tensor("ubiasall", [128, NT], F32))
        s1sg_all = es.enter_context(nc.sbuf_tensor("s1sgall", [128, NIC], F32))
        s1su_all = es.enter_context(nc.sbuf_tensor("s1suall", [128, NIC], F32))
        grow = es.enter_context(nc.sbuf_tensor("growbuf", [1, 128], F32))
        gcol = es.enter_context(nc.sbuf_tensor("gcolbuf", [1, 1], F32))
        gbc = es.enter_context(nc.sbuf_tensor("gbcbuf", [128, 1], F32))
        sig_col = es.enter_context(nc.sbuf_tensor("sigcol", [128, 1], F32))
        kvx_sb = es.enter_context(nc.sbuf_tensor("kvxsb", [128, 8 * 128], BF16))
        ks_sb = es.enter_context(nc.sbuf_tensor("kssb", [128, 8 * 16], BF16))
        xb = [es.enter_context(nc.sbuf_tensor(f"xb{i}", [128, C], F32)) for i in range(2)]
        x1b = [es.enter_context(nc.sbuf_tensor(f"x1b{i}", [128, C], F32)) for i in range(2)]
        sq = es.enter_context(nc.sbuf_tensor("sqscr", [128, C], F32))
        stat = es.enter_context(nc.sbuf_tensor("statscr", [128, 16], F32))
        constf = es.enter_context(nc.sbuf_tensor("constf_sb", [128, 2], F32))
        esBig = ExitStack()
        big = esBig.enter_context(nc.sbuf_tensor("big", [128, CCH * SEQ], BF16))

        blk = es.enter_context(nc.Block())
        p = Prog(nc, es)
        bigv = big[:].rearrange("p (c n) -> p c n", n=SEQ)

        xb_free = [{}, {}]
        x1b_free = [{}, {}]

        for o, i in ((ident[:], d_ident), (ones_col[:], d_ones_col),
                     (ones_row[:], d_ones_row), (ones_f32[:], d_ones_f32),
                     (eps_row[:], d_eps_row), (clipx_sb[:], d_clipx),
                     (clipd_sb[:], d_clipd), (constf[:], d_constf)):
            p.dma("sync", o, i, sem="sCONST")
        for g in range(8):
            p.dma("sync", ind_sb[:, g * 128:(g + 1) * 128], d_ind[g], sem="sCONST")
        sd_consts = p.cnt["sCONST"]
        p.dma("gpsimd", bf_bf[:], d_bf, sem="sCONST2")
        p.dma("gpsimd", bout_row[:], d_bout, sem="sCONST2")
        gd_consts = p.cnt["sCONST2"]

        p.add("vector", lambda e: e.memset(bias_row[:], 0.0), inc="VE")
        for j in range(16):
            p.add("vector", lambda e, j=j: e.tensor_copy(
                bias_row[:, j * 64:(j + 1) * 64], bf_bf[:]),
                wait={"sCONST2": gd_consts}, inc="VE")
            p.add("vector", lambda e, j=j: e.tensor_copy(
                bias_row[:, 2048 + j * 64: 2048 + (j + 1) * 64], bf_bf[:]), inc="VE")
        p.add("vector", lambda e: e.memset(stat[:, 10:14], 0.0), inc="VE")
        ve_consts = p.cnt["VE"]

        esA = ExitStack()
        if True:
            Wall = esA.enter_context(nc.sbuf_tensor("Wall", [128, CCH * 3 * C], BF16))
            wf_sb = esA.enter_context(nc.sbuf_tensor("wf_sb", [DH, DH], BF16))
            wTb = esA.enter_context(nc.sbuf_tensor("wTbuf", [64, 1024], BF16))
            xnfm0 = esA.enter_context(nc.sbuf_tensor("xnfm0", [128, CCH * 128], BF16))
            xnfm1 = esA.enter_context(nc.sbuf_tensor("xnfm1", [128, CCH * 128], BF16))
            xntok = esA.enter_context(nc.sbuf_tensor("xntok", [128, C], BF16))
            ktok0 = esA.enter_context(nc.sbuf_tensor("ktok0", [128, C], BF16))
            ktok1 = esA.enter_context(nc.sbuf_tensor("ktok1", [128, C], BF16))
            vtok0 = esA.enter_context(nc.sbuf_tensor("vtok0", [128, C], BF16))
            vtok1 = esA.enter_context(nc.sbuf_tensor("vtok1", [128, C], BF16))
            qtok0 = esA.enter_context(nc.sbuf_tensor("qtok0", [128, C], BF16))
            qtok1 = esA.enter_context(nc.sbuf_tensor("qtok1", [128, C], BF16))
            ptr6 = esA.enter_context(nc.psum_tensor("ptr6", [128, 1024], BF16))
            ptr7 = esA.enter_context(nc.psum_tensor("ptr7", [128, 1024], BF16))
            pkv = esA.enter_context(nc.psum_tensor("pkv", [128, 512], F32))
            pks = esA.enter_context(nc.psum_tensor("pks", [128, 512], F32))
            pa0 = esA.enter_context(nc.psum_tensor("pa0", [128, 512], F32))
            pa1 = esA.enter_context(nc.psum_tensor("pa1", [128, 512], F32))
            pa2 = esA.enter_context(nc.psum_tensor("pa2", [128, 512], F32))
            pa3 = esA.enter_context(nc.psum_tensor("pa3", [128, 512], F32))
            Wall_v = Wall[:].rearrange("p (c n) -> p c n", n=3 * C)
            pabank = [pa0, pa1, pa2, pa3]

            p.dma("gpsimd", wf_sb[:], d_wf, sem="sWF")
            gd_wf = p.cnt["sWF"]
            for cc in range(CCH):
                p.dma("gpsimd", Wall_v[:, cc, 1024:2048],
                      d_wqkv[cc * 128:(cc + 1) * 128, 2048:3072], sem="sWALL")
            gd_v = p.cnt["sWALL"]

            te_fold_tr_last = 0
            ve_fold = 0
            esW = ExitStack()
            if True:
                wstage = esW.enter_context(nc.sbuf_tensor("wstage", [128, CCH * 2048], BF16))
                wstage_v = wstage[:].rearrange("p (c n) -> p c n", n=2048)
                for cc in range(CCH):
                    p.dma("gpsimd", wstage_v[:, cc, 0:1024],
                          d_wqkv[cc * 128:(cc + 1) * 128, 0:1024], sem="sSTG")
                    p.dma("gpsimd", wstage_v[:, cc, 1024:2048],
                          d_wqkv[cc * 128:(cc + 1) * 128, 1024:2048], sem="sSTG")
                gd_stage = p.cnt["sSTG"]

                tr6_ve = 0
                mm_ve = [0, 0]
                wT_te = 0
                gidx = 0
                for src_off, dst_off in ((1024, 0), (0, 2048)):
                    for hh in range(H):
                        bk = gidx % 2
                        gidx += 1
                        te_tr = None
                        for cc in range(CCH):
                            te_tr = p.add("tensor", lambda e, cc=cc, so=src_off, hh=hh:
                                          e.transpose(ptr6[0:64, cc * 128:(cc + 1) * 128],
                                                      wstage_v[:, cc, so + hh * 64: so + (hh + 1) * 64],
                                                      ident[:]),
                                          wait={"sSTG": gd_stage, "sCONST": sd_consts,
                                                "VE": tr6_ve} if cc == 0 else None,
                                          inc="TE")
                        te_fold_tr_last = te_tr
                        v_wt = p.add("vector", lambda e: e.tensor_copy(
                            wTb[:], ptr6[0:64, :]),
                            wait={"TE": max(te_tr, wT_te)}, inc="VE")
                        tr6_ve = v_wt
                        te_mm = None
                        for cc in range(CCH):
                            te_mm = p.add("tensor", lambda e, cc=cc, bk=bk:
                                          e.matmul(pabank[bk][:, cc * 64:(cc + 1) * 64],
                                                   wTb[:, cc * 128:(cc + 1) * 128],
                                                   wf_sb[:], start=True, stop=True),
                                          wait={"VE": max(v_wt, mm_ve[bk]),
                                                "sWF": gd_wf} if cc == 0 else None,
                                          inc="TE")
                        wT_te = te_mm
                        mm_ve[bk] = p.add(
                            "vector",
                            lambda e, bk=bk, do=dst_off, hh=hh:
                            e.tensor_copy(
                                Wall_v[:, :, do + hh * 64: do + (hh + 1) * 64],
                                pabank[bk][:, 0:512].rearrange("p (c n) -> p c n", n=64)),
                            wait={"TE": te_mm}, inc="VE")
                ve_fold = p.cnt["VE"]
                if "wall" in dbg:
                    p.dma("sync", dbg["wall"], Wall[:], wait={"VE": ve_fold, "sWALL": gd_v}, sem="sDBG")
            esW.close()

            if True:
                wfbA = esA.enter_context(nc.sbuf_tensor("wfbA", [128, C], F32))
                wfbB = esA.enter_context(nc.sbuf_tensor("wfbB", [128, C], F32))
                wub = esA.enter_context(nc.sbuf_tensor("wub", [128, C], F32))
                wt1b = esA.enter_context(nc.sbuf_tensor("wt1b", [128, C], F32))
                whatbA = esA.enter_context(nc.sbuf_tensor("whatbA", [128, C], BF16))
                whatbB = esA.enter_context(nc.sbuf_tensor("whatbB", [128, C], BF16))
                whatbC = esA.enter_context(nc.sbuf_tensor("whatbC", [128, C], BF16))
                whatbD = esA.enter_context(nc.sbuf_tensor("whatbD", [128, C], BF16))
                wstgA = esA.enter_context(nc.sbuf_tensor("wstgA", [128, C], BF16))
                wstgB = esA.enter_context(nc.sbuf_tensor("wstgB", [128, C], BF16))
                wstat = esA.enter_context(nc.sbuf_tensor("wstat", [128, 8], F32))
                chunks = [("g", i) for i in range(NIC)] + [("u", i) for i in range(NIC)]
                wfb = [wfbA, wfbB]
                whatb = [whatbA, whatbB, whatbC, whatbD]
                wstg = [wstgA, wstgB]
                cp = {"s1": 0, "s2": 0,
                      "wfb_free": [{"TE": te_fold_tr_last}, {"TE": te_fold_tr_last}],
                      "what_free": [{}, {}, {}, {}], "wstg_free": [{}, {}],
                      "wstat_free": {}, "wub_free": {}, "what_v6": {}, "scales": {}}

                def prep_stage1():
                    if cp["s1"] >= len(chunks):
                        return
                    k = cp["s1"]
                    cp["s1"] += 1
                    kind, idx = chunks[k]
                    kb = k % 2
                    kb4 = k % 4
                    src_t = d_wg if kind == "g" else d_wu
                    gdv = p.dma("gpsimd", wfb[kb][:], src_t[idx * 128:(idx + 1) * 128, :],
                                wait=cp["wfb_free"][kb], sem=f"sWFB{kb}")
                    v1 = p.add("vector", lambda e, kb=kb: e.reduce_max(
                        wstat[:, 0:1], wfb[kb][:], AX.X, apply_absolute_value=True),
                        wait=merge({f"sWFB{kb}": gdv}, cp["wstat_free"]), inc="VE")
                    v2 = p.add("vector", lambda e: e.tensor_scalar(
                        wstat[:, 1:2], wstat[:, 0:1], 1.0 / 127.0, None, ALU.mult),
                        wait={"VE": v1}, keep_self=True, inc="VE")
                    dst = sg_all if kind == "g" else su_all
                    p.add("vector", lambda e, dst=dst, idx=idx: e.tensor_copy(
                        dst[:, idx:idx + 1], wstat[:, 1:2]),
                        wait={"VE": v2}, keep_self=True, inc="VE")
                    v3 = p.add("vector", lambda e: e.reciprocal(
                        wstat[:, 2:3], wstat[:, 1:2]),
                        wait={"VE": v2}, keep_self=True, inc="VE")
                    s1 = p.add("vector", lambda e, kb=kb: e.tensor_scalar(
                        wub[:], wfb[kb][:], wstat[:, 2:3], MAGIC, ALU.mult, ALU.add),
                        wait=merge({"VE": v3}, cp["wub_free"]), keep_self=True, inc="VE")
                    cp["wfb_free"][kb] = {"VE": s1}
                    cp["wstat_free"] = {"VE": s1}
                    v4 = p.add("vector", lambda e: e.tensor_scalar(
                        wt1b[:], wub[:], MAGIC, 127.0, ALU.subtract, ALU.min),
                        inc="VE")
                    cp["wub_free"] = {"VE": v4}
                    v6 = p.add("vector", lambda e, kb4=kb4: e.tensor_scalar(
                        whatb[kb4][:], wt1b[:], -128.0, None, ALU.max),
                        wait=cp["what_free"][kb4], inc="VE")
                    cp["what_v6"][k] = v6

                def prep_stage2(tr_psum, tr_free):
                    if cp["s2"] >= cp["s1"]:
                        return
                    k = cp["s2"]
                    cp["s2"] += 1
                    kind, idx = chunks[k]
                    kb = k % 2
                    kb4 = k % 4
                    v6 = cp["what_v6"].pop(k)
                    te_last = None
                    for j in range(8):
                        te_last = p.add("tensor", lambda e, j=j, kb4=kb4:
                                        e.transpose(tr_psum[:, j * 128:(j + 1) * 128],
                                                    whatb[kb4][:, j * 128:(j + 1) * 128],
                                                    ident[:]),
                                        wait=merge({"VE": v6}, tr_free[0])
                                        if j == 0 else None, inc="TE")
                    cp["what_free"][kb4] = {"TE": te_last}
                    se_ev = p.add("scalar", lambda e, kb=kb: e.activation(
                        wstg[kb][:], tr_psum[:, 0:1024], AF.Identity),
                        wait=merge({"TE": te_last}, cp["wstg_free"][kb]), inc="SE")
                    tr_free[0] = {"SE": se_ev}
                    mi = 0 if kind == "g" else 1
                    gdo = p.dma("gpsimd", d_wqT[mi, idx], wstg[kb][:],
                                wait={"SE": se_ev}, sem=f"sWSG{kb}")
                    cp["wstg_free"][kb] = {f"sWSG{kb}": gdo}

                v_pkv0 = p.add("vector", lambda e: e.memset(pkv[:], 0.0), inc="VE")
                v_pks0 = p.add("vector", lambda e: e.memset(pks[:], 0.0), inc="VE")
                xnfm = [xnfm0, xnfm1]
                xnfm_v = [xnfm0[:].rearrange("p (c n) -> p c n", n=128),
                          xnfm1[:].rearrange("p (c n) -> p c n", n=128)]
                ktok = [ktok0, ktok1]
                vtok = [vtok0, vtok1]
                qtok = [qtok0, qtok1]
                xnfm_free = [0, 0]
                ktok_free = [0, 0]
                vtok_free = [0, 0]
                qtok_free = [0, 0]
                bank_free = [0, 0, 0, 0]
                tr6_free = [ve_fold]
                tr7_free = [{}]
                xntok_free = [0]
                prev_v_var = 0
                kv_te_last = 0
                sd_x1_pb = [0] * NT

                def pa_front(tt):
                    bi = tt % 2
                    sdx = p.dma("sync", xb[bi][:], d_x[tt * 128:(tt + 1) * 128, :],
                                wait=xb_free[bi], sem=f"sXB{bi}")
                    s_sum = p.add("scalar", lambda e, bi=bi: e.activation(
                        sq[:], xb[bi][:], AF.Identity, accum_out=stat[:, 0:1]),
                        wait={f"sXB{bi}": sdx, "SE": pa_state["prev_s_var"]}, inc="SE")
                    s_sq = p.add("scalar", lambda e, bi=bi: e.activation(
                        sq[:], xb[bi][:], AF.Square, accum_out=stat[:, 1:2]),
                        inc="SE")
                    s_mean = p.add("scalar", lambda e: e.activation(
                        stat[:, 2:3], stat[:, 0:1], AF.Identity, scale=1.0 / C),
                        wait={"SE": s_sq}, keep_self=True, inc="SE")
                    s_m2 = p.add("scalar", lambda e: e.activation(
                        stat[:, 3:4], stat[:, 2:3], AF.Square),
                        wait={"SE": s_mean}, keep_self=True, inc="SE")
                    s_nm2 = p.add("scalar", lambda e: e.activation(
                        stat[:, 4:5], stat[:, 3:4], AF.Identity, scale=-1.0),
                        wait={"SE": s_m2}, keep_self=True, inc="SE")
                    s_var = p.add("scalar", lambda e: e.activation(
                        stat[:, 5:6], stat[:, 1:2], AF.Identity, scale=1.0 / C,
                        bias=stat[:, 4:5]),
                        wait={"SE": s_nm2}, keep_self=True, inc="SE")
                    s_sd = p.add("scalar", lambda e: e.activation(
                        stat[:, 6:7], stat[:, 5:6], AF.Sqrt, bias=constf[:, 0:1]),
                        wait={"SE": s_var}, keep_self=True, inc="SE")
                    pa_state["prev_s_var"] = s_sd
                    v_r = p.add("vector", lambda e: e.reciprocal(
                        stat[:, 7:8], stat[:, 6:7]), wait={"SE": s_sd}, inc="VE")
                    s_mr = p.add("scalar", lambda e: e.activation(
                        stat[:, 8:9], stat[:, 2:3], AF.Identity, scale=stat[:, 7:8]),
                        wait={"VE": v_r}, inc="SE")
                    s_xn = p.add("vector", lambda e, bi=bi: e.tensor_scalar(
                        xntok[:], xb[bi][:], stat[:, 7:8], stat[:, 8:9],
                        ALU.mult, ALU.subtract),
                        wait={"SE": s_mr, "TE": xntok_free[0]}, inc="VE")
                    xb_free[bi] = {"VE": s_xn}

                    te_tr = None
                    for j in range(8):
                        te_tr = p.add("tensor", lambda e, j=j: e.transpose(
                            ptr6[:, j * 128:(j + 1) * 128], xntok[:, j * 128:(j + 1) * 128],
                            ident[:]),
                            wait={"VE": max(s_xn, tr6_free[0])}
                            if j == 0 else None, inc="TE")
                    xntok_free[0] = te_tr
                    v_xnfm = p.add("vector", lambda e, bi=bi: e.tensor_copy(
                        xnfm[bi][:], ptr6[:, 0:1024]),
                        wait={"TE": max(te_tr, xnfm_free[bi])}, inc="VE")
                    tr6_free[0] = v_xnfm
                    pa_state["v_xnfm"][tt] = v_xnfm

                def pa_back(tt):
                    bi = tt % 2
                    v_xnfm = pa_state["v_xnfm"][tt]
                    grp_last_te = 0
                    for grp, bank, kind in ((0, 0, "k"), (1, 1, "k"), (2, 2, "v"),
                                            (3, 3, "v"), (4, 0, "q"), (5, 1, "q")):
                        te_g = None
                        for c in range(CCH):
                            last = (c == CCH - 1)
                            te_g = p.add(
                                "tensor",
                                lambda e, grp=grp, bank=bank, c=c, bi=bi, kind=kind:
                                e.matmul(pabank[bank][:, 0:512], xnfm_v[bi][:, c, :],
                                         Wall_v[:, c, grp * 512:(grp + 1) * 512],
                                         start=(c == 0),
                                         stop=(last and kind == "v")),
                                wait=merge({"VE": max(v_xnfm, ve_fold)},
                                           {"SE": bank_free[bank]},
                                           {"sWALL": gd_v}) if c == 0 else None,
                                inc="TE")
                        if kind != "v":
                            te_g = p.add("tensor", lambda e, grp=grp, bank=bank:
                                         e.matmul(pabank[bank][:, 0:512],
                                                  ones_row[0:1, 0:128],
                                                  bias_row[0:1, grp * 512:(grp + 1) * 512],
                                                  start=False, stop=True),
                                         wait={"VE": ve_consts}, inc="TE")
                        grp_last_te = te_g
                        if kind == "k":
                            se_e = p.add("scalar", lambda e, grp=grp, bank=bank, bi=bi:
                                         e.activation(ktok[bi][:, grp * 512:(grp + 1) * 512],
                                                      pabank[bank][:, 0:512], AF.Relu),
                                         wait={"TE": max(te_g, ktok_free[bi])}, inc="SE")
                        elif kind == "v":
                            se_e = p.add("scalar", lambda e, grp=grp, bank=bank, bi=bi:
                                         e.activation(vtok[bi][:, (grp - 2) * 512:(grp - 1) * 512],
                                                      pabank[bank][:, 0:512], AF.Identity),
                                         wait={"TE": max(te_g, vtok_free[bi])}, inc="SE")
                        else:
                            se_e = p.add("scalar", lambda e, grp=grp, bank=bank, bi=bi:
                                         e.activation(qtok[bi][:, (grp - 4) * 512:(grp - 3) * 512],
                                                      pabank[bank][:, 0:512], AF.Relu),
                                         wait={"TE": max(te_g, qtok_free[bi])}, inc="SE")
                        bank_free[bank] = se_e
                    se_kv_ready = max(bank_free)

                    te_kv = None
                    for hh in range(H):
                        pr = hh // 2
                        po = (hh % 2) * 64
                        te_kv = p.add("tensor", lambda e, hh=hh, pr=pr, po=po, bi=bi, tt=tt:
                                      e.matmul(pkv[po:po + 64, pr * 64:(pr + 1) * 64],
                                               ktok[bi][:, hh * 64:(hh + 1) * 64],
                                               vtok[bi][:, hh * 64:(hh + 1) * 64],
                                               start=False,
                                               stop=(tt == NT - 1 and hh == H - 1),
                                               skip_group_check=True),
                                      wait={"SE": se_kv_ready, "VE": v_pkv0}
                                      if hh == 0 else None, inc="TE")
                        te_kv = p.add("tensor", lambda e, hh=hh, po=po, bi=bi, tt=tt:
                                      e.matmul(pks[po:po + 64, hh:hh + 1],
                                               ktok[bi][:, hh * 64:(hh + 1) * 64],
                                               ones_col[:],
                                               start=False,
                                               stop=(tt == NT - 1 and hh == H - 1),
                                               skip_group_check=True),
                                      inc="TE")
                    ktok_free[bi] = te_kv
                    vtok_free[bi] = te_kv
                    pa_state["kv_te_last"] = te_kv

                    te_qt = None
                    for j in range(8):
                        te_qt = p.add("tensor", lambda e, j=j, bi=bi: e.transpose(
                            ptr7[:, j * 128:(j + 1) * 128], qtok[bi][:, j * 128:(j + 1) * 128],
                            ident[:]),
                            wait=merge({"SE": bank_free[1]}, tr7_free[0])
                            if j == 0 else None, inc="TE")
                    qtok_free[bi] = te_qt
                    v_qfm = p.add("vector", lambda e, tt=tt: e.tensor_copy(
                        bigv[:, :, tt * 128:(tt + 1) * 128], ptr7[:, 0:1024]),
                        wait={"TE": te_qt}, inc="VE")
                    tr7_free[0] = {"VE": v_qfm}
                    xnfm_free[bi] = grp_last_te

                    if tt == 0:
                        if "xnfm" in dbg:
                            p.dma("sync", dbg["xnfm"], xnfm0[:], wait={"VE": v_xnfm}, sem="sDBG")
                        if "stat0" in dbg:
                            p.dma("sync", dbg["stat0"], stat[:], wait={"VE": v_xnfm}, sem="sDBG")
                        if "ktok" in dbg:
                            p.dma("sync", dbg["ktok"], ktok0[:], wait={"SE": bank_free[1]}, sem="sDBG")
                        if "vtok" in dbg:
                            p.dma("sync", dbg["vtok"], vtok0[:], wait={"SE": bank_free[3]}, sem="sDBG")
                        if "qtok" in dbg:
                            p.dma("sync", dbg["qtok"], qtok0[:], wait={"SE": bank_free[1]}, sem="sDBG")

                pa_state = {"prev_s_var": 0, "v_xnfm": {}, "kv_te_last": 0}
                pa_front(0)
                for tt in range(NT):
                    if tt + 1 < NT:
                        pa_front(tt + 1)
                    want1 = min(len(chunks), 2 * (tt + 1))
                    while cp["s1"] < want1:
                        prep_stage1()
                    want2 = min(cp["s1"], max(0, 2 * tt))
                    while cp["s2"] < want2:
                        prep_stage2(ptr7, tr7_free)
                    pa_back(tt)
                kv_te_last = pa_state["kv_te_last"]
                while cp["s2"] < len(chunks):
                    prep_stage2(ptr7, tr7_free)
                ve_prep_pa = p.cnt["VE"]
                gd_prep_pa = (p.cnt.get("sWSG0", 0), p.cnt.get("sWSG1", 0))
                se_prep_pa = p.cnt["SE"]

                p.add("vector", lambda e: e.memset(kvx_sb[:], 0.0),
                      wait={"TE": kv_te_last}, inc="VE")
                p.add("vector", lambda e: e.memset(ks_sb[:], 0.0), inc="VE")
                for g in range(8):
                    p.add("vector", lambda e, g=g: e.tensor_copy(
                        kvx_sb[0:64, g * 128: g * 128 + 64],
                        pkv[0:64, g * 64:(g + 1) * 64]), inc="VE")
                    p.add("vector", lambda e, g=g: e.tensor_copy(
                        kvx_sb[64:128, g * 128 + 64: g * 128 + 128],
                        pkv[64:128, g * 64:(g + 1) * 64]), inc="VE")
                    p.add("vector", lambda e, g=g: e.tensor_copy(
                        ks_sb[0:64, g * 16 + 2 * g: g * 16 + 2 * g + 1],
                        pks[0:64, 2 * g:2 * g + 1]), inc="VE")
                    p.add("vector", lambda e, g=g: e.tensor_copy(
                        ks_sb[64:128, g * 16 + 2 * g + 1: g * 16 + 2 * g + 2],
                        pks[64:128, 2 * g + 1:2 * g + 2]), inc="VE")
                ve_kvx = p.cnt["VE"]
        esA.close()

        p.fence()

        esB = ExitStack()
        if True:
            WoutB = esB.enter_context(nc.sbuf_tensor("WoutB", [128, CCH * C], BF16))
            Awin0 = esB.enter_context(nc.sbuf_tensor("Awin0", [128, CCH * 512], BF16))
            Awin1 = esB.enter_context(nc.sbuf_tensor("Awin1", [128, CCH * 512], BF16))
            rdb0 = esB.enter_context(nc.sbuf_tensor("rdb0", [16, 512], BF16))
            rdb1 = esB.enter_context(nc.sbuf_tensor("rdb1", [16, 512], BF16))
            bcb0 = esB.enter_context(nc.sbuf_tensor("bcb0", [128, 512], F32))
            bcb1 = esB.enter_context(nc.sbuf_tensor("bcb1", [128, 512], F32))
            pnrm = esB.enter_context(nc.psum_tensor("pnrm", [128, 512], F32))
            pbc0 = esB.enter_context(nc.psum_tensor("pbc0", [128, 512], F32))
            pat0 = esB.enter_context(nc.psum_tensor("pat0", [128, 512], F32))
            pbc1 = esB.enter_context(nc.psum_tensor("pbc1", [128, 512], F32))
            pat1 = esB.enter_context(nc.psum_tensor("pat1", [128, 512], F32))
            po0 = esB.enter_context(nc.psum_tensor("po0", [128, 512], F32))
            po1 = esB.enter_context(nc.psum_tensor("po1", [128, 512], F32))
            ptrB = esB.enter_context(nc.psum_tensor("ptrB", [128, 1024], BF16))
            wfb_d = esB.enter_context(nc.sbuf_tensor("wfb_d", [128, NI], F32))
            wub_d = esB.enter_context(nc.sbuf_tensor("wub_d", [128, NI // 2], F32))
            wt1_d = esB.enter_context(nc.sbuf_tensor("wt1_d", [128, NI // 2], F32))
            what_d0 = esB.enter_context(nc.sbuf_tensor("what_d0", [128, NI], BF16))
            what_d1 = esB.enter_context(nc.sbuf_tensor("what_d1", [128, NI], BF16))
            wstg_d0 = esB.enter_context(nc.sbuf_tensor("wstg_d0", [128, NI], BF16))
            wstg_d1 = esB.enter_context(nc.sbuf_tensor("wstg_d1", [128, NI], BF16))
            wstat_d = esB.enter_context(nc.sbuf_tensor("wstat_d", [128, 4], F32))
            Wout_v = WoutB[:].rearrange("p (c n) -> p c n", n=C)
            for cc in range(CCH):
                p.dma("gpsimd", Wout_v[:, cc, :], d_wout[cc * 128:(cc + 1) * 128, :],
                      sem="sWOUT")
            gd_wout = p.cnt["sWOUT"]

            Awin = [Awin0, Awin1]
            Awin_v = [Awin0[:].rearrange("p (c n) -> p c n", n=512),
                      Awin1[:].rearrange("p (c n) -> p c n", n=512)]
            rd = [rdb0, rdb1]
            bc = [bcb0, bcb1]
            pbc = [pbc0, pbc1]
            pat = [pat0, pat1]
            Awin_free = [0, 0]
            rd_free = [0, 0]
            bc_free = [0, 0]
            pat_free = [0, 0]
            pbc_free = [0, 0]
            po_free = [0, 0]

            pb_state = {"ve_awin": {}, "prev_v_am": 0}

            def pb_front(ns):
                wi = ns % 2
                te_n = None
                for g in range(8):
                    te_n = p.add("tensor", lambda e, g=g, ns=ns: e.matmul(
                        pnrm[0:16, 0:512], ks_sb[:, g * 16:(g + 1) * 16],
                        bigv[:, g, ns * 512:(ns + 1) * 512],
                        start=(g == 0), stop=False),
                        wait={"VE": max(ve_kvx, pb_state.get("pnrm_free", 0))}
                        if g == 0 else None, inc="TE")
                te_n = p.add("tensor", lambda e: e.matmul(
                    pnrm[0:16, 0:512], eps_row[:], ones_row[0:1, 0:512],
                    start=False, stop=True), inc="TE")
                v_rd = p.add("vector", lambda e, wi=wi: e.reciprocal(
                    rd[wi][:], pnrm[0:16, 0:512]),
                    wait={"TE": max(te_n, rd_free[wi])}, inc="VE")
                pb_state["pnrm_free"] = v_rd

                for g in range(8):
                    pi = g % 2
                    te_bc = p.add("tensor", lambda e, g=g, pi=pi, wi=wi: e.matmul(
                        pbc[pi][:, 0:512], ind_sb[:, g * 128:(g + 1) * 128], rd[wi][:],
                        start=True, stop=True),
                        wait={"VE": v_rd, "SE": pbc_free[pi]}, inc="TE")
                    rd_free[wi] = te_bc
                    te_at = p.add("tensor", lambda e, g=g, pi=pi, ns=ns: e.matmul(
                        pat[pi][:, 0:512], kvx_sb[:, g * 128:(g + 1) * 128],
                        bigv[:, g, ns * 512:(ns + 1) * 512],
                        start=True, stop=True),
                        wait={"VE": pat_free[pi]}, inc="TE")
                    se_bc = p.add("scalar", lambda e, pi=pi: e.activation(
                        bc[pi][:], pbc[pi][:, 0:512], AF.Identity),
                        wait={"TE": te_bc, "VE": bc_free[pi]}, inc="SE")
                    pbc_free[pi] = se_bc
                    v_a = p.add("vector", lambda e, g=g, pi=pi, wi=wi: e.tensor_tensor(
                        Awin_v[wi][:, g, :], pat[pi][:, 0:512], bc[pi][:], ALU.mult),
                        wait={"TE": max(te_at, Awin_free[wi]), "SE": se_bc}, inc="VE")
                    pat_free[pi] = v_a
                    bc_free[pi] = v_a
                pb_state["ve_awin"][ns] = p.cnt["VE"]

            def pb_back(ns):
                wi = ns % 2
                ve_awin = pb_state["ve_awin"][ns]
                for tl in range(4):
                    tt = ns * 4 + tl
                    bi = tt % 2
                    sdx = p.dma("sync", xb[bi][:], d_x[tt * 128:(tt + 1) * 128, :],
                                wait=xb_free[bi], sem=f"sXB{bi}")
                    te_o = None
                    for c in range(CCH):
                        p.add("tensor", lambda e, c=c, tl=tl, wi=wi: e.matmul(
                            po0[:, 0:512], Awin_v[wi][:, c, tl * 128:(tl + 1) * 128],
                            Wout_v[:, c, 0:512], start=(c == 0), stop=False),
                            wait={"VE": max(ve_awin, po_free[0]), "sWOUT": gd_wout}
                            if c == 0 else None, inc="TE")
                        te_o = p.add("tensor", lambda e, c=c, tl=tl, wi=wi: e.matmul(
                            po1[:, 0:512], Awin_v[wi][:, c, tl * 128:(tl + 1) * 128],
                            Wout_v[:, c, 512:1024], start=(c == 0), stop=False),
                            wait={"VE": po_free[1]} if c == 0 else None, inc="TE")
                    p.add("tensor", lambda e: e.matmul(
                        po0[:, 0:512], ones_row[0:1, 0:128], bout_row[0:1, 0:512],
                        start=False, stop=True), wait={"sCONST2": gd_consts}, inc="TE")
                    te_o = p.add("tensor", lambda e: e.matmul(
                        po1[:, 0:512], ones_row[0:1, 0:128], bout_row[0:1, 512:1024],
                        start=False, stop=True), inc="TE")
                    if tl == 3:
                        Awin_free[wi] = te_o
                    v_x1a = p.add("vector", lambda e, bi=bi: e.scalar_tensor_tensor(
                        x1b[bi][:, 0:512], po0[:, 0:512], 1.0, xb[bi][:, 0:512],
                        ALU.mult, ALU.add),
                        wait=merge({"TE": te_o, f"sXB{bi}": sdx}, x1b_free[bi]), inc="VE")
                    v_x1b = p.add("vector", lambda e, bi=bi: e.scalar_tensor_tensor(
                        x1b[bi][:, 512:1024], po1[:, 0:512], 1.0, xb[bi][:, 512:1024],
                        ALU.mult, ALU.add), inc="VE")
                    po_free[0] = v_x1b
                    po_free[1] = v_x1b
                    xb_free[bi] = {"VE": v_x1b}
                    sd_x1 = p.dma("sync", d_x1[tt * 128:(tt + 1) * 128, :], x1b[bi][:],
                                  wait={"VE": v_x1b}, sem=f"sX1W{bi}")
                    sd_x1_pb[tt] = {f"sX1W{bi}": sd_x1}
                    x1b_free[bi] = {f"sX1W{bi}": sd_x1}
                    s_sum = p.add("scalar", lambda e, bi=bi: e.activation(
                        sq[:], x1b[bi][:], AF.Identity, accum_out=stat[:, 0:1]),
                        wait={"VE": max(v_x1b, pb_state["prev_v_am"])}, inc="SE")
                    s_sq2 = p.add("scalar", lambda e, bi=bi: e.activation(
                        sq[:], x1b[bi][:], AF.Square, accum_out=stat[:, 1:2]), inc="SE")
                    s_mean = p.add("scalar", lambda e: e.activation(
                        stat[:, 2:3], stat[:, 0:1], AF.Identity, scale=1.0 / C),
                        wait={"SE": s_sq2}, keep_self=True, inc="SE")
                    s_m2 = p.add("scalar", lambda e: e.activation(
                        stat[:, 3:4], stat[:, 2:3], AF.Square),
                        wait={"SE": s_mean}, keep_self=True, inc="SE")
                    s_nm2 = p.add("scalar", lambda e: e.activation(
                        stat[:, 4:5], stat[:, 3:4], AF.Identity, scale=-1.0),
                        wait={"SE": s_m2}, keep_self=True, inc="SE")
                    s_var = p.add("scalar", lambda e: e.activation(
                        stat[:, 5:6], stat[:, 1:2], AF.Identity, scale=1.0 / C,
                        bias=stat[:, 4:5]),
                        wait={"SE": s_nm2}, keep_self=True, inc="SE")
                    s_sd2 = p.add("scalar", lambda e: e.activation(
                        stat[:, 6:7], stat[:, 5:6], AF.Sqrt, bias=constf[:, 0:1]),
                        wait={"SE": s_var}, keep_self=True, inc="SE")
                    v_r2 = p.add("vector", lambda e, tt=tt: e.reciprocal(
                        r2_all[:, tt:tt + 1], stat[:, 6:7]), wait={"SE": s_sd2}, inc="VE")
                    v_nmr2 = p.add("vector", lambda e, tt=tt: e.tensor_scalar(
                        nmr2_all[:, tt:tt + 1], stat[:, 2:3], r2_all[:, tt:tt + 1], -1.0,
                        ALU.mult, ALU.mult),
                        wait={"VE": v_r2}, keep_self=True, inc="VE")
                    s_u = p.add("scalar", lambda e, bi=bi, tt=tt: e.activation(
                        sq[:], x1b[bi][:], AF.Identity, bias=nmr2_all[:, tt:tt + 1],
                        scale=r2_all[:, tt:tt + 1]),
                        wait={"VE": v_nmr2}, inc="SE")
                    v_am = p.add("vector", lambda e: e.reduce_max(
                        stat[:, 8:9], sq[:], AX.X, apply_absolute_value=True),
                        wait={"SE": s_u}, inc="VE")
                    amo, amn = 10 + (tt % 2), 10 + ((tt + 1) % 2)
                    pb_state["prev_v_am"] = p.add(
                        "vector", lambda e, amo=amo, amn=amn: e.tensor_scalar(
                            stat[:, amn:amn + 1], stat[:, 8:9], stat[:, amo:amo + 1], None,
                            ALU.max), wait={"VE": v_am}, keep_self=True, inc="VE")

            what_d = [what_d0, what_d1]
            wstg_d = [wstg_d0, wstg_d1]
            dp = {"s1": 0, "s2": 0, "wfb_free": {}, "wub_free": {}, "wstat_free": {},
                  "what_free": [{}, {}], "wstg_free": [{}, {}], "v6": {},
                  "trB_free": {}}

            def dprep_stage1():
                if dp["s1"] >= CCH:
                    return
                j = dp["s1"]
                dp["s1"] += 1
                kb = j % 2
                gdv = p.dma("gpsimd", wfb_d[:], d_wd[j * 128:(j + 1) * 128, :],
                            wait=dp["wfb_free"], sem="sDL")
                v1 = p.add("vector", lambda e: e.reduce_max(
                    wstat_d[:, 0:1], wfb_d[:], AX.X, apply_absolute_value=True),
                    wait=merge({"sDL": gdv}, dp["wstat_free"]), inc="VE")
                v2 = p.add("vector", lambda e: e.tensor_scalar(
                    wstat_d[:, 1:2], wstat_d[:, 0:1], 1.0 / 127.0, None, ALU.mult),
                    wait={"VE": v1}, keep_self=True, inc="VE")
                v3 = p.add("vector", lambda e: e.reciprocal(
                    wstat_d[:, 2:3], wstat_d[:, 1:2]),
                    wait={"VE": v2}, keep_self=True, inc="VE")
                s1v = None
                v6 = None
                for p0 in range(0, NI, NI // 2):
                    w = NI // 2
                    s1v = p.add("vector", lambda e, p0=p0, w=w: e.tensor_scalar(
                        wub_d[:, 0:w], wfb_d[:, p0:p0 + w], wstat_d[:, 2:3], MAGIC,
                        ALU.mult, ALU.add),
                        wait=merge({"VE": v3}, dp["wub_free"]), keep_self=True, inc="VE")
                    v4 = p.add("vector", lambda e, w=w: e.tensor_scalar(
                        wt1_d[:, 0:w], wub_d[:, 0:w], MAGIC, 127.0,
                        ALU.subtract, ALU.min), inc="VE")
                    p.add("vector", lambda e, w=w: e.tensor_scalar(
                        wub_d[:, 0:w], wt1_d[:, 0:w], -128.0, None, ALU.max), inc="VE")
                    v6 = p.add("vector", lambda e, kb=kb, p0=p0, w=w: e.tensor_scalar(
                        what_d[kb][:, p0:p0 + w], wub_d[:, 0:w], wstat_d[:, 1:2], None,
                        ALU.mult), wait=dp["what_free"][kb], inc="VE")
                    dp["wub_free"] = {"VE": v6}
                dp["wfb_free"] = {"VE": s1v}
                dp["wstat_free"] = {"VE": s1v}
                dp["v6"][j] = v6

            def dprep_stage2():
                if dp["s2"] >= dp["s1"]:
                    return
                j = dp["s2"]
                dp["s2"] += 1
                kb = j % 2
                v6 = dp["v6"].pop(j)
                te_last = None
                for r0 in range(0, NIC, 8):
                    rblks = min(8, NIC - r0)
                    for jj in range(rblks):
                        te_last = p.add("tensor", lambda e, r0=r0, jj=jj, kb=kb:
                                        e.transpose(ptrB[:, jj * 128:(jj + 1) * 128],
                                                    what_d[kb][:, (r0 + jj) * 128:(r0 + jj + 1) * 128],
                                                    ident[:]),
                                        wait=merge({"VE": v6}, dp["trB_free"])
                                        if jj == 0 else None, inc="TE")
                    se_ev = p.add("scalar", lambda e, r0=r0, rblks=rblks, kb=kb:
                                  e.activation(wstg_d[kb][:, r0 * 128:(r0 + rblks) * 128],
                                               ptrB[:, 0:rblks * 128], AF.Identity),
                                  wait=merge({"TE": te_last}, dp["wstg_free"][kb]),
                                  inc="SE")
                    dp["trB_free"] = {"SE": se_ev}
                dp["what_free"][kb] = {"TE": te_last}
                gdo = p.dma("gpsimd", d_wdT[:, j].transpose([1, 0, 2]),
                            wstg_d[kb][:].rearrange("p (i n) -> p i n", n=128),
                            wait=dp["trB_free"], sem=f"sDS{kb}")
                dp["wstg_free"][kb] = {f"sDS{kb}": gdo}

            pb_front(0)
            for ns in range(8):
                if ns + 1 < 8:
                    pb_front(ns + 1)
                dprep_stage1()
                pb_back(ns)
                dprep_stage2()
            while dp["s2"] < CCH:
                dprep_stage2()
            gd_prep_pb = (p.cnt.get("sDS0", 0), p.cnt.get("sDS1", 0))
            ve_pb = p.cnt["VE"]
            sd_pb = dict(p.cnt)

            if "x1" in dbg:
                for t2 in range(NT):
                    p.dma("sync", dbg["x1"][t2 * 128:(t2 + 1) * 128, :],
                          d_x1[t2 * 128:(t2 + 1) * 128, :],
                          wait={"sX1W0": p.cnt.get("sX1W0", 0),
                                "sX1W1": p.cnt.get("sX1W1", 0)}, sem="sDBG")
            if "q_fm" in dbg:
                p.dma("sync", dbg["q_fm"], big[:], wait={"VE": ve_pb}, sem="sDBG")
            if "kvx" in dbg:
                p.dma("sync", dbg["kvx"], kvx_sb[:], wait={"VE": ve_pb}, sem="sDBG")
            if "ks" in dbg:
                p.dma("sync", dbg["ks"], ks_sb[:], wait={"VE": ve_pb}, sem="sDBG")
        esB.close()

        def barrier(acc_ap, cc_in, cc_out, scr, s_col, rs_col, clip_sb, wait_d):
            sd1 = p.dma("sync", cc_in, acc_ap, wait=wait_d, sem="sBAR")
            ge = p.add("gpsimd", lambda e, ci=cc_in, co=cc_out: e.collective_compute(
                "AllReduce", ALU.max, replica_groups=[list(range(N_CORES))],
                ins=[ci], outs=[co]),
                wait={"sBAR": sd1}, inc="GE")
            sd2 = p.dma("sync", grow[:], cc_out.rearrange("a one -> one a"),
                        wait={"GE": ge}, sem="sBAR")
            v1 = p.add("vector", lambda e: e.reduce_max(gcol[:], grow[:], AX.X),
                       wait={"sBAR": sd2}, inc="VE")
            v2 = p.add("vector", lambda e: e.tensor_scalar(
                grow[:], ones_f32[:], gcol[0:1, 0:1], None, ALU.mult),
                wait={"VE": v1}, keep_self=True, inc="VE")
            sd3 = p.dma("sync", scr.rearrange("a one -> one a"), grow[:],
                        wait={"VE": v2}, sem="sBAR")
            sd4 = p.dma("sync", gbc[:], scr, wait={"sBAR": sd3}, sem="sBAR")
            s_sig = p.add("scalar", lambda e, cs=clip_sb: e.activation(
                sig_col[:], cs[:], AF.Sigmoid),
                wait={"sBAR": sd4, "sCONST": sd_consts}, inc="SE")
            v3 = p.add("vector", lambda e, s=s_col: e.scalar_tensor_tensor(
                s[:], sig_col[:], 1.0 / 127.0, gbc[:], ALU.mult, ALU.mult),
                wait={"SE": s_sig, "sBAR": sd4}, inc="VE")
            v4 = p.add("vector", lambda e, s=s_col, r=rs_col: e.reciprocal(r[:], s[:]),
                       wait={"VE": v3}, keep_self=True, inc="VE")
            return v4

        v_b1 = barrier(stat[:, 10:11], d_cc1_in, d_cc1_out, d_s128a, s1_col, rs1_col,
                       clipx_sb, {"VE": ve_pb})
        p.add("vector", lambda e: e.tensor_scalar(
            uscale_all[:], r2_all[:], rs1_col[:], None, ALU.mult),
            wait={"VE": v_b1}, keep_self=True, inc="VE")
        p.add("vector", lambda e: e.tensor_scalar(
            ubias_all[:], nmr2_all[:], rs1_col[:], None, ALU.mult), inc="VE")
        p.add("vector", lambda e: e.tensor_scalar(
            s1sg_all[:], sg_all[:], s1_col[:], None, ALU.mult),
            wait={"VE": ve_prep_pa}, inc="VE")
        v_b1e = p.add("vector", lambda e: e.tensor_scalar(
            s1su_all[:], su_all[:], s1_col[:], None, ALU.mult), inc="VE")
        if "s1" in dbg:
            p.dma("sync", dbg["s1"], s1_col[:], wait={"VE": v_b1e}, sem="sDBG")

        p.fence()

        esC = ExitStack()
        if True:
            wqTsb = esC.enter_context(nc.sbuf_tensor("wqTsb", [128, 2 * NIC * C], BF16))
            ubuf = esC.enter_context(nc.sbuf_tensor("ubuf", [128, C], F32))
            t1buf = esC.enter_context(nc.sbuf_tensor("t1buf", [128, C], F32))
            xqtok = esC.enter_context(nc.sbuf_tensor("xqtok", [128, C], BF16))
            sgb0 = esC.enter_context(nc.sbuf_tensor("sgb0", [128, 512], BF16))
            sgb1 = esC.enter_context(nc.sbuf_tensor("sgb1", [128, 512], BF16))
            hb0 = esC.enter_context(nc.sbuf_tensor("hb0", [128, 512], F32))
            hb1 = esC.enter_context(nc.sbuf_tensor("hb1", [128, 512], F32))
            hb2 = esC.enter_context(nc.sbuf_tensor("hb2", [128, 512], F32))
            pg0 = esC.enter_context(nc.psum_tensor("pg0", [128, 512], F32))
            pu0 = esC.enter_context(nc.psum_tensor("pu0", [128, 512], F32))
            pg1 = esC.enter_context(nc.psum_tensor("pg1", [128, 512], F32))
            pu1 = esC.enter_context(nc.psum_tensor("pu1", [128, 512], F32))
            pg2 = esC.enter_context(nc.psum_tensor("pg2", [128, 512], F32))
            pu2 = esC.enter_context(nc.psum_tensor("pu2", [128, 512], F32))
            ptrC = esC.enter_context(nc.psum_tensor("ptrC", [128, 1024], BF16))
            wqT_v = wqTsb[:].rearrange("p (m i c) -> p m i c", m=2, i=NIC)
            for mi in range(2):
                for ii in range(NIC):
                    p.dma("gpsimd", wqT_v[:, mi, ii, :], d_wqT[mi, ii],
                          wait={"sWSG0": gd_prep_pa[0], "sWSG1": gd_prep_pa[1]},
                          sem="sWQT")
            gd_wqT = p.cnt["sWQT"]

            xq_evac_ve = [0] * NT
            x1_free = [dict(x1b_free[0]), dict(x1b_free[1])]
            trC_free = [0]
            ubuf_free = [0]
            xqtok_free2 = [0]

            def emit_xq_tile(tt):
                bi = tt % 2
                sdx = p.dma("sync", x1b[bi][:], d_x1[tt * 128:(tt + 1) * 128, :],
                            wait=merge(x1_free[bi], sd_x1_pb[tt]), sem=f"sX1R{bi}")
                s_u = p.add("scalar", lambda e, bi=bi, tt=tt: e.activation(
                    ubuf[:], x1b[bi][:], AF.Identity, bias=ubias_all[:, tt:tt + 1],
                    scale=uscale_all[:, tt:tt + 1]),
                    wait={f"sX1R{bi}": sdx, "VE": max(v_b1e, ubuf_free[0])}, inc="SE")
                v_t1 = p.add("vector", lambda e: e.tensor_scalar(
                    t1buf[:], ubuf[:], MAGIC, MAGIC, ALU.add, ALU.subtract),
                    wait={"SE": s_u}, inc="VE")
                ubuf_free[0] = v_t1
                v_xq = p.add("vector", lambda e: e.tensor_scalar(
                    xqtok[:], t1buf[:], 127.0, -128.0, ALU.min, ALU.max),
                    wait={"TE": xqtok_free2[0]}, inc="VE")
                x1_free[bi] = {"VE": v_xq}
                te_tr = None
                for j in range(8):
                    te_tr = p.add("tensor", lambda e, j=j: e.transpose(
                        ptrC[:, j * 128:(j + 1) * 128], xqtok[:, j * 128:(j + 1) * 128],
                        ident[:]),
                        wait={"VE": max(v_xq, trC_free[0])} if j == 0 else None, inc="TE")
                xqtok_free2[0] = te_tr
                v_ev = p.add("vector", lambda e, tt=tt: e.tensor_copy(
                    bigv[:, :, tt * 128:(tt + 1) * 128], ptrC[:, 0:1024]),
                    wait={"TE": te_tr}, inc="VE")
                trC_free[0] = v_ev
                xq_evac_ve[tt] = v_ev

            for tt in range(8):
                emit_xq_tile(tt)

            pg = [pg0, pg1, pg2]
            pu = [pu0, pu1, pu2]
            sgb = [sgb0, sgb1]
            hb = [hb0, hb1, hb2]
            pg_free = [0, 0, 0]
            pu_free = [0, 0, 0]
            sg_free = [0, 0]
            hb_free = [{}, {}, {}]
            prev_hm = 0
            hi = 0

            for ns in range(8):
                for t3 in range(8 + ns * 3, min(NT, 8 + (ns + 1) * 3)):
                    emit_xq_tile(t3)
                need_ve = xq_evac_ve[4 * ns + 3]
                for ii in range(NIC):
                    pi = (ns * NIC + ii) % 3
                    te_g = None
                    for hbk in range(CCH):
                        te_g = p.add("tensor", lambda e, ii=ii, hbk=hbk, ns=ns, pi=pi:
                                     e.matmul(pg[pi][:, 0:512],
                                              wqT_v[:, 0, ii, hbk * 128:(hbk + 1) * 128],
                                              bigv[:, hbk, ns * 512:(ns + 1) * 512],
                                              start=(hbk == 0), stop=(hbk == CCH - 1)),
                                     wait={"VE": need_ve, "sWQT": gd_wqT, "SE": pg_free[pi]}
                                     if hbk == 0 else None, inc="TE")
                    te_u = None
                    for hbk in range(CCH):
                        te_u = p.add("tensor", lambda e, ii=ii, hbk=hbk, ns=ns, pi=pi:
                                     e.matmul(pu[pi][:, 0:512],
                                              wqT_v[:, 1, ii, hbk * 128:(hbk + 1) * 128],
                                              bigv[:, hbk, ns * 512:(ns + 1) * 512],
                                              start=(hbk == 0), stop=(hbk == CCH - 1)),
                                     wait={"VE": pu_free[pi]} if hbk == 0 else None,
                                     inc="TE")
                    si = (ns * NIC + ii) % 2
                    s_sg = p.add("scalar", lambda e, ii=ii, pi=pi, si=si: e.activation(
                        sgb[si][:], pg[pi][:, 0:512], AF.Silu,
                        scale=s1sg_all[:, ii:ii + 1]),
                        wait={"TE": te_g, "VE": max(v_b1e, sg_free[si])}, inc="SE")
                    pg_free[pi] = s_sg
                    hbi = hi % 3
                    hi += 1
                    v_h = p.add("vector", lambda e, ii=ii, pi=pi, si=si, hbi=hbi:
                                e.scalar_tensor_tensor(hb[hbi][:], pu[pi][:, 0:512],
                                                       s1su_all[:, ii:ii + 1], sgb[si][:],
                                                       ALU.mult, ALU.mult),
                                wait=merge({"TE": te_u, "SE": s_sg}, hb_free[hbi]),
                                inc="VE")
                    pu_free[pi] = v_h
                    sg_free[si] = v_h
                    v_hm = p.add("vector", lambda e, hbi=hbi: e.reduce_max(
                        stat[:, 9:10], hb[hbi][:], AX.X, apply_absolute_value=True),
                        wait={"VE": prev_hm}, inc="VE")
                    kk = ns * NIC + ii
                    hmo, hmn = 12 + (kk % 2), 12 + ((kk + 1) % 2)
                    prev_hm = p.add("vector", lambda e, hmo=hmo, hmn=hmn: e.tensor_scalar(
                        stat[:, hmn:hmn + 1], stat[:, 9:10], stat[:, hmo:hmo + 1], None,
                        ALU.max), wait={"VE": v_hm}, keep_self=True, inc="VE")
                    sd_h = p.dma("sync", d_h[ii][:, ns * 512:(ns + 1) * 512], hb[hbi][:],
                                 wait={"VE": v_h}, sem=f"sHW{hbi}")
                    hb_free[hbi] = {f"sHW{hbi}": sd_h}
            ve_pc = p.cnt["VE"]
            sd_pc = {f"sHW{k}": p.cnt.get(f"sHW{k}", 0) for k in range(3)}
            if "xq" in dbg:
                p.dma("sync", dbg["xq"], big[:], wait={"VE": ve_pc}, sem="sDBG")
            if "h0" in dbg:
                p.dma("sync", dbg["h0"], d_h[0], wait=sd_pc, sem="sDBG")
        esC.close()

        if "hmax" in dbg:
            p.dma("sync", dbg["hmax"], stat[:, 12:13], wait={"VE": ve_pc}, sem="sDBG")
        if "hall" in dbg:
            for ci2 in range(NIC):
                p.dma("sync", dbg["hall"][ci2 * 128:(ci2 + 1) * 128, :], d_h[ci2],
                      wait=sd_pc, sem="sDBG")
        v_b2 = barrier(stat[:, 12:13], d_cc2_in, d_cc2_out, d_s128b, s2_col, rs2_col,
                       clipd_sb, {"VE": ve_pc})
        if "s2" in dbg:
            p.dma("sync", dbg["s2"], s2_col[:], wait={"VE": v_b2}, sem="sDBG")

        esBig.close()
        p.fence()

        esD = ExitStack()
        if True:
            wdTsb = esD.enter_context(nc.sbuf_tensor("wdTsb", [128, NIC * C], BF16))
            hqs0 = esD.enter_context(nc.sbuf_tensor("hqs0", [128, NIC * 512], BF16))
            hqs1 = esD.enter_context(nc.sbuf_tensor("hqs1", [128, NIC * 512], BF16))
            hch0 = esD.enter_context(nc.sbuf_tensor("hch0", [128, 512], F32))
            hch1 = esD.enter_context(nc.sbuf_tensor("hch1", [128, 512], F32))
            hch2 = esD.enter_context(nc.sbuf_tensor("hch2", [128, 512], F32))
            uch0 = esD.enter_context(nc.sbuf_tensor("uch0", [128, 512], F32))
            uch1 = esD.enter_context(nc.sbuf_tensor("uch1", [128, 512], F32))
            tch0 = esD.enter_context(nc.sbuf_tensor("tch0", [128, 512], F32))
            tch1 = esD.enter_context(nc.sbuf_tensor("tch1", [128, 512], F32))
            outb0 = esD.enter_context(nc.sbuf_tensor("outb0", [128, C], F32))
            outb1 = esD.enter_context(nc.sbuf_tensor("outb1", [128, C], F32))
            pd0 = esD.enter_context(nc.psum_tensor("pd0", [128, 512], F32))
            pd1 = esD.enter_context(nc.psum_tensor("pd1", [128, 512], F32))
            pd2 = esD.enter_context(nc.psum_tensor("pd2", [128, 512], F32))
            pd3 = esD.enter_context(nc.psum_tensor("pd3", [128, 512], F32))

            wdT_v = wdTsb[:].rearrange("p (i c) -> p i c", i=NIC)
            wdT_ld = []
            for ci in range(NIC):
                v = p.dma("gpsimd", wdT_v[:, ci, :].rearrange("p (c b) -> p c b", b=128),
                          d_wdT[ci].transpose([1, 0, 2]),
                          wait={"sDS0": gd_prep_pb[0], "sDS1": gd_prep_pb[1]},
                          sem=f"sWDT{ci}")
                wdT_ld.append(v)

            hqs = [hqs0, hqs1]
            hqsv = [h[:].rearrange("p (i n) -> p i n", n=512) for h in hqs]
            hch = [hch0, hch1, hch2]
            uch = [uch0, uch1]
            tch = [tch0, tch1]
            outb = [outb0, outb1]
            pdk = [(pd0, pd1), (pd2, pd3)]
            hch_free = [{}, {}, {}]
            uch_free = [{}, {}]
            tch_free = [{}, {}]
            hqs_free = [{}, {}]
            pd_free = [0, 0]
            outb_free = [0, 0]
            sd_out_last = None
            v_hq_last_arr = [0] * 8

            def pd_quant(ns):
                si = ns % 2
                v_hq_last = None
                for ci in range(NIC):
                    k = ns * NIC + ci
                    h3 = k % 3
                    u2 = k % 2
                    sdh = p.dma("sync", hch[h3][:],
                                d_h[ci][:, ns * 512:(ns + 1) * 512],
                                wait=merge(hch_free[h3], sd_pc if k < 3 else None),
                                sem=f"sHR{h3}")
                    if k == 0 and "pdu" in dbg:
                        sd_d1 = p.dma("sync", dbg["pdu"][:, 0:512], hch[h3][:],
                                      wait={f"sHR{h3}": sdh}, sem="sDBG")
                        hch_free[h3] = merge(hch_free[h3], {"sDBG": sd_d1})
                    s_u = p.add("scalar", lambda e, h3=h3, u2=u2: e.activation(
                        uch[u2][:], hch[h3][:], AF.Identity, bias=constf[:, 1:2],
                        scale=rs2_col[:]),
                        wait=merge({f"sHR{h3}": sdh, "VE": v_b2}, uch_free[u2]), inc="SE")
                    hch_free[h3] = merge(hch_free[h3], {"SE": s_u})
                    if k == 0 and "pdu" in dbg:
                        sd_d2 = p.dma("sync", dbg["pdu"][:, 512:1024], uch[u2][:],
                                      wait={"SE": s_u}, sem="sDBG")
                        uch_free[u2] = merge(uch_free[u2], {"sDBG": sd_d2})
                    v_t1 = p.add("vector", lambda e, u2=u2: e.tensor_scalar(
                        tch[u2][:], uch[u2][:], MAGIC, 127.0, ALU.subtract, ALU.min),
                        wait=merge({"SE": s_u}, tch_free[u2]), inc="VE")
                    uch_free[u2] = {"VE": v_t1}
                    if k == 0 and "pdu" in dbg:
                        sd_d3 = p.dma("sync", dbg["pdu"][:, 1024:1536], tch[u2][:],
                                      wait={"VE": v_t1}, sem="sDBG")
                        tch_free[u2] = merge(tch_free[u2], {"sDBG": sd_d3})
                    v_hq = p.add("vector", lambda e, si=si, ci=ci, u2=u2:
                        e.tensor_scalar(hqsv[si][:, ci, :], tch[u2][:], -128.0, None,
                                        ALU.max),
                        wait=merge({"VE": v_t1}, hqs_free[si] if ci == 0 else None),
                        keep_self=True, inc="VE")
                    tch_free[u2] = {"VE": v_hq}
                    v_hq_last = v_hq
                v_hq_last_arr[ns] = v_hq_last
                if ns == 0 and "hqs" in dbg:
                    sd_hqd = p.dma("sync", dbg["hqs"], hqs0[:], wait={"VE": v_hq_last},
                                   sem="sDBG")
                    hqs_free[0] = merge(hqs_free[0], {"sDBG": sd_hqd})

            def pd_mms(ns):
                si = ns % 2
                v_hq_last = v_hq_last_arr[ns]
                if ns == 0 and "wdts" in dbg:
                    p.dma("sync", dbg["wdts"], wdTsb[:],
                          wait={f"sWDT{c2}": wdT_ld[c2] for c2 in range(NIC)}, sem="sDBG")
                for tl in range(4):
                    tt = ns * 4 + tl
                    bi = tt % 2
                    sdx = p.dma("gpsimd", x1b[bi][:], d_x1[tt * 128:(tt + 1) * 128, :],
                                wait=merge(x1_free[bi], sd_x1_pb[tt]), sem=f"sX1R{bi}")
                    pdi = tt % 2
                    pda, pdb = pdk[pdi]
                    te_d = None
                    for ci in range(NIC):
                        p.add("tensor", lambda e, ci=ci, si=si, tl=tl, pda=pda: e.matmul(
                            pda[:, 0:512], hqsv[si][:, ci, tl * 128:(tl + 1) * 128],
                            wdT_v[:, ci, 0:512], start=(ci == 0), stop=(ci == NIC - 1)),
                            wait=merge({"VE": max(v_hq_last, pd_free[pdi])}
                                       if ci == 0 else None,
                                       {f"sWDT{ci}": wdT_ld[ci]}), inc="TE")
                        te_d = p.add("tensor", lambda e, ci=ci, si=si, tl=tl, pdb=pdb: e.matmul(
                            pdb[:, 0:512], hqsv[si][:, ci, tl * 128:(tl + 1) * 128],
                            wdT_v[:, ci, 512:1024], start=(ci == 0), stop=(ci == NIC - 1)),
                            inc="TE")
                    if tl == 3:
                        hqs_free[si] = merge({}, {"TE": te_d})
                    v_o1 = p.add("vector", lambda e, bi=bi, pda=pda: e.scalar_tensor_tensor(
                        outb[bi][:, 0:512], pda[:, 0:512], s2_col[:], x1b[bi][:, 0:512],
                        ALU.mult, ALU.add),
                        wait=merge({"TE": te_d, f"sX1R{bi}": sdx},
                                   {f"sOUT{bi}": outb_free[bi]}), inc="VE")
                    v_o2 = p.add("vector", lambda e, bi=bi, pdb=pdb: e.scalar_tensor_tensor(
                        outb[bi][:, 512:1024], pdb[:, 0:512], x1b[bi]
                        if False else s2_col[:], x1b[bi][:, 512:1024],
                        ALU.mult, ALU.add), inc="VE")
                    pd_free[pdi] = v_o2
                    x1_free[bi] = {"VE": v_o2}
                    sd_o = p.dma("sync", d_out[tt * 128:(tt + 1) * 128, :], outb[bi][:],
                                 wait={"VE": v_o2}, sem=f"sOUT{bi}")
                    outb_free[bi] = sd_o

            pd_quant(0)
            for ns in range(8):
                if ns + 1 < 8:
                    pd_quant(ns + 1)
                pd_mms(ns)

            p.add("sync", lambda e: None,
                  wait={"sOUT0": p.cnt["sOUT0"], "sOUT1": p.cnt["sOUT1"]})
        esD.close()

        with nc.allow_low_precision(reason="bf16 matmul operands / quantized codes by design; fp32 accumulation everywhere"):
            p.flush(blk)
    return nc


_NC_CACHE = {}


def _consts():
    ident = np.eye(128, dtype=np.float32).astype(BF)
    ones_col = np.ones((128, 1), np.float32).astype(BF)
    ones_row = np.ones((1, 1024), np.float32).astype(BF)
    ones_f32 = np.ones((1, 128), np.float32)
    eps_row = np.full((1, 16), 1e-6, np.float32).astype(BF)
    ind = np.zeros((8, 16, 128), np.float32)
    for g in range(8):
        ind[g, 2 * g, 0:64] = 1.0
        ind[g, 2 * g + 1, 64:128] = 1.0
    return ident, ones_col, ones_row, ones_f32, eps_row, ind.astype(BF)


def kernel(L0_attn_x, L0_attn_W_qkv, L0_attn_W_f, L0_attn_b_f, L0_attn_W_out,
           L0_attn_b_out, L0_ffn_clip_x, L0_ffn_clip_down, L0_ffn_w_gate,
           L0_ffn_w_up, L0_ffn_w_down):
    x = np.ascontiguousarray(np.asarray(L0_attn_x, dtype=np.float32))
    ident, ones_col, ones_row, ones_f32, eps_row, ind = _consts()
    shared = {
        "wqkv": np.ascontiguousarray(np.asarray(L0_attn_W_qkv, np.float32)),
        "wf": np.ascontiguousarray(np.asarray(L0_attn_W_f, np.float32)),
        "bfv": np.ascontiguousarray(np.asarray(L0_attn_b_f, np.float32).reshape(1, DH)),
        "wout": np.ascontiguousarray(np.asarray(L0_attn_W_out, np.float32)),
        "bout": np.ascontiguousarray(np.asarray(L0_attn_b_out, np.float32).reshape(1, C)),
        "clipx": np.ascontiguousarray(np.broadcast_to(
            np.asarray(L0_ffn_clip_x, np.float32).reshape(1, 1), (128, 1))),
        "clipd": np.ascontiguousarray(np.broadcast_to(
            np.asarray(L0_ffn_clip_down, np.float32).reshape(1, 1), (128, 1))),
        "wg": np.ascontiguousarray(np.asarray(L0_ffn_w_gate, np.float32)),
        "wu": np.ascontiguousarray(np.asarray(L0_ffn_w_up, np.float32)),
        "wd": np.ascontiguousarray(np.asarray(L0_ffn_w_down, np.float32)),
        "ident": ident, "ones_col": ones_col, "ones_row": ones_row,
        "ones_f32": ones_f32, "eps_row": eps_row, "ind": ind,
        "constf": np.ascontiguousarray(np.broadcast_to(
            np.array([[EPS, MAGIC]], np.float32), (128, 2))),
    }
    if "nc" not in _NC_CACHE:
        _NC_CACHE["nc"] = build()
    nc = _NC_CACHE["nc"]
    in_maps = [dict(shared, x=np.ascontiguousarray(x[b])) for b in range(N_CORES)]
    res = run_bass_kernel_spmd(nc, in_maps, core_ids=list(range(N_CORES)))
    out = np.stack([np.asarray(res.results[b]["out"]) for b in range(N_CORES)], axis=0)
    return out


# version 12
# speedup vs baseline: 1.3484x; 1.0350x over previous
import numpy as np
import ml_dtypes
from contextlib import ExitStack

import concourse.bass as bass
import concourse.mybir as mybir
from concourse.bass_utils import run_bass_kernel_spmd

F32 = mybir.dt.float32
BF16 = mybir.dt.bfloat16
AF = mybir.ActivationFunctionType
ALU = mybir.AluOpType
AX = mybir.AxisListType
BF = ml_dtypes.bfloat16

MAGIC = float(np.float32(1.5 * 2 ** 23))
N_CORES = 8
SEQ = 4096
C = 1024
NT = SEQ // 128
CCH = 8
H = 16
DH = 64
NI = 2816
NIC = NI // 128
EPS = 1e-5

COMPUTE_SEMS = {"TE", "VE", "SE", "GE"}


def merge(*ds):
    out = {}
    for d in ds:
        for k, v in (d or {}).items():
            if v:
                out[k] = max(out.get(k, 0), v)
    return out


class Prog:
    ENGS = ("sync", "tensor", "vector", "scalar", "gpsimd")

    def __init__(self, nc, es):
        self.nc = nc
        self.es = es
        self.sems = {}
        self.cnt = {}
        self.streams = {e: [] for e in self.ENGS}
        self.lastwait = {e: {} for e in self.ENGS}
        self.eng_of_sem = {"TE": "tensor", "VE": "vector", "SE": "scalar",
                           "GE": "gpsimd"}
        for name in self.eng_of_sem:
            self.sems[name] = es.enter_context(nc.semaphore(name))
            self.cnt[name] = 0

    def sem(self, name):
        if name not in self.sems:
            self.sems[name] = self.es.enter_context(self.nc.semaphore(name))
            self.cnt[name] = 0
        return name

    SERIAL = False

    def add(self, eng, fn, wait=None, inc=None, inc_amt=1, keep_self=False):
        if Prog.SERIAL and inc is not None:
            wait = merge(wait, getattr(self, "_last", None))
            keep_self = True
        w = {}
        for s, v in (wait or {}).items():
            if not v:
                continue
            assert v <= self.cnt[s], f"wait {s}>={v} but only {self.cnt[s]} emitted"
            if (not keep_self) and s in COMPUTE_SEMS and self.eng_of_sem[s] == eng:
                continue
            if self.lastwait[eng].get(s, -1) < v:
                w[s] = v
                self.lastwait[eng][s] = v
        if inc:
            self.cnt[inc] += inc_amt
        self.streams[eng].append((w, fn, inc, inc_amt))
        if Prog.SERIAL and inc is not None:
            self._last = {inc: self.cnt[inc]}
        return self.cnt[inc] if inc else None

    def dma(self, eng, out, in_, wait=None, sem=None):
        assert sem is not None, "dma needs an explicit stream semaphore"
        self.sem(sem)
        return self.add(eng, lambda e, o=out, i=in_: e.dma_start(out=o, in_=i),
                        wait=wait, inc=sem, inc_amt=16)

    def fence(self):
        cur = dict(self.cnt)
        for eng in self.ENGS:
            self.add(eng, lambda e: None, wait=dict(cur))

    def flush(self, blk):
        for eng in self.ENGS:
            ops = self.streams[eng]
            self.streams[eng] = []
            if not ops:
                continue
            sems = self.sems

            def mk(ops_):
                def run(e):
                    for w, fn, inc, amt in ops_:
                        for s, v in w.items():
                            e.wait_ge(sems[s], v)
                        ins = fn(e)
                        if inc is not None:
                            ins.then_inc(sems[inc], amt)
                return run
            getattr(blk, eng)(mk(ops))


def build(debug=()):
    nc = bass.Bass("TRN2", target_bir_lowering=False)

    d_x = nc.dram_tensor("x", [SEQ, C], F32, kind="ExternalInput").ap()
    d_wqkv = nc.dram_tensor("wqkv", [C, 3 * C], F32, kind="ExternalInput").ap()
    d_wf = nc.dram_tensor("wf", [DH, DH], F32, kind="ExternalInput").ap()
    d_bf = nc.dram_tensor("bfv", [1, DH], F32, kind="ExternalInput").ap()
    d_wout = nc.dram_tensor("wout", [C, C], F32, kind="ExternalInput").ap()
    d_bout = nc.dram_tensor("bout", [1, C], F32, kind="ExternalInput").ap()
    d_clipx = nc.dram_tensor("clipx", [128, 1], F32, kind="ExternalInput").ap()
    d_clipd = nc.dram_tensor("clipd", [128, 1], F32, kind="ExternalInput").ap()
    d_wg = nc.dram_tensor("wg", [NI, C], F32, kind="ExternalInput").ap()
    d_wu = nc.dram_tensor("wu", [NI, C], F32, kind="ExternalInput").ap()
    d_wd = nc.dram_tensor("wd", [C, NI], F32, kind="ExternalInput").ap()
    d_ident = nc.dram_tensor("ident", [128, 128], BF16, kind="ExternalInput").ap()
    d_ones_col = nc.dram_tensor("ones_col", [128, 1], BF16, kind="ExternalInput").ap()
    d_ones_row = nc.dram_tensor("ones_row", [1, 1024], BF16, kind="ExternalInput").ap()
    d_ones_f32 = nc.dram_tensor("ones_f32", [1, 128], F32, kind="ExternalInput").ap()
    d_eps_row = nc.dram_tensor("eps_row", [1, 16], BF16, kind="ExternalInput").ap()
    d_ind = nc.dram_tensor("ind", [8, 16, 128], BF16, kind="ExternalInput").ap()
    d_constf = nc.dram_tensor("constf", [128, 2], F32, kind="ExternalInput").ap()

    d_out = nc.dram_tensor("out", [SEQ, C], F32, kind="ExternalOutput").ap()

    d_x1 = nc.dram_tensor("x1_scr", [SEQ, C], F32, kind="Internal").ap()
    d_h = nc.dram_tensor("h_scr", [NIC, 128, SEQ], F32, kind="Internal").ap()
    d_wqT = nc.dram_tensor("wqT_scr", [2, NIC, 128, CCH * 128], BF16, kind="Internal").ap()
    d_wdT = nc.dram_tensor("wdT_scr", [NIC, CCH, 128, 128], BF16, kind="Internal").ap()
    d_cc1_in = nc.dram_tensor("cc1_in", [128, 1], F32, kind="Internal").ap()
    d_cc1_out = nc.dram_tensor("cc1_out", [128, 1], F32, kind="Internal", addr_space="Shared").ap()
    d_cc2_in = nc.dram_tensor("cc2_in", [128, 1], F32, kind="Internal").ap()
    d_cc2_out = nc.dram_tensor("cc2_out", [128, 1], F32, kind="Internal", addr_space="Shared").ap()
    d_s128a = nc.dram_tensor("s128a", [128, 1], F32, kind="Internal").ap()
    d_s128b = nc.dram_tensor("s128b", [128, 1], F32, kind="Internal").ap()

    dbg = {}
    for name in debug:
        shp = {"x1": ([SEQ, C], F32), "q_fm": ([128, CCH * SEQ], BF16),
               "kvx": ([128, 8 * 128], BF16), "ks": ([128, 8 * 16], BF16),
               "xq": ([128, CCH * SEQ], BF16), "h0": ([128, SEQ], F32),
               "s1": ([128, 1], F32), "s2": ([128, 1], F32),
               "wall": ([128, CCH * 3 * C], BF16), "xnfm": ([128, CCH * 128], BF16),
               "wdts": ([128, NIC * C], BF16), "hqs": ([128, NIC * 512], BF16),
               "hmax": ([128, 1], F32), "hall": ([NIC * 128, SEQ], F32),
               "pdu": ([128, 3 * 512], F32),
               "ktok": ([128, C], BF16), "vtok": ([128, C], BF16),
               "qtok": ([128, C], BF16), "stat0": ([128, 16], F32)}[name]
        dt_ = BF16 if name in ("q_fm", "kvx", "ks", "xq", "wall", "xnfm", "ktok",
                               "vtok", "qtok", "wdts", "hqs") else F32
        dbg[name] = nc.dram_tensor("dbg_" + name, shp[0], dt_, kind="ExternalOutput").ap()

    es_all = ExitStack()
    with es_all:
        es = es_all
        ident = es.enter_context(nc.sbuf_tensor("identb", [128, 128], BF16))
        ones_col = es.enter_context(nc.sbuf_tensor("onescol", [128, 1], BF16))
        ones_row = es.enter_context(nc.sbuf_tensor("onesrow", [1, 1024], BF16))
        ones_f32 = es.enter_context(nc.sbuf_tensor("onesf32", [1, 128], F32))
        eps_row = es.enter_context(nc.sbuf_tensor("epsrow", [1, 16], BF16))
        ind_sb = es.enter_context(nc.sbuf_tensor("indsb", [16, 8 * 128], BF16))
        bias_row = es.enter_context(nc.sbuf_tensor("biasrow", [1, 3 * C], BF16))
        bf_bf = es.enter_context(nc.sbuf_tensor("bfbf", [1, DH], BF16))
        bout_row = es.enter_context(nc.sbuf_tensor("boutrow", [1, C], BF16))
        clipx_sb = es.enter_context(nc.sbuf_tensor("clipxsb", [128, 1], F32))
        clipd_sb = es.enter_context(nc.sbuf_tensor("clipdsb", [128, 1], F32))
        r2_all = es.enter_context(nc.sbuf_tensor("r2all", [128, NT], F32))
        nmr2_all = es.enter_context(nc.sbuf_tensor("nmr2all", [128, NT], F32))
        sg_all = es.enter_context(nc.sbuf_tensor("sgall", [128, NIC], F32))
        su_all = es.enter_context(nc.sbuf_tensor("suall", [128, NIC], F32))
        s1_col = es.enter_context(nc.sbuf_tensor("s1col", [128, 1], F32))
        rs1_col = es.enter_context(nc.sbuf_tensor("rs1col", [128, 1], F32))
        s2_col = es.enter_context(nc.sbuf_tensor("s2col", [128, 1], F32))
        rs2_col = es.enter_context(nc.sbuf_tensor("rs2col", [128, 1], F32))
        uscale_all = es.enter_context(nc.sbuf_tensor("uscaleall", [128, NT], F32))
        ubias_all = es.enter_context(nc.sbuf_tensor("ubiasall", [128, NT], F32))
        s1sg_all = es.enter_context(nc.sbuf_tensor("s1sgall", [128, NIC], F32))
        s1su_all = es.enter_context(nc.sbuf_tensor("s1suall", [128, NIC], F32))
        grow = es.enter_context(nc.sbuf_tensor("growbuf", [1, 128], F32))
        gcol = es.enter_context(nc.sbuf_tensor("gcolbuf", [1, 1], F32))
        gbc = es.enter_context(nc.sbuf_tensor("gbcbuf", [128, 1], F32))
        sig_col = es.enter_context(nc.sbuf_tensor("sigcol", [128, 1], F32))
        kvx_sb = es.enter_context(nc.sbuf_tensor("kvxsb", [128, 8 * 128], BF16))
        ks_sb = es.enter_context(nc.sbuf_tensor("kssb", [128, 8 * 16], BF16))
        xb = [es.enter_context(nc.sbuf_tensor(f"xb{i}", [128, C], F32)) for i in range(2)]
        x1b = [es.enter_context(nc.sbuf_tensor(f"x1b{i}", [128, C], F32)) for i in range(2)]
        sq = es.enter_context(nc.sbuf_tensor("sqscr", [128, C], F32))
        stat = es.enter_context(nc.sbuf_tensor("statscr", [128, 16], F32))
        constf = es.enter_context(nc.sbuf_tensor("constf_sb", [128, 2], F32))
        esBig = ExitStack()
        big = esBig.enter_context(nc.sbuf_tensor("big", [128, CCH * SEQ], BF16))

        blk = es.enter_context(nc.Block())
        p = Prog(nc, es)
        bigv = big[:].rearrange("p (c n) -> p c n", n=SEQ)

        xb_free = [{}, {}]
        x1b_free = [{}, {}]

        for o, i in ((ident[:], d_ident), (ones_col[:], d_ones_col),
                     (ones_row[:], d_ones_row), (ones_f32[:], d_ones_f32),
                     (eps_row[:], d_eps_row), (clipx_sb[:], d_clipx),
                     (clipd_sb[:], d_clipd), (constf[:], d_constf)):
            p.dma("sync", o, i, sem="sCONST")
        for g in range(8):
            p.dma("sync", ind_sb[:, g * 128:(g + 1) * 128], d_ind[g], sem="sCONST")
        sd_consts = p.cnt["sCONST"]
        p.dma("gpsimd", bf_bf[:], d_bf, sem="sCONST2")
        p.dma("gpsimd", bout_row[:], d_bout, sem="sCONST2")
        gd_consts = p.cnt["sCONST2"]

        p.add("vector", lambda e: e.memset(bias_row[:], 0.0), inc="VE")
        for j in range(16):
            p.add("vector", lambda e, j=j: e.tensor_copy(
                bias_row[:, j * 64:(j + 1) * 64], bf_bf[:]),
                wait={"sCONST2": gd_consts}, inc="VE")
            p.add("vector", lambda e, j=j: e.tensor_copy(
                bias_row[:, 2048 + j * 64: 2048 + (j + 1) * 64], bf_bf[:]), inc="VE")
        p.add("vector", lambda e: e.memset(stat[:, 10:14], 0.0), inc="VE")
        ve_consts = p.cnt["VE"]

        esA = ExitStack()
        if True:
            Wall = esA.enter_context(nc.sbuf_tensor("Wall", [128, CCH * 3 * C], BF16))
            wf_sb = esA.enter_context(nc.sbuf_tensor("wf_sb", [DH, DH], BF16))
            wTb = esA.enter_context(nc.sbuf_tensor("wTbuf", [64, 1024], BF16))
            xnfm0 = esA.enter_context(nc.sbuf_tensor("xnfm0", [128, CCH * 128], BF16))
            xnfm1 = esA.enter_context(nc.sbuf_tensor("xnfm1", [128, CCH * 128], BF16))
            xntok = esA.enter_context(nc.sbuf_tensor("xntok", [128, C], BF16))
            ktok0 = esA.enter_context(nc.sbuf_tensor("ktok0", [128, C], BF16))
            ktok1 = esA.enter_context(nc.sbuf_tensor("ktok1", [128, C], BF16))
            vtok0 = esA.enter_context(nc.sbuf_tensor("vtok0", [128, C], BF16))
            vtok1 = esA.enter_context(nc.sbuf_tensor("vtok1", [128, C], BF16))
            qtok0 = esA.enter_context(nc.sbuf_tensor("qtok0", [128, C], BF16))
            qtok1 = esA.enter_context(nc.sbuf_tensor("qtok1", [128, C], BF16))
            ptr6 = esA.enter_context(nc.psum_tensor("ptr6", [128, 1024], BF16))
            ptr7 = esA.enter_context(nc.psum_tensor("ptr7", [128, 1024], BF16))
            pkv = esA.enter_context(nc.psum_tensor("pkv", [128, 512], F32))
            pks = esA.enter_context(nc.psum_tensor("pks", [128, 512], F32))
            pa0 = esA.enter_context(nc.psum_tensor("pa0", [128, 512], F32))
            pa1 = esA.enter_context(nc.psum_tensor("pa1", [128, 512], F32))
            pa2 = esA.enter_context(nc.psum_tensor("pa2", [128, 512], F32))
            pa3 = esA.enter_context(nc.psum_tensor("pa3", [128, 512], F32))
            Wall_v = Wall[:].rearrange("p (c n) -> p c n", n=3 * C)
            pabank = [pa0, pa1, pa2, pa3]

            p.dma("gpsimd", wf_sb[:], d_wf, sem="sWF")
            gd_wf = p.cnt["sWF"]
            for cc in range(CCH):
                p.dma("gpsimd", Wall_v[:, cc, 1024:2048],
                      d_wqkv[cc * 128:(cc + 1) * 128, 2048:3072], sem="sWALL")
            gd_v = p.cnt["sWALL"]

            te_fold_tr_last = 0
            ve_fold = 0
            esW = ExitStack()
            if True:
                wstage = esW.enter_context(nc.sbuf_tensor("wstage", [128, CCH * 2048], BF16))
                wstage_v = wstage[:].rearrange("p (c n) -> p c n", n=2048)
                for cc in range(CCH):
                    p.dma("gpsimd", wstage_v[:, cc, 0:1024],
                          d_wqkv[cc * 128:(cc + 1) * 128, 0:1024], sem="sSTG")
                    p.dma("gpsimd", wstage_v[:, cc, 1024:2048],
                          d_wqkv[cc * 128:(cc + 1) * 128, 1024:2048], sem="sSTG")
                gd_stage = p.cnt["sSTG"]

                tr6_ve = 0
                mm_ve = [0, 0]
                wT_te = 0
                gidx = 0
                for src_off, dst_off in ((1024, 0), (0, 2048)):
                    for hh in range(H):
                        bk = gidx % 2
                        gidx += 1
                        te_tr = None
                        for cc in range(CCH):
                            te_tr = p.add("tensor", lambda e, cc=cc, so=src_off, hh=hh:
                                          e.transpose(ptr6[0:64, cc * 128:(cc + 1) * 128],
                                                      wstage_v[:, cc, so + hh * 64: so + (hh + 1) * 64],
                                                      ident[:]),
                                          wait={"sSTG": gd_stage, "sCONST": sd_consts,
                                                "VE": tr6_ve} if cc == 0 else None,
                                          inc="TE")
                        te_fold_tr_last = te_tr
                        v_wt = p.add("vector", lambda e: e.tensor_copy(
                            wTb[:], ptr6[0:64, :]),
                            wait={"TE": max(te_tr, wT_te)}, inc="VE")
                        tr6_ve = v_wt
                        te_mm = None
                        for cc in range(CCH):
                            te_mm = p.add("tensor", lambda e, cc=cc, bk=bk:
                                          e.matmul(pabank[bk][:, cc * 64:(cc + 1) * 64],
                                                   wTb[:, cc * 128:(cc + 1) * 128],
                                                   wf_sb[:], start=True, stop=True),
                                          wait={"VE": max(v_wt, mm_ve[bk]),
                                                "sWF": gd_wf} if cc == 0 else None,
                                          inc="TE")
                        wT_te = te_mm
                        mm_ve[bk] = p.add(
                            "vector",
                            lambda e, bk=bk, do=dst_off, hh=hh:
                            e.tensor_copy(
                                Wall_v[:, :, do + hh * 64: do + (hh + 1) * 64],
                                pabank[bk][:, 0:512].rearrange("p (c n) -> p c n", n=64)),
                            wait={"TE": te_mm}, inc="VE")
                ve_fold = p.cnt["VE"]
                if "wall" in dbg:
                    p.dma("sync", dbg["wall"], Wall[:], wait={"VE": ve_fold, "sWALL": gd_v}, sem="sDBG")
            esW.close()

            if True:
                wfbA = esA.enter_context(nc.sbuf_tensor("wfbA", [128, C], F32))
                wfbB = esA.enter_context(nc.sbuf_tensor("wfbB", [128, C], F32))
                wub = esA.enter_context(nc.sbuf_tensor("wub", [128, C], F32))
                wt1b = esA.enter_context(nc.sbuf_tensor("wt1b", [128, C], F32))
                whatbA = esA.enter_context(nc.sbuf_tensor("whatbA", [128, C], BF16))
                whatbB = esA.enter_context(nc.sbuf_tensor("whatbB", [128, C], BF16))
                whatbC = esA.enter_context(nc.sbuf_tensor("whatbC", [128, C], BF16))
                whatbD = esA.enter_context(nc.sbuf_tensor("whatbD", [128, C], BF16))
                wstgA = esA.enter_context(nc.sbuf_tensor("wstgA", [128, C], BF16))
                wstgB = esA.enter_context(nc.sbuf_tensor("wstgB", [128, C], BF16))
                wstat = esA.enter_context(nc.sbuf_tensor("wstat", [128, 8], F32))
                chunks = [("g", i) for i in range(NIC)] + [("u", i) for i in range(NIC)]
                wfb = [wfbA, wfbB]
                whatb = [whatbA, whatbB, whatbC, whatbD]
                wstg = [wstgA, wstgB]
                cp = {"s1": 0, "s2": 0,
                      "wfb_free": [{"TE": te_fold_tr_last}, {"TE": te_fold_tr_last}],
                      "what_free": [{}, {}, {}, {}], "wstg_free": [{}, {}],
                      "wstat_free": {}, "wub_free": {}, "what_v6": {}, "scales": {}}

                def prep_stage1():
                    if cp["s1"] >= len(chunks):
                        return
                    k = cp["s1"]
                    cp["s1"] += 1
                    kind, idx = chunks[k]
                    kb = k % 2
                    kb4 = k % 4
                    src_t = d_wg if kind == "g" else d_wu
                    gdv = p.dma("gpsimd", wfb[kb][:], src_t[idx * 128:(idx + 1) * 128, :],
                                wait=cp["wfb_free"][kb], sem=f"sWFB{kb}")
                    v1 = p.add("vector", lambda e, kb=kb: e.reduce_max(
                        wstat[:, 0:1], wfb[kb][:], AX.X, apply_absolute_value=True),
                        wait=merge({f"sWFB{kb}": gdv}, cp["wstat_free"]), inc="VE")
                    v2 = p.add("vector", lambda e: e.tensor_scalar(
                        wstat[:, 1:2], wstat[:, 0:1], 1.0 / 127.0, None, ALU.mult),
                        wait={"VE": v1}, keep_self=True, inc="VE")
                    dst = sg_all if kind == "g" else su_all
                    p.add("vector", lambda e, dst=dst, idx=idx: e.tensor_copy(
                        dst[:, idx:idx + 1], wstat[:, 1:2]),
                        wait={"VE": v2}, keep_self=True, inc="VE")
                    v3 = p.add("vector", lambda e: e.reciprocal(
                        wstat[:, 2:3], wstat[:, 1:2]),
                        wait={"VE": v2}, keep_self=True, inc="VE")
                    s1 = p.add("vector", lambda e, kb=kb: e.tensor_scalar(
                        wub[:], wfb[kb][:], wstat[:, 2:3], MAGIC, ALU.mult, ALU.add),
                        wait=merge({"VE": v3}, cp["wub_free"]), keep_self=True, inc="VE")
                    cp["wfb_free"][kb] = {"VE": s1}
                    cp["wstat_free"] = {"VE": s1}
                    v4 = p.add("vector", lambda e: e.tensor_scalar(
                        wt1b[:], wub[:], MAGIC, 127.0, ALU.subtract, ALU.min),
                        inc="VE")
                    cp["wub_free"] = {"VE": v4}
                    v6 = p.add("vector", lambda e, kb4=kb4: e.tensor_scalar(
                        whatb[kb4][:], wt1b[:], -128.0, None, ALU.max),
                        wait=cp["what_free"][kb4], inc="VE")
                    cp["what_v6"][k] = v6

                def prep_stage2(tr_psum, tr_free):
                    if cp["s2"] >= cp["s1"]:
                        return
                    k = cp["s2"]
                    cp["s2"] += 1
                    kind, idx = chunks[k]
                    kb = k % 2
                    kb4 = k % 4
                    v6 = cp["what_v6"].pop(k)
                    te_last = None
                    for j in range(8):
                        te_last = p.add("tensor", lambda e, j=j, kb4=kb4:
                                        e.transpose(tr_psum[:, j * 128:(j + 1) * 128],
                                                    whatb[kb4][:, j * 128:(j + 1) * 128],
                                                    ident[:]),
                                        wait=merge({"VE": v6}, tr_free[0])
                                        if j == 0 else None, inc="TE")
                    cp["what_free"][kb4] = {"TE": te_last}
                    se_ev = p.add("scalar", lambda e, kb=kb: e.activation(
                        wstg[kb][:], tr_psum[:, 0:1024], AF.Identity),
                        wait=merge({"TE": te_last}, cp["wstg_free"][kb]), inc="SE")
                    tr_free[0] = {"SE": se_ev}
                    mi = 0 if kind == "g" else 1
                    gdo = p.dma("gpsimd", d_wqT[mi, idx], wstg[kb][:],
                                wait={"SE": se_ev}, sem=f"sWSG{kb}")
                    cp["wstg_free"][kb] = {f"sWSG{kb}": gdo}

                v_pkv0 = p.add("vector", lambda e: e.memset(pkv[:], 0.0), inc="VE")
                v_pks0 = p.add("vector", lambda e: e.memset(pks[:], 0.0), inc="VE")
                xnfm = [xnfm0, xnfm1]
                xnfm_v = [xnfm0[:].rearrange("p (c n) -> p c n", n=128),
                          xnfm1[:].rearrange("p (c n) -> p c n", n=128)]
                ktok = [ktok0, ktok1]
                vtok = [vtok0, vtok1]
                qtok = [qtok0, qtok1]
                xnfm_free = [0, 0]
                ktok_free = [0, 0]
                vtok_free = [0, 0]
                qtok_free = [0, 0]
                bank_free = [0, 0, 0, 0]
                tr6_free = [ve_fold]
                tr7_free = [{}]
                xntok_free = [0]
                prev_v_var = 0
                kv_te_last = 0
                sd_x1_pb = [0] * NT

                def pa_front(tt):
                    bi = tt % 2
                    sdx = p.dma("sync", xb[bi][:], d_x[tt * 128:(tt + 1) * 128, :],
                                wait=xb_free[bi], sem=f"sXB{bi}")
                    s_sum = p.add("scalar", lambda e, bi=bi: e.activation(
                        sq[:], xb[bi][:], AF.Identity, accum_out=stat[:, 0:1]),
                        wait={f"sXB{bi}": sdx, "SE": pa_state["prev_s_var"]}, inc="SE")
                    s_sq = p.add("scalar", lambda e, bi=bi: e.activation(
                        sq[:], xb[bi][:], AF.Square, accum_out=stat[:, 1:2]),
                        inc="SE")
                    s_mean = p.add("scalar", lambda e: e.activation(
                        stat[:, 2:3], stat[:, 0:1], AF.Identity, scale=1.0 / C),
                        wait={"SE": s_sq}, keep_self=True, inc="SE")
                    s_m2 = p.add("scalar", lambda e: e.activation(
                        stat[:, 3:4], stat[:, 2:3], AF.Square),
                        wait={"SE": s_mean}, keep_self=True, inc="SE")
                    s_nm2 = p.add("scalar", lambda e: e.activation(
                        stat[:, 4:5], stat[:, 3:4], AF.Identity, scale=-1.0),
                        wait={"SE": s_m2}, keep_self=True, inc="SE")
                    s_var = p.add("scalar", lambda e: e.activation(
                        stat[:, 5:6], stat[:, 1:2], AF.Identity, scale=1.0 / C,
                        bias=stat[:, 4:5]),
                        wait={"SE": s_nm2}, keep_self=True, inc="SE")
                    s_sd = p.add("scalar", lambda e: e.activation(
                        stat[:, 6:7], stat[:, 5:6], AF.Sqrt, bias=constf[:, 0:1]),
                        wait={"SE": s_var}, keep_self=True, inc="SE")
                    pa_state["prev_s_var"] = s_sd
                    v_r = p.add("vector", lambda e: e.reciprocal(
                        stat[:, 7:8], stat[:, 6:7]), wait={"SE": s_sd}, inc="VE")
                    s_mr = p.add("scalar", lambda e: e.activation(
                        stat[:, 8:9], stat[:, 2:3], AF.Identity, scale=stat[:, 7:8]),
                        wait={"VE": v_r}, inc="SE")
                    s_xn = p.add("vector", lambda e, bi=bi: e.tensor_scalar(
                        xntok[:], xb[bi][:], stat[:, 7:8], stat[:, 8:9],
                        ALU.mult, ALU.subtract),
                        wait={"SE": s_mr, "TE": xntok_free[0]}, inc="VE")
                    xb_free[bi] = {"VE": s_xn}
                    pa_state["s_xn"][tt] = s_xn

                def pa_frontT(tt):
                    bi = tt % 2
                    s_xn = pa_state["s_xn"][tt]
                    te_tr = None
                    for j in range(8):
                        te_tr = p.add("tensor", lambda e, j=j: e.transpose(
                            ptr6[:, j * 128:(j + 1) * 128], xntok[:, j * 128:(j + 1) * 128],
                            ident[:]),
                            wait={"VE": max(s_xn, tr6_free[0])}
                            if j == 0 else None, inc="TE")
                    xntok_free[0] = te_tr
                    v_xnfm = p.add("vector", lambda e, bi=bi: e.tensor_copy(
                        xnfm[bi][:], ptr6[:, 0:1024]),
                        wait={"TE": max(te_tr, xnfm_free[bi])}, inc="VE")
                    tr6_free[0] = v_xnfm
                    pa_state["v_xnfm"][tt] = v_xnfm

                def pa_back(tt):
                    bi = tt % 2
                    v_xnfm = pa_state["v_xnfm"][tt]
                    grp_last_te = 0
                    for grp, bank, kind in ((0, 0, "k"), (1, 1, "k"), (2, 2, "v"),
                                            (3, 3, "v"), (4, 0, "q"), (5, 1, "q")):
                        te_g = None
                        for c in range(CCH):
                            last = (c == CCH - 1)
                            te_g = p.add(
                                "tensor",
                                lambda e, grp=grp, bank=bank, c=c, bi=bi, kind=kind:
                                e.matmul(pabank[bank][:, 0:512], xnfm_v[bi][:, c, :],
                                         Wall_v[:, c, grp * 512:(grp + 1) * 512],
                                         start=(c == 0),
                                         stop=(last and kind == "v")),
                                wait=merge({"VE": max(v_xnfm, ve_fold)},
                                           {"SE": bank_free[bank]},
                                           {"sWALL": gd_v}) if c == 0 else None,
                                inc="TE")
                        if kind != "v":
                            te_g = p.add("tensor", lambda e, grp=grp, bank=bank:
                                         e.matmul(pabank[bank][:, 0:512],
                                                  ones_row[0:1, 0:128],
                                                  bias_row[0:1, grp * 512:(grp + 1) * 512],
                                                  start=False, stop=True),
                                         wait={"VE": ve_consts}, inc="TE")
                        grp_last_te = te_g
                        if kind == "k":
                            se_e = p.add("scalar", lambda e, grp=grp, bank=bank, bi=bi:
                                         e.activation(ktok[bi][:, grp * 512:(grp + 1) * 512],
                                                      pabank[bank][:, 0:512], AF.Relu),
                                         wait={"TE": max(te_g, ktok_free[bi])}, inc="SE")
                        elif kind == "v":
                            se_e = p.add("scalar", lambda e, grp=grp, bank=bank, bi=bi:
                                         e.activation(vtok[bi][:, (grp - 2) * 512:(grp - 1) * 512],
                                                      pabank[bank][:, 0:512], AF.Identity),
                                         wait={"TE": max(te_g, vtok_free[bi])}, inc="SE")
                        else:
                            se_e = p.add("scalar", lambda e, grp=grp, bank=bank, bi=bi:
                                         e.activation(qtok[bi][:, (grp - 4) * 512:(grp - 3) * 512],
                                                      pabank[bank][:, 0:512], AF.Relu),
                                         wait={"TE": max(te_g, qtok_free[bi])}, inc="SE")
                        bank_free[bank] = se_e
                    se_kv_ready = max(bank_free)

                    te_kv = None
                    for hh in range(H):
                        pr = hh // 2
                        po = (hh % 2) * 64
                        te_kv = p.add("tensor", lambda e, hh=hh, pr=pr, po=po, bi=bi, tt=tt:
                                      e.matmul(pkv[po:po + 64, pr * 64:(pr + 1) * 64],
                                               ktok[bi][:, hh * 64:(hh + 1) * 64],
                                               vtok[bi][:, hh * 64:(hh + 1) * 64],
                                               start=False,
                                               stop=(tt == NT - 1 and hh == H - 1),
                                               skip_group_check=True),
                                      wait={"SE": se_kv_ready, "VE": v_pkv0}
                                      if hh == 0 else None, inc="TE")
                        te_kv = p.add("tensor", lambda e, hh=hh, po=po, bi=bi, tt=tt:
                                      e.matmul(pks[po:po + 64, hh:hh + 1],
                                               ktok[bi][:, hh * 64:(hh + 1) * 64],
                                               ones_col[:],
                                               start=False,
                                               stop=(tt == NT - 1 and hh == H - 1),
                                               skip_group_check=True),
                                      inc="TE")
                    ktok_free[bi] = te_kv
                    vtok_free[bi] = te_kv
                    pa_state["kv_te_last"] = te_kv

                    te_qt = None
                    for j in range(8):
                        te_qt = p.add("tensor", lambda e, j=j, bi=bi: e.transpose(
                            ptr7[:, j * 128:(j + 1) * 128], qtok[bi][:, j * 128:(j + 1) * 128],
                            ident[:]),
                            wait=merge({"SE": bank_free[1]}, tr7_free[0])
                            if j == 0 else None, inc="TE")
                    qtok_free[bi] = te_qt
                    v_qfm = p.add("vector", lambda e, tt=tt: e.tensor_copy(
                        bigv[:, :, tt * 128:(tt + 1) * 128], ptr7[:, 0:1024]),
                        wait={"TE": te_qt}, inc="VE")
                    tr7_free[0] = {"VE": v_qfm}
                    xnfm_free[bi] = grp_last_te

                    if tt == 0:
                        if "xnfm" in dbg:
                            p.dma("sync", dbg["xnfm"], xnfm0[:], wait={"VE": v_xnfm}, sem="sDBG")
                        if "stat0" in dbg:
                            p.dma("sync", dbg["stat0"], stat[:], wait={"VE": v_xnfm}, sem="sDBG")
                        if "ktok" in dbg:
                            p.dma("sync", dbg["ktok"], ktok0[:], wait={"SE": bank_free[1]}, sem="sDBG")
                        if "vtok" in dbg:
                            p.dma("sync", dbg["vtok"], vtok0[:], wait={"SE": bank_free[3]}, sem="sDBG")
                        if "qtok" in dbg:
                            p.dma("sync", dbg["qtok"], qtok0[:], wait={"SE": bank_free[1]}, sem="sDBG")

                pa_state = {"prev_s_var": 0, "v_xnfm": {}, "kv_te_last": 0, "s_xn": {}}
                pa_front(0)
                pa_frontT(0)
                for tt in range(NT):
                    if tt + 1 < NT:
                        pa_front(tt + 1)
                    want1 = min(len(chunks), 2 * (tt + 1))
                    while cp["s1"] < want1:
                        prep_stage1()
                    want2 = min(cp["s1"], max(0, 2 * tt))
                    while cp["s2"] < want2:
                        prep_stage2(ptr7, tr7_free)
                    pa_back(tt)
                    if tt + 1 < NT:
                        pa_frontT(tt + 1)
                kv_te_last = pa_state["kv_te_last"]
                while cp["s2"] < len(chunks):
                    prep_stage2(ptr7, tr7_free)
                ve_prep_pa = p.cnt["VE"]
                gd_prep_pa = (p.cnt.get("sWSG0", 0), p.cnt.get("sWSG1", 0))
                se_prep_pa = p.cnt["SE"]

                p.add("vector", lambda e: e.memset(kvx_sb[:], 0.0),
                      wait={"TE": kv_te_last}, inc="VE")
                p.add("vector", lambda e: e.memset(ks_sb[:], 0.0), inc="VE")
                for g in range(8):
                    p.add("vector", lambda e, g=g: e.tensor_copy(
                        kvx_sb[0:64, g * 128: g * 128 + 64],
                        pkv[0:64, g * 64:(g + 1) * 64]), inc="VE")
                    p.add("vector", lambda e, g=g: e.tensor_copy(
                        kvx_sb[64:128, g * 128 + 64: g * 128 + 128],
                        pkv[64:128, g * 64:(g + 1) * 64]), inc="VE")
                    p.add("vector", lambda e, g=g: e.tensor_copy(
                        ks_sb[0:64, g * 16 + 2 * g: g * 16 + 2 * g + 1],
                        pks[0:64, 2 * g:2 * g + 1]), inc="VE")
                    p.add("vector", lambda e, g=g: e.tensor_copy(
                        ks_sb[64:128, g * 16 + 2 * g + 1: g * 16 + 2 * g + 2],
                        pks[64:128, 2 * g + 1:2 * g + 2]), inc="VE")
                ve_kvx = p.cnt["VE"]
        esA.close()

        p.fence()

        esB = ExitStack()
        if True:
            WoutB = esB.enter_context(nc.sbuf_tensor("WoutB", [128, CCH * C], BF16))
            Awin0 = esB.enter_context(nc.sbuf_tensor("Awin0", [128, CCH * 512], BF16))
            Awin1 = esB.enter_context(nc.sbuf_tensor("Awin1", [128, CCH * 512], BF16))
            rdb0 = esB.enter_context(nc.sbuf_tensor("rdb0", [16, 512], BF16))
            rdb1 = esB.enter_context(nc.sbuf_tensor("rdb1", [16, 512], BF16))
            bcb0 = esB.enter_context(nc.sbuf_tensor("bcb0", [128, 512], F32))
            bcb1 = esB.enter_context(nc.sbuf_tensor("bcb1", [128, 512], F32))
            pnrm = esB.enter_context(nc.psum_tensor("pnrm", [128, 512], F32))
            pbc0 = esB.enter_context(nc.psum_tensor("pbc0", [128, 512], F32))
            pat0 = esB.enter_context(nc.psum_tensor("pat0", [128, 512], F32))
            pbc1 = esB.enter_context(nc.psum_tensor("pbc1", [128, 512], F32))
            pat1 = esB.enter_context(nc.psum_tensor("pat1", [128, 512], F32))
            po0 = esB.enter_context(nc.psum_tensor("po0", [128, 512], F32))
            po1 = esB.enter_context(nc.psum_tensor("po1", [128, 512], F32))
            ptrB = esB.enter_context(nc.psum_tensor("ptrB", [128, 1024], BF16))
            wfb_d = esB.enter_context(nc.sbuf_tensor("wfb_d", [128, NI], F32))
            wub_d = esB.enter_context(nc.sbuf_tensor("wub_d", [128, NI // 2], F32))
            wt1_d = esB.enter_context(nc.sbuf_tensor("wt1_d", [128, NI // 2], F32))
            what_d0 = esB.enter_context(nc.sbuf_tensor("what_d0", [128, NI], BF16))
            what_d1 = esB.enter_context(nc.sbuf_tensor("what_d1", [128, NI], BF16))
            wstg_d0 = esB.enter_context(nc.sbuf_tensor("wstg_d0", [128, NI], BF16))
            wstg_d1 = esB.enter_context(nc.sbuf_tensor("wstg_d1", [128, NI], BF16))
            wstat_d = esB.enter_context(nc.sbuf_tensor("wstat_d", [128, 4], F32))
            Wout_v = WoutB[:].rearrange("p (c n) -> p c n", n=C)
            for cc in range(CCH):
                p.dma("gpsimd", Wout_v[:, cc, :], d_wout[cc * 128:(cc + 1) * 128, :],
                      sem="sWOUT")
            gd_wout = p.cnt["sWOUT"]

            Awin = [Awin0, Awin1]
            Awin_v = [Awin0[:].rearrange("p (c n) -> p c n", n=512),
                      Awin1[:].rearrange("p (c n) -> p c n", n=512)]
            rd = [rdb0, rdb1]
            bc = [bcb0, bcb1]
            pbc = [pbc0, pbc1]
            pat = [pat0, pat1]
            Awin_free = [0, 0]
            rd_free = [0, 0]
            bc_free = [0, 0]
            pat_free = [0, 0]
            pbc_free = [0, 0]
            po_free = [0, 0]

            pb_state = {"ve_awin": {}, "prev_v_am": 0}

            def pb_front(ns):
                wi = ns % 2
                te_n = None
                for g in range(8):
                    te_n = p.add("tensor", lambda e, g=g, ns=ns: e.matmul(
                        pnrm[0:16, 0:512], ks_sb[:, g * 16:(g + 1) * 16],
                        bigv[:, g, ns * 512:(ns + 1) * 512],
                        start=(g == 0), stop=False),
                        wait={"VE": max(ve_kvx, pb_state.get("pnrm_free", 0))}
                        if g == 0 else None, inc="TE")
                te_n = p.add("tensor", lambda e: e.matmul(
                    pnrm[0:16, 0:512], eps_row[:], ones_row[0:1, 0:512],
                    start=False, stop=True), inc="TE")
                v_rd = p.add("vector", lambda e, wi=wi: e.reciprocal(
                    rd[wi][:], pnrm[0:16, 0:512]),
                    wait={"TE": max(te_n, rd_free[wi])}, inc="VE")
                pb_state["pnrm_free"] = v_rd

                for g in range(8):
                    pi = g % 2
                    te_bc = p.add("tensor", lambda e, g=g, pi=pi, wi=wi: e.matmul(
                        pbc[pi][:, 0:512], ind_sb[:, g * 128:(g + 1) * 128], rd[wi][:],
                        start=True, stop=True),
                        wait={"VE": v_rd, "SE": pbc_free[pi]}, inc="TE")
                    rd_free[wi] = te_bc
                    te_at = p.add("tensor", lambda e, g=g, pi=pi, ns=ns: e.matmul(
                        pat[pi][:, 0:512], kvx_sb[:, g * 128:(g + 1) * 128],
                        bigv[:, g, ns * 512:(ns + 1) * 512],
                        start=True, stop=True),
                        wait={"VE": pat_free[pi]}, inc="TE")
                    se_bc = p.add("scalar", lambda e, pi=pi: e.activation(
                        bc[pi][:], pbc[pi][:, 0:512], AF.Identity),
                        wait={"TE": te_bc, "VE": bc_free[pi]}, inc="SE")
                    pbc_free[pi] = se_bc
                    v_a = p.add("vector", lambda e, g=g, pi=pi, wi=wi: e.tensor_tensor(
                        Awin_v[wi][:, g, :], pat[pi][:, 0:512], bc[pi][:], ALU.mult),
                        wait={"TE": max(te_at, Awin_free[wi]), "SE": se_bc}, inc="VE")
                    pat_free[pi] = v_a
                    bc_free[pi] = v_a
                pb_state["ve_awin"][ns] = p.cnt["VE"]

            def pb_back(ns):
                wi = ns % 2
                ve_awin = pb_state["ve_awin"][ns]
                for tl in range(4):
                    tt = ns * 4 + tl
                    bi = tt % 2
                    sdx = p.dma("sync", xb[bi][:], d_x[tt * 128:(tt + 1) * 128, :],
                                wait=xb_free[bi], sem=f"sXB{bi}")
                    te_o = None
                    for c in range(CCH):
                        p.add("tensor", lambda e, c=c, tl=tl, wi=wi: e.matmul(
                            po0[:, 0:512], Awin_v[wi][:, c, tl * 128:(tl + 1) * 128],
                            Wout_v[:, c, 0:512], start=(c == 0), stop=False),
                            wait={"VE": max(ve_awin, po_free[0]), "sWOUT": gd_wout}
                            if c == 0 else None, inc="TE")
                        te_o = p.add("tensor", lambda e, c=c, tl=tl, wi=wi: e.matmul(
                            po1[:, 0:512], Awin_v[wi][:, c, tl * 128:(tl + 1) * 128],
                            Wout_v[:, c, 512:1024], start=(c == 0), stop=False),
                            wait={"VE": po_free[1]} if c == 0 else None, inc="TE")
                    p.add("tensor", lambda e: e.matmul(
                        po0[:, 0:512], ones_row[0:1, 0:128], bout_row[0:1, 0:512],
                        start=False, stop=True), wait={"sCONST2": gd_consts}, inc="TE")
                    te_o = p.add("tensor", lambda e: e.matmul(
                        po1[:, 0:512], ones_row[0:1, 0:128], bout_row[0:1, 512:1024],
                        start=False, stop=True), inc="TE")
                    if tl == 3:
                        Awin_free[wi] = te_o
                    v_x1a = p.add("vector", lambda e, bi=bi: e.scalar_tensor_tensor(
                        x1b[bi][:, 0:512], po0[:, 0:512], 1.0, xb[bi][:, 0:512],
                        ALU.mult, ALU.add),
                        wait=merge({"TE": te_o, f"sXB{bi}": sdx}, x1b_free[bi]), inc="VE")
                    v_x1b = p.add("vector", lambda e, bi=bi: e.scalar_tensor_tensor(
                        x1b[bi][:, 512:1024], po1[:, 0:512], 1.0, xb[bi][:, 512:1024],
                        ALU.mult, ALU.add), inc="VE")
                    po_free[0] = v_x1b
                    po_free[1] = v_x1b
                    xb_free[bi] = {"VE": v_x1b}
                    sd_x1 = p.dma("sync", d_x1[tt * 128:(tt + 1) * 128, :], x1b[bi][:],
                                  wait={"VE": v_x1b}, sem=f"sX1W{bi}")
                    sd_x1_pb[tt] = {f"sX1W{bi}": sd_x1}
                    x1b_free[bi] = {f"sX1W{bi}": sd_x1}
                    s_sum = p.add("scalar", lambda e, bi=bi: e.activation(
                        sq[:], x1b[bi][:], AF.Identity, accum_out=stat[:, 0:1]),
                        wait={"VE": max(v_x1b, pb_state["prev_v_am"])}, inc="SE")
                    s_sq2 = p.add("scalar", lambda e, bi=bi: e.activation(
                        sq[:], x1b[bi][:], AF.Square, accum_out=stat[:, 1:2]), inc="SE")
                    s_mean = p.add("scalar", lambda e: e.activation(
                        stat[:, 2:3], stat[:, 0:1], AF.Identity, scale=1.0 / C),
                        wait={"SE": s_sq2}, keep_self=True, inc="SE")
                    s_m2 = p.add("scalar", lambda e: e.activation(
                        stat[:, 3:4], stat[:, 2:3], AF.Square),
                        wait={"SE": s_mean}, keep_self=True, inc="SE")
                    s_nm2 = p.add("scalar", lambda e: e.activation(
                        stat[:, 4:5], stat[:, 3:4], AF.Identity, scale=-1.0),
                        wait={"SE": s_m2}, keep_self=True, inc="SE")
                    s_var = p.add("scalar", lambda e: e.activation(
                        stat[:, 5:6], stat[:, 1:2], AF.Identity, scale=1.0 / C,
                        bias=stat[:, 4:5]),
                        wait={"SE": s_nm2}, keep_self=True, inc="SE")
                    s_sd2 = p.add("scalar", lambda e: e.activation(
                        stat[:, 6:7], stat[:, 5:6], AF.Sqrt, bias=constf[:, 0:1]),
                        wait={"SE": s_var}, keep_self=True, inc="SE")
                    v_r2 = p.add("vector", lambda e, tt=tt: e.reciprocal(
                        r2_all[:, tt:tt + 1], stat[:, 6:7]), wait={"SE": s_sd2}, inc="VE")
                    v_nmr2 = p.add("vector", lambda e, tt=tt: e.tensor_scalar(
                        nmr2_all[:, tt:tt + 1], stat[:, 2:3], r2_all[:, tt:tt + 1], -1.0,
                        ALU.mult, ALU.mult),
                        wait={"VE": v_r2}, keep_self=True, inc="VE")
                    s_u = p.add("scalar", lambda e, bi=bi, tt=tt: e.activation(
                        sq[:], x1b[bi][:], AF.Identity, bias=nmr2_all[:, tt:tt + 1],
                        scale=r2_all[:, tt:tt + 1]),
                        wait={"VE": v_nmr2}, inc="SE")
                    v_am = p.add("vector", lambda e: e.reduce_max(
                        stat[:, 8:9], sq[:], AX.X, apply_absolute_value=True),
                        wait={"SE": s_u}, inc="VE")
                    amo, amn = 10 + (tt % 2), 10 + ((tt + 1) % 2)
                    pb_state["prev_v_am"] = p.add(
                        "vector", lambda e, amo=amo, amn=amn: e.tensor_scalar(
                            stat[:, amn:amn + 1], stat[:, 8:9], stat[:, amo:amo + 1], None,
                            ALU.max), wait={"VE": v_am}, keep_self=True, inc="VE")

            what_d = [what_d0, what_d1]
            wstg_d = [wstg_d0, wstg_d1]
            dp = {"s1": 0, "s2": 0, "wfb_free": {}, "wub_free": {}, "wstat_free": {},
                  "what_free": [{}, {}], "wstg_free": [{}, {}], "v6": {},
                  "trB_free": {}}

            def dprep_stage1():
                if dp["s1"] >= CCH:
                    return
                j = dp["s1"]
                dp["s1"] += 1
                kb = j % 2
                gdv = p.dma("gpsimd", wfb_d[:], d_wd[j * 128:(j + 1) * 128, :],
                            wait=dp["wfb_free"], sem="sDL")
                v1 = p.add("vector", lambda e: e.reduce_max(
                    wstat_d[:, 0:1], wfb_d[:], AX.X, apply_absolute_value=True),
                    wait=merge({"sDL": gdv}, dp["wstat_free"]), inc="VE")
                v2 = p.add("vector", lambda e: e.tensor_scalar(
                    wstat_d[:, 1:2], wstat_d[:, 0:1], 1.0 / 127.0, None, ALU.mult),
                    wait={"VE": v1}, keep_self=True, inc="VE")
                v3 = p.add("vector", lambda e: e.reciprocal(
                    wstat_d[:, 2:3], wstat_d[:, 1:2]),
                    wait={"VE": v2}, keep_self=True, inc="VE")
                s1v = None
                v6 = None
                for p0 in range(0, NI, NI // 2):
                    w = NI // 2
                    s1v = p.add("vector", lambda e, p0=p0, w=w: e.tensor_scalar(
                        wub_d[:, 0:w], wfb_d[:, p0:p0 + w], wstat_d[:, 2:3], MAGIC,
                        ALU.mult, ALU.add),
                        wait=merge({"VE": v3}, dp["wub_free"]), keep_self=True, inc="VE")
                    v4 = p.add("vector", lambda e, w=w: e.tensor_scalar(
                        wt1_d[:, 0:w], wub_d[:, 0:w], MAGIC, 127.0,
                        ALU.subtract, ALU.min), inc="VE")
                    p.add("vector", lambda e, w=w: e.tensor_scalar(
                        wub_d[:, 0:w], wt1_d[:, 0:w], -128.0, None, ALU.max), inc="VE")
                    v6 = p.add("vector", lambda e, kb=kb, p0=p0, w=w: e.tensor_scalar(
                        what_d[kb][:, p0:p0 + w], wub_d[:, 0:w], wstat_d[:, 1:2], None,
                        ALU.mult), wait=dp["what_free"][kb], inc="VE")
                    dp["wub_free"] = {"VE": v6}
                dp["wfb_free"] = {"VE": s1v}
                dp["wstat_free"] = {"VE": s1v}
                dp["v6"][j] = v6

            def dprep_stage2():
                if dp["s2"] >= dp["s1"]:
                    return
                j = dp["s2"]
                dp["s2"] += 1
                kb = j % 2
                v6 = dp["v6"].pop(j)
                te_last = None
                for r0 in range(0, NIC, 8):
                    rblks = min(8, NIC - r0)
                    for jj in range(rblks):
                        te_last = p.add("tensor", lambda e, r0=r0, jj=jj, kb=kb:
                                        e.transpose(ptrB[:, jj * 128:(jj + 1) * 128],
                                                    what_d[kb][:, (r0 + jj) * 128:(r0 + jj + 1) * 128],
                                                    ident[:]),
                                        wait=merge({"VE": v6}, dp["trB_free"])
                                        if jj == 0 else None, inc="TE")
                    se_ev = p.add("scalar", lambda e, r0=r0, rblks=rblks, kb=kb:
                                  e.activation(wstg_d[kb][:, r0 * 128:(r0 + rblks) * 128],
                                               ptrB[:, 0:rblks * 128], AF.Identity),
                                  wait=merge({"TE": te_last}, dp["wstg_free"][kb]),
                                  inc="SE")
                    dp["trB_free"] = {"SE": se_ev}
                dp["what_free"][kb] = {"TE": te_last}
                gdo = p.dma("gpsimd", d_wdT[:, j].transpose([1, 0, 2]),
                            wstg_d[kb][:].rearrange("p (i n) -> p i n", n=128),
                            wait=dp["trB_free"], sem=f"sDS{kb}")
                dp["wstg_free"][kb] = {f"sDS{kb}": gdo}

            pb_front(0)
            for ns in range(8):
                if ns + 1 < 8:
                    pb_front(ns + 1)
                dprep_stage1()
                pb_back(ns)
                dprep_stage2()
            while dp["s2"] < CCH:
                dprep_stage2()
            gd_prep_pb = (p.cnt.get("sDS0", 0), p.cnt.get("sDS1", 0))
            ve_pb = p.cnt["VE"]
            sd_pb = dict(p.cnt)

            if "x1" in dbg:
                for t2 in range(NT):
                    p.dma("sync", dbg["x1"][t2 * 128:(t2 + 1) * 128, :],
                          d_x1[t2 * 128:(t2 + 1) * 128, :],
                          wait={"sX1W0": p.cnt.get("sX1W0", 0),
                                "sX1W1": p.cnt.get("sX1W1", 0)}, sem="sDBG")
            if "q_fm" in dbg:
                p.dma("sync", dbg["q_fm"], big[:], wait={"VE": ve_pb}, sem="sDBG")
            if "kvx" in dbg:
                p.dma("sync", dbg["kvx"], kvx_sb[:], wait={"VE": ve_pb}, sem="sDBG")
            if "ks" in dbg:
                p.dma("sync", dbg["ks"], ks_sb[:], wait={"VE": ve_pb}, sem="sDBG")
        esB.close()

        def barrier(acc_ap, cc_in, cc_out, scr, s_col, rs_col, clip_sb, wait_d):
            sd1 = p.dma("sync", cc_in, acc_ap, wait=wait_d, sem="sBAR")
            ge = p.add("gpsimd", lambda e, ci=cc_in, co=cc_out: e.collective_compute(
                "AllReduce", ALU.max, replica_groups=[list(range(N_CORES))],
                ins=[ci], outs=[co]),
                wait={"sBAR": sd1}, inc="GE")
            sd2 = p.dma("sync", grow[:], cc_out.rearrange("a one -> one a"),
                        wait={"GE": ge}, sem="sBAR")
            v1 = p.add("vector", lambda e: e.reduce_max(gcol[:], grow[:], AX.X),
                       wait={"sBAR": sd2}, inc="VE")
            v2 = p.add("vector", lambda e: e.tensor_scalar(
                grow[:], ones_f32[:], gcol[0:1, 0:1], None, ALU.mult),
                wait={"VE": v1}, keep_self=True, inc="VE")
            sd3 = p.dma("sync", scr.rearrange("a one -> one a"), grow[:],
                        wait={"VE": v2}, sem="sBAR")
            sd4 = p.dma("sync", gbc[:], scr, wait={"sBAR": sd3}, sem="sBAR")
            s_sig = p.add("scalar", lambda e, cs=clip_sb: e.activation(
                sig_col[:], cs[:], AF.Sigmoid),
                wait={"sBAR": sd4, "sCONST": sd_consts}, inc="SE")
            v3 = p.add("vector", lambda e, s=s_col: e.scalar_tensor_tensor(
                s[:], sig_col[:], 1.0 / 127.0, gbc[:], ALU.mult, ALU.mult),
                wait={"SE": s_sig, "sBAR": sd4}, inc="VE")
            v4 = p.add("vector", lambda e, s=s_col, r=rs_col: e.reciprocal(r[:], s[:]),
                       wait={"VE": v3}, keep_self=True, inc="VE")
            return v4

        v_b1 = barrier(stat[:, 10:11], d_cc1_in, d_cc1_out, d_s128a, s1_col, rs1_col,
                       clipx_sb, {"VE": ve_pb})
        p.add("vector", lambda e: e.tensor_scalar(
            uscale_all[:], r2_all[:], rs1_col[:], None, ALU.mult),
            wait={"VE": v_b1}, keep_self=True, inc="VE")
        p.add("vector", lambda e: e.tensor_scalar(
            ubias_all[:], nmr2_all[:], rs1_col[:], None, ALU.mult), inc="VE")
        p.add("vector", lambda e: e.tensor_scalar(
            s1sg_all[:], sg_all[:], s1_col[:], None, ALU.mult),
            wait={"VE": ve_prep_pa}, inc="VE")
        v_b1e = p.add("vector", lambda e: e.tensor_scalar(
            s1su_all[:], su_all[:], s1_col[:], None, ALU.mult), inc="VE")
        if "s1" in dbg:
            p.dma("sync", dbg["s1"], s1_col[:], wait={"VE": v_b1e}, sem="sDBG")

        p.fence()

        esC = ExitStack()
        if True:
            wqTsb = esC.enter_context(nc.sbuf_tensor("wqTsb", [128, 2 * NIC * C], BF16))
            ubuf = esC.enter_context(nc.sbuf_tensor("ubuf", [128, C], F32))
            t1buf = esC.enter_context(nc.sbuf_tensor("t1buf", [128, C], F32))
            xqtok = esC.enter_context(nc.sbuf_tensor("xqtok", [128, C], BF16))
            sgb0 = esC.enter_context(nc.sbuf_tensor("sgb0", [128, 512], BF16))
            sgb1 = esC.enter_context(nc.sbuf_tensor("sgb1", [128, 512], BF16))
            hb0 = esC.enter_context(nc.sbuf_tensor("hb0", [128, 512], F32))
            hb1 = esC.enter_context(nc.sbuf_tensor("hb1", [128, 512], F32))
            hb2 = esC.enter_context(nc.sbuf_tensor("hb2", [128, 512], F32))
            pg0 = esC.enter_context(nc.psum_tensor("pg0", [128, 512], F32))
            pu0 = esC.enter_context(nc.psum_tensor("pu0", [128, 512], F32))
            pg1 = esC.enter_context(nc.psum_tensor("pg1", [128, 512], F32))
            pu1 = esC.enter_context(nc.psum_tensor("pu1", [128, 512], F32))
            pg2 = esC.enter_context(nc.psum_tensor("pg2", [128, 512], F32))
            pu2 = esC.enter_context(nc.psum_tensor("pu2", [128, 512], F32))
            ptrC = esC.enter_context(nc.psum_tensor("ptrC", [128, 1024], BF16))
            wqT_v = wqTsb[:].rearrange("p (m i c) -> p m i c", m=2, i=NIC)
            for mi in range(2):
                for ii in range(NIC):
                    p.dma("gpsimd", wqT_v[:, mi, ii, :], d_wqT[mi, ii],
                          wait={"sWSG0": gd_prep_pa[0], "sWSG1": gd_prep_pa[1]},
                          sem="sWQT")
            gd_wqT = p.cnt["sWQT"]

            xq_evac_ve = [0] * NT
            x1_free = [dict(x1b_free[0]), dict(x1b_free[1])]
            trC_free = [0]
            ubuf_free = [0]
            xqtok_free2 = [0]

            def emit_xq_tile(tt):
                bi = tt % 2
                sdx = p.dma("sync", x1b[bi][:], d_x1[tt * 128:(tt + 1) * 128, :],
                            wait=merge(x1_free[bi], sd_x1_pb[tt]), sem=f"sX1R{bi}")
                s_u = p.add("scalar", lambda e, bi=bi, tt=tt: e.activation(
                    ubuf[:], x1b[bi][:], AF.Identity, bias=ubias_all[:, tt:tt + 1],
                    scale=uscale_all[:, tt:tt + 1]),
                    wait={f"sX1R{bi}": sdx, "VE": max(v_b1e, ubuf_free[0])}, inc="SE")
                v_t1 = p.add("vector", lambda e: e.tensor_scalar(
                    t1buf[:], ubuf[:], MAGIC, MAGIC, ALU.add, ALU.subtract),
                    wait={"SE": s_u}, inc="VE")
                ubuf_free[0] = v_t1
                v_xq = p.add("vector", lambda e: e.tensor_scalar(
                    xqtok[:], t1buf[:], 127.0, -128.0, ALU.min, ALU.max),
                    wait={"TE": xqtok_free2[0]}, inc="VE")
                x1_free[bi] = {"VE": v_xq}
                te_tr = None
                for j in range(8):
                    te_tr = p.add("tensor", lambda e, j=j: e.transpose(
                        ptrC[:, j * 128:(j + 1) * 128], xqtok[:, j * 128:(j + 1) * 128],
                        ident[:]),
                        wait={"VE": max(v_xq, trC_free[0])} if j == 0 else None, inc="TE")
                xqtok_free2[0] = te_tr
                v_ev = p.add("vector", lambda e, tt=tt: e.tensor_copy(
                    bigv[:, :, tt * 128:(tt + 1) * 128], ptrC[:, 0:1024]),
                    wait={"TE": te_tr}, inc="VE")
                trC_free[0] = v_ev
                xq_evac_ve[tt] = v_ev

            for tt in range(8):
                emit_xq_tile(tt)

            pg = [pg0, pg1, pg2]
            pu = [pu0, pu1, pu2]
            sgb = [sgb0, sgb1]
            hb = [hb0, hb1, hb2]
            pg_free = [0, 0, 0]
            pu_free = [0, 0, 0]
            sg_free = [0, 0]
            hb_free = [{}, {}, {}]
            prev_hm = 0
            hi = 0

            for ns in range(8):
                for t3 in range(8 + ns * 3, min(NT, 8 + (ns + 1) * 3)):
                    emit_xq_tile(t3)
                need_ve = xq_evac_ve[4 * ns + 3]
                for ii in range(NIC):
                    pi = (ns * NIC + ii) % 3
                    te_g = None
                    for hbk in range(CCH):
                        te_g = p.add("tensor", lambda e, ii=ii, hbk=hbk, ns=ns, pi=pi:
                                     e.matmul(pg[pi][:, 0:512],
                                              wqT_v[:, 0, ii, hbk * 128:(hbk + 1) * 128],
                                              bigv[:, hbk, ns * 512:(ns + 1) * 512],
                                              start=(hbk == 0), stop=(hbk == CCH - 1)),
                                     wait={"VE": need_ve, "sWQT": gd_wqT, "SE": pg_free[pi]}
                                     if hbk == 0 else None, inc="TE")
                    te_u = None
                    for hbk in range(CCH):
                        te_u = p.add("tensor", lambda e, ii=ii, hbk=hbk, ns=ns, pi=pi:
                                     e.matmul(pu[pi][:, 0:512],
                                              wqT_v[:, 1, ii, hbk * 128:(hbk + 1) * 128],
                                              bigv[:, hbk, ns * 512:(ns + 1) * 512],
                                              start=(hbk == 0), stop=(hbk == CCH - 1)),
                                     wait={"VE": pu_free[pi]} if hbk == 0 else None,
                                     inc="TE")
                    si = (ns * NIC + ii) % 2
                    s_sg = p.add("scalar", lambda e, ii=ii, pi=pi, si=si: e.activation(
                        sgb[si][:], pg[pi][:, 0:512], AF.Silu,
                        scale=s1sg_all[:, ii:ii + 1]),
                        wait={"TE": te_g, "VE": max(v_b1e, sg_free[si])}, inc="SE")
                    pg_free[pi] = s_sg
                    hbi = hi % 3
                    hi += 1
                    v_h = p.add("vector", lambda e, ii=ii, pi=pi, si=si, hbi=hbi:
                                e.scalar_tensor_tensor(hb[hbi][:], pu[pi][:, 0:512],
                                                       s1su_all[:, ii:ii + 1], sgb[si][:],
                                                       ALU.mult, ALU.mult),
                                wait=merge({"TE": te_u, "SE": s_sg}, hb_free[hbi]),
                                inc="VE")
                    pu_free[pi] = v_h
                    sg_free[si] = v_h
                    v_hm = p.add("vector", lambda e, hbi=hbi: e.reduce_max(
                        stat[:, 9:10], hb[hbi][:], AX.X, apply_absolute_value=True),
                        wait={"VE": prev_hm}, inc="VE")
                    kk = ns * NIC + ii
                    hmo, hmn = 12 + (kk % 2), 12 + ((kk + 1) % 2)
                    prev_hm = p.add("vector", lambda e, hmo=hmo, hmn=hmn: e.tensor_scalar(
                        stat[:, hmn:hmn + 1], stat[:, 9:10], stat[:, hmo:hmo + 1], None,
                        ALU.max), wait={"VE": v_hm}, keep_self=True, inc="VE")
                    sd_h = p.dma("sync", d_h[ii][:, ns * 512:(ns + 1) * 512], hb[hbi][:],
                                 wait={"VE": v_h}, sem=f"sHW{hbi}")
                    hb_free[hbi] = {f"sHW{hbi}": sd_h}
            ve_pc = p.cnt["VE"]
            sd_pc = {f"sHW{k}": p.cnt.get(f"sHW{k}", 0) for k in range(3)}
            if "xq" in dbg:
                p.dma("sync", dbg["xq"], big[:], wait={"VE": ve_pc}, sem="sDBG")
            if "h0" in dbg:
                p.dma("sync", dbg["h0"], d_h[0], wait=sd_pc, sem="sDBG")
        esC.close()

        if "hmax" in dbg:
            p.dma("sync", dbg["hmax"], stat[:, 12:13], wait={"VE": ve_pc}, sem="sDBG")
        if "hall" in dbg:
            for ci2 in range(NIC):
                p.dma("sync", dbg["hall"][ci2 * 128:(ci2 + 1) * 128, :], d_h[ci2],
                      wait=sd_pc, sem="sDBG")
        v_b2 = barrier(stat[:, 12:13], d_cc2_in, d_cc2_out, d_s128b, s2_col, rs2_col,
                       clipd_sb, {"VE": ve_pc})
        if "s2" in dbg:
            p.dma("sync", dbg["s2"], s2_col[:], wait={"VE": v_b2}, sem="sDBG")

        esBig.close()
        p.fence()

        esD = ExitStack()
        if True:
            wdTsb = esD.enter_context(nc.sbuf_tensor("wdTsb", [128, NIC * C], BF16))
            hqs0 = esD.enter_context(nc.sbuf_tensor("hqs0", [128, NIC * 512], BF16))
            hqs1 = esD.enter_context(nc.sbuf_tensor("hqs1", [128, NIC * 512], BF16))
            hch0 = esD.enter_context(nc.sbuf_tensor("hch0", [128, 512], F32))
            hch1 = esD.enter_context(nc.sbuf_tensor("hch1", [128, 512], F32))
            hch2 = esD.enter_context(nc.sbuf_tensor("hch2", [128, 512], F32))
            uch0 = esD.enter_context(nc.sbuf_tensor("uch0", [128, 512], F32))
            uch1 = esD.enter_context(nc.sbuf_tensor("uch1", [128, 512], F32))
            tch0 = esD.enter_context(nc.sbuf_tensor("tch0", [128, 512], F32))
            tch1 = esD.enter_context(nc.sbuf_tensor("tch1", [128, 512], F32))
            outb0 = esD.enter_context(nc.sbuf_tensor("outb0", [128, C], F32))
            outb1 = esD.enter_context(nc.sbuf_tensor("outb1", [128, C], F32))
            pd0 = esD.enter_context(nc.psum_tensor("pd0", [128, 512], F32))
            pd1 = esD.enter_context(nc.psum_tensor("pd1", [128, 512], F32))
            pd2 = esD.enter_context(nc.psum_tensor("pd2", [128, 512], F32))
            pd3 = esD.enter_context(nc.psum_tensor("pd3", [128, 512], F32))

            wdT_v = wdTsb[:].rearrange("p (i c) -> p i c", i=NIC)
            wdT_ld = []
            for ci in range(NIC):
                v = p.dma("gpsimd", wdT_v[:, ci, :].rearrange("p (c b) -> p c b", b=128),
                          d_wdT[ci].transpose([1, 0, 2]),
                          wait={"sDS0": gd_prep_pb[0], "sDS1": gd_prep_pb[1]},
                          sem=f"sWDT{ci}")
                wdT_ld.append(v)

            hqs = [hqs0, hqs1]
            hqsv = [h[:].rearrange("p (i n) -> p i n", n=512) for h in hqs]
            hch = [hch0, hch1, hch2]
            uch = [uch0, uch1]
            tch = [tch0, tch1]
            outb = [outb0, outb1]
            pdk = [(pd0, pd1), (pd2, pd3)]
            hch_free = [{}, {}, {}]
            uch_free = [{}, {}]
            tch_free = [{}, {}]
            hqs_free = [{}, {}]
            pd_free = [0, 0]
            outb_free = [0, 0]
            sd_out_last = None
            v_hq_last_arr = [0] * 8

            def pd_quant(ns):
                si = ns % 2
                v_hq_last = None
                for ci in range(NIC):
                    k = ns * NIC + ci
                    h3 = k % 3
                    u2 = k % 2
                    sdh = p.dma("sync", hch[h3][:],
                                d_h[ci][:, ns * 512:(ns + 1) * 512],
                                wait=merge(hch_free[h3], sd_pc if k < 3 else None),
                                sem=f"sHR{h3}")
                    if k == 0 and "pdu" in dbg:
                        sd_d1 = p.dma("sync", dbg["pdu"][:, 0:512], hch[h3][:],
                                      wait={f"sHR{h3}": sdh}, sem="sDBG")
                        hch_free[h3] = merge(hch_free[h3], {"sDBG": sd_d1})
                    s_u = p.add("scalar", lambda e, h3=h3, u2=u2: e.activation(
                        uch[u2][:], hch[h3][:], AF.Identity, bias=constf[:, 1:2],
                        scale=rs2_col[:]),
                        wait=merge({f"sHR{h3}": sdh, "VE": v_b2}, uch_free[u2]), inc="SE")
                    hch_free[h3] = merge(hch_free[h3], {"SE": s_u})
                    if k == 0 and "pdu" in dbg:
                        sd_d2 = p.dma("sync", dbg["pdu"][:, 512:1024], uch[u2][:],
                                      wait={"SE": s_u}, sem="sDBG")
                        uch_free[u2] = merge(uch_free[u2], {"sDBG": sd_d2})
                    v_t1 = p.add("vector", lambda e, u2=u2: e.tensor_scalar(
                        tch[u2][:], uch[u2][:], MAGIC, 127.0, ALU.subtract, ALU.min),
                        wait=merge({"SE": s_u}, tch_free[u2]), inc="VE")
                    uch_free[u2] = {"VE": v_t1}
                    if k == 0 and "pdu" in dbg:
                        sd_d3 = p.dma("sync", dbg["pdu"][:, 1024:1536], tch[u2][:],
                                      wait={"VE": v_t1}, sem="sDBG")
                        tch_free[u2] = merge(tch_free[u2], {"sDBG": sd_d3})
                    v_hq = p.add("vector", lambda e, si=si, ci=ci, u2=u2:
                        e.tensor_scalar(hqsv[si][:, ci, :], tch[u2][:], -128.0, None,
                                        ALU.max),
                        wait=merge({"VE": v_t1}, hqs_free[si] if ci == 0 else None),
                        keep_self=True, inc="VE")
                    tch_free[u2] = {"VE": v_hq}
                    v_hq_last = v_hq
                v_hq_last_arr[ns] = v_hq_last
                if ns == 0 and "hqs" in dbg:
                    sd_hqd = p.dma("sync", dbg["hqs"], hqs0[:], wait={"VE": v_hq_last},
                                   sem="sDBG")
                    hqs_free[0] = merge(hqs_free[0], {"sDBG": sd_hqd})

            def pd_mms(ns):
                si = ns % 2
                v_hq_last = v_hq_last_arr[ns]
                if ns == 0 and "wdts" in dbg:
                    p.dma("sync", dbg["wdts"], wdTsb[:],
                          wait={f"sWDT{c2}": wdT_ld[c2] for c2 in range(NIC)}, sem="sDBG")
                for tl in range(4):
                    tt = ns * 4 + tl
                    bi = tt % 2
                    sdx = p.dma("gpsimd", x1b[bi][:], d_x1[tt * 128:(tt + 1) * 128, :],
                                wait=merge(x1_free[bi], sd_x1_pb[tt]), sem=f"sX1R{bi}")
                    pdi = tt % 2
                    pda, pdb = pdk[pdi]
                    te_d = None
                    for ci in range(NIC):
                        p.add("tensor", lambda e, ci=ci, si=si, tl=tl, pda=pda: e.matmul(
                            pda[:, 0:512], hqsv[si][:, ci, tl * 128:(tl + 1) * 128],
                            wdT_v[:, ci, 0:512], start=(ci == 0), stop=(ci == NIC - 1)),
                            wait=merge({"VE": max(v_hq_last, pd_free[pdi])}
                                       if ci == 0 else None,
                                       {f"sWDT{ci}": wdT_ld[ci]}), inc="TE")
                        te_d = p.add("tensor", lambda e, ci=ci, si=si, tl=tl, pdb=pdb: e.matmul(
                            pdb[:, 0:512], hqsv[si][:, ci, tl * 128:(tl + 1) * 128],
                            wdT_v[:, ci, 512:1024], start=(ci == 0), stop=(ci == NIC - 1)),
                            inc="TE")
                    if tl == 3:
                        hqs_free[si] = merge({}, {"TE": te_d})
                    v_o1 = p.add("vector", lambda e, bi=bi, pda=pda: e.scalar_tensor_tensor(
                        outb[bi][:, 0:512], pda[:, 0:512], s2_col[:], x1b[bi][:, 0:512],
                        ALU.mult, ALU.add),
                        wait=merge({"TE": te_d, f"sX1R{bi}": sdx},
                                   {f"sOUT{bi}": outb_free[bi]}), inc="VE")
                    v_o2 = p.add("vector", lambda e, bi=bi, pdb=pdb: e.scalar_tensor_tensor(
                        outb[bi][:, 512:1024], pdb[:, 0:512], x1b[bi]
                        if False else s2_col[:], x1b[bi][:, 512:1024],
                        ALU.mult, ALU.add), inc="VE")
                    pd_free[pdi] = v_o2
                    x1_free[bi] = {"VE": v_o2}
                    sd_o = p.dma("sync", d_out[tt * 128:(tt + 1) * 128, :], outb[bi][:],
                                 wait={"VE": v_o2}, sem=f"sOUT{bi}")
                    outb_free[bi] = sd_o

            pd_quant(0)
            for ns in range(8):
                if ns + 1 < 8:
                    pd_quant(ns + 1)
                pd_mms(ns)

            p.add("sync", lambda e: None,
                  wait={"sOUT0": p.cnt["sOUT0"], "sOUT1": p.cnt["sOUT1"]})
        esD.close()

        with nc.allow_low_precision(reason="bf16 matmul operands / quantized codes by design; fp32 accumulation everywhere"):
            p.flush(blk)
    return nc


_NC_CACHE = {}


def _consts():
    ident = np.eye(128, dtype=np.float32).astype(BF)
    ones_col = np.ones((128, 1), np.float32).astype(BF)
    ones_row = np.ones((1, 1024), np.float32).astype(BF)
    ones_f32 = np.ones((1, 128), np.float32)
    eps_row = np.full((1, 16), 1e-6, np.float32).astype(BF)
    ind = np.zeros((8, 16, 128), np.float32)
    for g in range(8):
        ind[g, 2 * g, 0:64] = 1.0
        ind[g, 2 * g + 1, 64:128] = 1.0
    return ident, ones_col, ones_row, ones_f32, eps_row, ind.astype(BF)


def kernel(L0_attn_x, L0_attn_W_qkv, L0_attn_W_f, L0_attn_b_f, L0_attn_W_out,
           L0_attn_b_out, L0_ffn_clip_x, L0_ffn_clip_down, L0_ffn_w_gate,
           L0_ffn_w_up, L0_ffn_w_down):
    x = np.ascontiguousarray(np.asarray(L0_attn_x, dtype=np.float32))
    ident, ones_col, ones_row, ones_f32, eps_row, ind = _consts()
    shared = {
        "wqkv": np.ascontiguousarray(np.asarray(L0_attn_W_qkv, np.float32)),
        "wf": np.ascontiguousarray(np.asarray(L0_attn_W_f, np.float32)),
        "bfv": np.ascontiguousarray(np.asarray(L0_attn_b_f, np.float32).reshape(1, DH)),
        "wout": np.ascontiguousarray(np.asarray(L0_attn_W_out, np.float32)),
        "bout": np.ascontiguousarray(np.asarray(L0_attn_b_out, np.float32).reshape(1, C)),
        "clipx": np.ascontiguousarray(np.broadcast_to(
            np.asarray(L0_ffn_clip_x, np.float32).reshape(1, 1), (128, 1))),
        "clipd": np.ascontiguousarray(np.broadcast_to(
            np.asarray(L0_ffn_clip_down, np.float32).reshape(1, 1), (128, 1))),
        "wg": np.ascontiguousarray(np.asarray(L0_ffn_w_gate, np.float32)),
        "wu": np.ascontiguousarray(np.asarray(L0_ffn_w_up, np.float32)),
        "wd": np.ascontiguousarray(np.asarray(L0_ffn_w_down, np.float32)),
        "ident": ident, "ones_col": ones_col, "ones_row": ones_row,
        "ones_f32": ones_f32, "eps_row": eps_row, "ind": ind,
        "constf": np.ascontiguousarray(np.broadcast_to(
            np.array([[EPS, MAGIC]], np.float32), (128, 2))),
    }
    if "nc" not in _NC_CACHE:
        _NC_CACHE["nc"] = build()
    nc = _NC_CACHE["nc"]
    in_maps = [dict(shared, x=np.ascontiguousarray(x[b])) for b in range(N_CORES)]
    res = run_bass_kernel_spmd(nc, in_maps, core_ids=list(range(N_CORES)))
    out = np.stack([np.asarray(res.results[b]["out"]) for b in range(N_CORES)], axis=0)
    return out


# version 13
# speedup vs baseline: 1.3948x; 1.0344x over previous
import numpy as np
import ml_dtypes
from contextlib import ExitStack

import concourse.bass as bass
import concourse.mybir as mybir
from concourse.bass_utils import run_bass_kernel_spmd

F32 = mybir.dt.float32
BF16 = mybir.dt.bfloat16
AF = mybir.ActivationFunctionType
ALU = mybir.AluOpType
AX = mybir.AxisListType
BF = ml_dtypes.bfloat16

MAGIC = float(np.float32(1.5 * 2 ** 23))
N_CORES = 8
SEQ = 4096
C = 1024
NT = SEQ // 128
CCH = 8
H = 16
DH = 64
NI = 2816
NIC = NI // 128
EPS = 1e-5

COMPUTE_SEMS = {"TE", "VE", "SE", "GE"}


def merge(*ds):
    out = {}
    for d in ds:
        for k, v in (d or {}).items():
            if v:
                out[k] = max(out.get(k, 0), v)
    return out


class Prog:
    ENGS = ("sync", "tensor", "vector", "scalar", "gpsimd")

    def __init__(self, nc, es):
        self.nc = nc
        self.es = es
        self.sems = {}
        self.cnt = {}
        self.streams = {e: [] for e in self.ENGS}
        self.lastwait = {e: {} for e in self.ENGS}
        self.eng_of_sem = {"TE": "tensor", "VE": "vector", "SE": "scalar",
                           "GE": "gpsimd"}
        for name in self.eng_of_sem:
            self.sems[name] = es.enter_context(nc.semaphore(name))
            self.cnt[name] = 0

    def sem(self, name):
        if name not in self.sems:
            self.sems[name] = self.es.enter_context(self.nc.semaphore(name))
            self.cnt[name] = 0
        return name

    SERIAL = False

    def add(self, eng, fn, wait=None, inc=None, inc_amt=1, keep_self=False):
        if Prog.SERIAL and inc is not None:
            wait = merge(wait, getattr(self, "_last", None))
            keep_self = True
        w = {}
        for s, v in (wait or {}).items():
            if not v:
                continue
            assert v <= self.cnt[s], f"wait {s}>={v} but only {self.cnt[s]} emitted"
            if (not keep_self) and s in COMPUTE_SEMS and self.eng_of_sem[s] == eng:
                continue
            if self.lastwait[eng].get(s, -1) < v:
                w[s] = v
                self.lastwait[eng][s] = v
        if inc:
            self.cnt[inc] += inc_amt
        self.streams[eng].append((w, fn, inc, inc_amt))
        if Prog.SERIAL and inc is not None:
            self._last = {inc: self.cnt[inc]}
        return self.cnt[inc] if inc else None

    def dma(self, eng, out, in_, wait=None, sem=None):
        assert sem is not None, "dma needs an explicit stream semaphore"
        self.sem(sem)
        return self.add(eng, lambda e, o=out, i=in_: e.dma_start(out=o, in_=i),
                        wait=wait, inc=sem, inc_amt=16)

    def fence(self):
        cur = dict(self.cnt)
        for eng in self.ENGS:
            self.add(eng, lambda e: None, wait=dict(cur))

    def flush(self, blk):
        for eng in self.ENGS:
            ops = self.streams[eng]
            self.streams[eng] = []
            if not ops:
                continue
            sems = self.sems

            def mk(ops_):
                def run(e):
                    for w, fn, inc, amt in ops_:
                        for s, v in w.items():
                            e.wait_ge(sems[s], v)
                        ins = fn(e)
                        if inc is not None:
                            ins.then_inc(sems[inc], amt)
                return run
            getattr(blk, eng)(mk(ops))


def build(debug=()):
    nc = bass.Bass("TRN2", target_bir_lowering=False)

    d_x = nc.dram_tensor("x", [SEQ, C], F32, kind="ExternalInput").ap()
    d_wqkv = nc.dram_tensor("wqkv", [C, 3 * C], F32, kind="ExternalInput").ap()
    d_wf = nc.dram_tensor("wf", [DH, DH], F32, kind="ExternalInput").ap()
    d_bf = nc.dram_tensor("bfv", [1, DH], F32, kind="ExternalInput").ap()
    d_wout = nc.dram_tensor("wout", [C, C], F32, kind="ExternalInput").ap()
    d_bout = nc.dram_tensor("bout", [1, C], F32, kind="ExternalInput").ap()
    d_clipx = nc.dram_tensor("clipx", [128, 1], F32, kind="ExternalInput").ap()
    d_clipd = nc.dram_tensor("clipd", [128, 1], F32, kind="ExternalInput").ap()
    d_wg = nc.dram_tensor("wg", [NI, C], F32, kind="ExternalInput").ap()
    d_wu = nc.dram_tensor("wu", [NI, C], F32, kind="ExternalInput").ap()
    d_wd = nc.dram_tensor("wd", [C, NI], F32, kind="ExternalInput").ap()
    d_ident = nc.dram_tensor("ident", [128, 128], BF16, kind="ExternalInput").ap()
    d_ones_col = nc.dram_tensor("ones_col", [128, 1], BF16, kind="ExternalInput").ap()
    d_ones_row = nc.dram_tensor("ones_row", [1, 1024], BF16, kind="ExternalInput").ap()
    d_ones_f32 = nc.dram_tensor("ones_f32", [1, 128], F32, kind="ExternalInput").ap()
    d_eps_row = nc.dram_tensor("eps_row", [1, 16], BF16, kind="ExternalInput").ap()
    d_ind = nc.dram_tensor("ind", [8, 16, 128], BF16, kind="ExternalInput").ap()
    d_constf = nc.dram_tensor("constf", [128, 2], F32, kind="ExternalInput").ap()

    d_out = nc.dram_tensor("out", [SEQ, C], F32, kind="ExternalOutput").ap()

    d_x1 = nc.dram_tensor("x1_scr", [SEQ, C], F32, kind="Internal").ap()
    d_h = nc.dram_tensor("h_scr", [NIC, 128, SEQ], F32, kind="Internal").ap()
    d_wqT = nc.dram_tensor("wqT_scr", [2, NIC, 128, CCH * 128], BF16, kind="Internal").ap()
    d_wdT = nc.dram_tensor("wdT_scr", [NIC, CCH, 128, 128], BF16, kind="Internal").ap()
    d_cc1_in = nc.dram_tensor("cc1_in", [128, 1], F32, kind="Internal").ap()
    d_cc1_out = nc.dram_tensor("cc1_out", [128, 1], F32, kind="Internal", addr_space="Shared").ap()
    d_cc2_in = nc.dram_tensor("cc2_in", [128, 1], F32, kind="Internal").ap()
    d_cc2_out = nc.dram_tensor("cc2_out", [128, 1], F32, kind="Internal", addr_space="Shared").ap()
    d_s128a = nc.dram_tensor("s128a", [128, 1], F32, kind="Internal").ap()
    d_s128b = nc.dram_tensor("s128b", [128, 1], F32, kind="Internal").ap()

    dbg = {}
    for name in debug:
        shp = {"x1": ([SEQ, C], F32), "q_fm": ([128, CCH * SEQ], BF16),
               "kvx": ([128, 8 * 128], BF16), "ks": ([128, 8 * 16], BF16),
               "xq": ([128, CCH * SEQ], BF16), "h0": ([128, SEQ], F32),
               "s1": ([128, 1], F32), "s2": ([128, 1], F32),
               "wall": ([128, CCH * 3 * C], BF16), "xnfm": ([128, CCH * 128], BF16),
               "wdts": ([128, NIC * C], BF16), "hqs": ([128, NIC * 512], BF16),
               "hmax": ([128, 1], F32), "hall": ([NIC * 128, SEQ], F32),
               "pdu": ([128, 3 * 512], F32),
               "ktok": ([128, C], BF16), "vtok": ([128, C], BF16),
               "qtok": ([128, C], BF16), "stat0": ([128, 16], F32)}[name]
        dt_ = BF16 if name in ("q_fm", "kvx", "ks", "xq", "wall", "xnfm", "ktok",
                               "vtok", "qtok", "wdts", "hqs") else F32
        dbg[name] = nc.dram_tensor("dbg_" + name, shp[0], dt_, kind="ExternalOutput").ap()

    es_all = ExitStack()
    with es_all:
        es = es_all
        ident = es.enter_context(nc.sbuf_tensor("identb", [128, 128], BF16))
        ones_col = es.enter_context(nc.sbuf_tensor("onescol", [128, 1], BF16))
        ones_row = es.enter_context(nc.sbuf_tensor("onesrow", [1, 1024], BF16))
        ones_f32 = es.enter_context(nc.sbuf_tensor("onesf32", [1, 128], F32))
        eps_row = es.enter_context(nc.sbuf_tensor("epsrow", [1, 16], BF16))
        ind_sb = es.enter_context(nc.sbuf_tensor("indsb", [16, 8 * 128], BF16))
        bias_row = es.enter_context(nc.sbuf_tensor("biasrow", [1, 3 * C], BF16))
        bf_bf = es.enter_context(nc.sbuf_tensor("bfbf", [1, DH], BF16))
        bout_row = es.enter_context(nc.sbuf_tensor("boutrow", [1, C], BF16))
        clipx_sb = es.enter_context(nc.sbuf_tensor("clipxsb", [128, 1], F32))
        clipd_sb = es.enter_context(nc.sbuf_tensor("clipdsb", [128, 1], F32))
        r2_all = es.enter_context(nc.sbuf_tensor("r2all", [128, NT], F32))
        nmr2_all = es.enter_context(nc.sbuf_tensor("nmr2all", [128, NT], F32))
        sg_all = es.enter_context(nc.sbuf_tensor("sgall", [128, NIC], F32))
        su_all = es.enter_context(nc.sbuf_tensor("suall", [128, NIC], F32))
        s1_col = es.enter_context(nc.sbuf_tensor("s1col", [128, 1], F32))
        rs1_col = es.enter_context(nc.sbuf_tensor("rs1col", [128, 1], F32))
        s2_col = es.enter_context(nc.sbuf_tensor("s2col", [128, 1], F32))
        rs2_col = es.enter_context(nc.sbuf_tensor("rs2col", [128, 1], F32))
        uscale_all = es.enter_context(nc.sbuf_tensor("uscaleall", [128, NT], F32))
        ubias_all = es.enter_context(nc.sbuf_tensor("ubiasall", [128, NT], F32))
        s1sg_all = es.enter_context(nc.sbuf_tensor("s1sgall", [128, NIC], F32))
        s1su_all = es.enter_context(nc.sbuf_tensor("s1suall", [128, NIC], F32))
        grow = es.enter_context(nc.sbuf_tensor("growbuf", [1, 128], F32))
        gcol = es.enter_context(nc.sbuf_tensor("gcolbuf", [1, 1], F32))
        gbc = es.enter_context(nc.sbuf_tensor("gbcbuf", [128, 1], F32))
        sig_col = es.enter_context(nc.sbuf_tensor("sigcol", [128, 1], F32))
        kvx_sb = es.enter_context(nc.sbuf_tensor("kvxsb", [128, 8 * 128], BF16))
        ks_sb = es.enter_context(nc.sbuf_tensor("kssb", [128, 8 * 16], BF16))
        xb = [es.enter_context(nc.sbuf_tensor(f"xb{i}", [128, C], F32)) for i in range(2)]
        x1b = [es.enter_context(nc.sbuf_tensor(f"x1b{i}", [128, C], F32)) for i in range(2)]
        sq = es.enter_context(nc.sbuf_tensor("sqscr", [128, C], F32))
        stat = es.enter_context(nc.sbuf_tensor("statscr", [128, 16], F32))
        constf = es.enter_context(nc.sbuf_tensor("constf_sb", [128, 2], F32))
        esBig = ExitStack()
        big = esBig.enter_context(nc.sbuf_tensor("big", [128, CCH * SEQ], BF16))

        blk = es.enter_context(nc.Block())
        p = Prog(nc, es)
        bigv = big[:].rearrange("p (c n) -> p c n", n=SEQ)

        xb_free = [{}, {}]
        x1b_free = [{}, {}]

        for o, i in ((ident[:], d_ident), (ones_col[:], d_ones_col),
                     (ones_row[:], d_ones_row), (ones_f32[:], d_ones_f32),
                     (eps_row[:], d_eps_row), (clipx_sb[:], d_clipx),
                     (clipd_sb[:], d_clipd), (constf[:], d_constf)):
            p.dma("sync", o, i, sem="sCONST")
        for g in range(8):
            p.dma("sync", ind_sb[:, g * 128:(g + 1) * 128], d_ind[g], sem="sCONST")
        sd_consts = p.cnt["sCONST"]
        p.dma("gpsimd", bf_bf[:], d_bf, sem="sCONST2")
        p.dma("gpsimd", bout_row[:], d_bout, sem="sCONST2")
        gd_consts = p.cnt["sCONST2"]

        p.add("vector", lambda e: e.memset(bias_row[:], 0.0), inc="VE")
        for j in range(16):
            p.add("vector", lambda e, j=j: e.tensor_copy(
                bias_row[:, j * 64:(j + 1) * 64], bf_bf[:]),
                wait={"sCONST2": gd_consts}, inc="VE")
            p.add("vector", lambda e, j=j: e.tensor_copy(
                bias_row[:, 2048 + j * 64: 2048 + (j + 1) * 64], bf_bf[:]), inc="VE")
        p.add("vector", lambda e: e.memset(stat[:, 10:14], 0.0), inc="VE")
        ve_consts = p.cnt["VE"]

        esA = ExitStack()
        if True:
            Wall = esA.enter_context(nc.sbuf_tensor("Wall", [128, CCH * 3 * C], BF16))
            wf_sb = esA.enter_context(nc.sbuf_tensor("wf_sb", [DH, DH], BF16))
            wTb = esA.enter_context(nc.sbuf_tensor("wTbuf", [64, 1024], BF16))
            xnfm0 = esA.enter_context(nc.sbuf_tensor("xnfm0", [128, CCH * 128], BF16))
            xnfm1 = esA.enter_context(nc.sbuf_tensor("xnfm1", [128, CCH * 128], BF16))
            xntok = esA.enter_context(nc.sbuf_tensor("xntok", [128, C], BF16))
            ktok0 = esA.enter_context(nc.sbuf_tensor("ktok0", [128, C], BF16))
            ktok1 = esA.enter_context(nc.sbuf_tensor("ktok1", [128, C], BF16))
            vtok0 = esA.enter_context(nc.sbuf_tensor("vtok0", [128, C], BF16))
            vtok1 = esA.enter_context(nc.sbuf_tensor("vtok1", [128, C], BF16))
            qtok0 = esA.enter_context(nc.sbuf_tensor("qtok0", [128, C], BF16))
            qtok1 = esA.enter_context(nc.sbuf_tensor("qtok1", [128, C], BF16))
            ptr6 = esA.enter_context(nc.psum_tensor("ptr6", [128, 1024], BF16))
            ptr7 = esA.enter_context(nc.psum_tensor("ptr7", [128, 1024], BF16))
            pkv = esA.enter_context(nc.psum_tensor("pkv", [128, 512], F32))
            pks = esA.enter_context(nc.psum_tensor("pks", [128, 512], F32))
            pa0 = esA.enter_context(nc.psum_tensor("pa0", [128, 512], F32))
            pa1 = esA.enter_context(nc.psum_tensor("pa1", [128, 512], F32))
            pa2 = esA.enter_context(nc.psum_tensor("pa2", [128, 512], F32))
            pa3 = esA.enter_context(nc.psum_tensor("pa3", [128, 512], F32))
            Wall_v = Wall[:].rearrange("p (c n) -> p c n", n=3 * C)
            pabank = [pa0, pa1, pa2, pa3]

            p.dma("gpsimd", wf_sb[:], d_wf, sem="sWF")
            gd_wf = p.cnt["sWF"]
            for cc in range(CCH):
                p.dma("gpsimd", Wall_v[:, cc, 1024:2048],
                      d_wqkv[cc * 128:(cc + 1) * 128, 2048:3072], sem="sWALL")
            gd_v = p.cnt["sWALL"]

            te_fold_tr_last = 0
            ve_fold = 0
            esW = ExitStack()
            if True:
                wstage = esW.enter_context(nc.sbuf_tensor("wstage", [128, CCH * 2048], BF16))
                wstage_v = wstage[:].rearrange("p (c n) -> p c n", n=2048)
                for cc in range(CCH):
                    p.dma("gpsimd", wstage_v[:, cc, 0:1024],
                          d_wqkv[cc * 128:(cc + 1) * 128, 0:1024], sem="sSTG")
                    p.dma("gpsimd", wstage_v[:, cc, 1024:2048],
                          d_wqkv[cc * 128:(cc + 1) * 128, 1024:2048], sem="sSTG")
                gd_stage = p.cnt["sSTG"]

                tr6_ve = 0
                mm_ve = [0, 0]
                wT_te = 0
                gidx = 0
                for src_off, dst_off in ((1024, 0), (0, 2048)):
                    for hh in range(H):
                        bk = gidx % 2
                        gidx += 1
                        te_tr = None
                        for cc in range(CCH):
                            te_tr = p.add("tensor", lambda e, cc=cc, so=src_off, hh=hh:
                                          e.transpose(ptr6[0:64, cc * 128:(cc + 1) * 128],
                                                      wstage_v[:, cc, so + hh * 64: so + (hh + 1) * 64],
                                                      ident[:]),
                                          wait={"sSTG": gd_stage, "sCONST": sd_consts,
                                                "VE": tr6_ve} if cc == 0 else None,
                                          inc="TE")
                        te_fold_tr_last = te_tr
                        v_wt = p.add("vector", lambda e: e.tensor_copy(
                            wTb[:], ptr6[0:64, :]),
                            wait={"TE": max(te_tr, wT_te)}, inc="VE")
                        tr6_ve = v_wt
                        te_mm = None
                        for cc in range(CCH):
                            te_mm = p.add("tensor", lambda e, cc=cc, bk=bk:
                                          e.matmul(pabank[bk][:, cc * 64:(cc + 1) * 64],
                                                   wTb[:, cc * 128:(cc + 1) * 128],
                                                   wf_sb[:], start=True, stop=True),
                                          wait={"VE": max(v_wt, mm_ve[bk]),
                                                "sWF": gd_wf} if cc == 0 else None,
                                          inc="TE")
                        wT_te = te_mm
                        mm_ve[bk] = p.add(
                            "vector",
                            lambda e, bk=bk, do=dst_off, hh=hh:
                            e.tensor_copy(
                                Wall_v[:, :, do + hh * 64: do + (hh + 1) * 64],
                                pabank[bk][:, 0:512].rearrange("p (c n) -> p c n", n=64)),
                            wait={"TE": te_mm}, inc="VE")
                ve_fold = p.cnt["VE"]
                if "wall" in dbg:
                    p.dma("sync", dbg["wall"], Wall[:], wait={"VE": ve_fold, "sWALL": gd_v}, sem="sDBG")
            esW.close()

            if True:
                wfbA = esA.enter_context(nc.sbuf_tensor("wfbA", [128, C], F32))
                wfbB = esA.enter_context(nc.sbuf_tensor("wfbB", [128, C], F32))
                wub = esA.enter_context(nc.sbuf_tensor("wub", [128, C], F32))
                wt1b = esA.enter_context(nc.sbuf_tensor("wt1b", [128, C], F32))
                whatbA = esA.enter_context(nc.sbuf_tensor("whatbA", [128, C], BF16))
                whatbB = esA.enter_context(nc.sbuf_tensor("whatbB", [128, C], BF16))
                whatbC = esA.enter_context(nc.sbuf_tensor("whatbC", [128, C], BF16))
                whatbD = esA.enter_context(nc.sbuf_tensor("whatbD", [128, C], BF16))
                wstgA = esA.enter_context(nc.sbuf_tensor("wstgA", [128, C], BF16))
                wstgB = esA.enter_context(nc.sbuf_tensor("wstgB", [128, C], BF16))
                wstat = esA.enter_context(nc.sbuf_tensor("wstat", [128, 8], F32))
                chunks = [("g", i) for i in range(NIC)] + [("u", i) for i in range(NIC)]
                wfb = [wfbA, wfbB]
                whatb = [whatbA, whatbB, whatbC, whatbD]
                wstg = [wstgA, wstgB]
                cp = {"s1": 0, "s2": 0,
                      "wfb_free": [{"TE": te_fold_tr_last}, {"TE": te_fold_tr_last}],
                      "what_free": [{}, {}, {}, {}], "wstg_free": [{}, {}],
                      "wstat_free": {}, "wub_free": {}, "what_v6": {}, "scales": {}}

                def prep_stage1():
                    if cp["s1"] >= len(chunks):
                        return
                    k = cp["s1"]
                    cp["s1"] += 1
                    kind, idx = chunks[k]
                    kb = k % 2
                    kb4 = k % 4
                    src_t = d_wg if kind == "g" else d_wu
                    gdv = p.dma("gpsimd", wfb[kb][:], src_t[idx * 128:(idx + 1) * 128, :],
                                wait=cp["wfb_free"][kb], sem=f"sWFB{kb}")
                    v1 = p.add("vector", lambda e, kb=kb: e.reduce_max(
                        wstat[:, 0:1], wfb[kb][:], AX.X, apply_absolute_value=True),
                        wait=merge({f"sWFB{kb}": gdv}, cp["wstat_free"]), inc="VE")
                    v2 = p.add("vector", lambda e: e.tensor_scalar(
                        wstat[:, 1:2], wstat[:, 0:1], 1.0 / 127.0, None, ALU.mult),
                        wait={"VE": v1}, keep_self=True, inc="VE")
                    dst = sg_all if kind == "g" else su_all
                    p.add("vector", lambda e, dst=dst, idx=idx: e.tensor_copy(
                        dst[:, idx:idx + 1], wstat[:, 1:2]),
                        wait={"VE": v2}, keep_self=True, inc="VE")
                    v3 = p.add("vector", lambda e: e.reciprocal(
                        wstat[:, 2:3], wstat[:, 1:2]),
                        wait={"VE": v2}, keep_self=True, inc="VE")
                    s1 = p.add("vector", lambda e, kb=kb: e.tensor_scalar(
                        wub[:], wfb[kb][:], wstat[:, 2:3], MAGIC, ALU.mult, ALU.add),
                        wait=merge({"VE": v3}, cp["wub_free"]), keep_self=True, inc="VE")
                    cp["wfb_free"][kb] = {"VE": s1}
                    cp["wstat_free"] = {"VE": s1}
                    v4 = p.add("vector", lambda e: e.tensor_scalar(
                        wt1b[:], wub[:], MAGIC, 127.0, ALU.subtract, ALU.min),
                        inc="VE")
                    cp["wub_free"] = {"VE": v4}
                    v6 = p.add("vector", lambda e, kb4=kb4: e.tensor_scalar(
                        whatb[kb4][:], wt1b[:], -128.0, None, ALU.max),
                        wait=cp["what_free"][kb4], inc="VE")
                    cp["what_v6"][k] = v6

                def prep_stage2(tr_psum, tr_free):
                    if cp["s2"] >= cp["s1"]:
                        return
                    k = cp["s2"]
                    cp["s2"] += 1
                    kind, idx = chunks[k]
                    kb = k % 2
                    kb4 = k % 4
                    v6 = cp["what_v6"].pop(k)
                    te_last = None
                    for j in range(8):
                        te_last = p.add("tensor", lambda e, j=j, kb4=kb4:
                                        e.transpose(tr_psum[:, j * 128:(j + 1) * 128],
                                                    whatb[kb4][:, j * 128:(j + 1) * 128],
                                                    ident[:]),
                                        wait=merge({"VE": v6}, tr_free[0])
                                        if j == 0 else None, inc="TE")
                    cp["what_free"][kb4] = {"TE": te_last}
                    se_ev = p.add("scalar", lambda e, kb=kb: e.activation(
                        wstg[kb][:], tr_psum[:, 0:1024], AF.Identity),
                        wait=merge({"TE": te_last}, cp["wstg_free"][kb]), inc="SE")
                    tr_free[0] = {"SE": se_ev}
                    mi = 0 if kind == "g" else 1
                    gdo = p.dma("gpsimd", d_wqT[mi, idx], wstg[kb][:],
                                wait={"SE": se_ev}, sem=f"sWSG{kb}")
                    cp["wstg_free"][kb] = {f"sWSG{kb}": gdo}

                v_pkv0 = p.add("vector", lambda e: e.memset(pkv[:], 0.0), inc="VE")
                v_pks0 = p.add("vector", lambda e: e.memset(pks[:], 0.0), inc="VE")
                xnfm = [xnfm0, xnfm1]
                xnfm_v = [xnfm0[:].rearrange("p (c n) -> p c n", n=128),
                          xnfm1[:].rearrange("p (c n) -> p c n", n=128)]
                ktok = [ktok0, ktok1]
                vtok = [vtok0, vtok1]
                qtok = [qtok0, qtok1]
                xnfm_free = [0, 0]
                ktok_free = [0, 0]
                vtok_free = [0, 0]
                qtok_free = [0, 0]
                bank_free = [0, 0, 0, 0]
                tr6_free = [ve_fold]
                tr7_free = [{}]
                xntok_free = [0]
                prev_v_var = 0
                kv_te_last = 0
                sd_x1_pb = [0] * NT

                def pa_front(tt):
                    bi = tt % 2
                    sdx = p.dma("sync", xb[bi][:], d_x[tt * 128:(tt + 1) * 128, :],
                                wait=xb_free[bi], sem=f"sXB{bi}")
                    s_sum = p.add("scalar", lambda e, bi=bi: e.activation(
                        sq[:], xb[bi][:], AF.Identity, accum_out=stat[:, 0:1]),
                        wait={f"sXB{bi}": sdx, "SE": pa_state["prev_s_var"]}, inc="SE")
                    s_sq = p.add("scalar", lambda e, bi=bi: e.activation(
                        sq[:], xb[bi][:], AF.Square, accum_out=stat[:, 1:2]),
                        inc="SE")
                    s_mean = p.add("scalar", lambda e: e.activation(
                        stat[:, 2:3], stat[:, 0:1], AF.Identity, scale=1.0 / C),
                        wait={"SE": s_sq}, keep_self=True, inc="SE")
                    s_m2 = p.add("scalar", lambda e: e.activation(
                        stat[:, 3:4], stat[:, 2:3], AF.Square),
                        wait={"SE": s_mean}, keep_self=True, inc="SE")
                    s_nm2 = p.add("scalar", lambda e: e.activation(
                        stat[:, 4:5], stat[:, 3:4], AF.Identity, scale=-1.0),
                        wait={"SE": s_m2}, keep_self=True, inc="SE")
                    s_var = p.add("scalar", lambda e: e.activation(
                        stat[:, 5:6], stat[:, 1:2], AF.Identity, scale=1.0 / C,
                        bias=stat[:, 4:5]),
                        wait={"SE": s_nm2}, keep_self=True, inc="SE")
                    s_sd = p.add("scalar", lambda e: e.activation(
                        stat[:, 6:7], stat[:, 5:6], AF.Sqrt, bias=constf[:, 0:1]),
                        wait={"SE": s_var}, keep_self=True, inc="SE")
                    pa_state["prev_s_var"] = s_sd
                    v_r = p.add("vector", lambda e: e.reciprocal(
                        stat[:, 7:8], stat[:, 6:7]), wait={"SE": s_sd}, inc="VE")
                    s_mr = p.add("scalar", lambda e: e.activation(
                        stat[:, 8:9], stat[:, 2:3], AF.Identity, scale=stat[:, 7:8]),
                        wait={"VE": v_r}, inc="SE")
                    s_xn = p.add("vector", lambda e, bi=bi: e.tensor_scalar(
                        xntok[:], xb[bi][:], stat[:, 7:8], stat[:, 8:9],
                        ALU.mult, ALU.subtract),
                        wait={"SE": s_mr, "TE": xntok_free[0]}, inc="VE")
                    xb_free[bi] = {"VE": s_xn}
                    pa_state["s_xn"][tt] = s_xn

                def pa_frontT(tt):
                    bi = tt % 2
                    s_xn = pa_state["s_xn"][tt]
                    te_tr = None
                    for j in range(8):
                        te_tr = p.add("tensor", lambda e, j=j: e.transpose(
                            ptr6[:, j * 128:(j + 1) * 128], xntok[:, j * 128:(j + 1) * 128],
                            ident[:]),
                            wait={"VE": max(s_xn, tr6_free[0])}
                            if j == 0 else None, inc="TE")
                    xntok_free[0] = te_tr
                    v_xnfm = p.add("vector", lambda e, bi=bi: e.tensor_copy(
                        xnfm[bi][:], ptr6[:, 0:1024]),
                        wait={"TE": max(te_tr, xnfm_free[bi])}, inc="VE")
                    tr6_free[0] = v_xnfm
                    pa_state["v_xnfm"][tt] = v_xnfm

                def pa_back(tt):
                    bi = tt % 2
                    v_xnfm = pa_state["v_xnfm"][tt]
                    grp_last_te = 0
                    for grp, bank, kind in ((0, 0, "k"), (1, 1, "k"), (2, 2, "v"),
                                            (3, 3, "v"), (4, 0, "q"), (5, 1, "q")):
                        te_g = None
                        for c in range(CCH):
                            last = (c == CCH - 1)
                            te_g = p.add(
                                "tensor",
                                lambda e, grp=grp, bank=bank, c=c, bi=bi, kind=kind:
                                e.matmul(pabank[bank][:, 0:512], xnfm_v[bi][:, c, :],
                                         Wall_v[:, c, grp * 512:(grp + 1) * 512],
                                         start=(c == 0),
                                         stop=(last and kind == "v")),
                                wait=merge({"VE": max(v_xnfm, ve_fold)},
                                           {"SE": bank_free[bank]},
                                           {"sWALL": gd_v}) if c == 0 else None,
                                inc="TE")
                        if kind != "v":
                            te_g = p.add("tensor", lambda e, grp=grp, bank=bank:
                                         e.matmul(pabank[bank][:, 0:512],
                                                  ones_row[0:1, 0:128],
                                                  bias_row[0:1, grp * 512:(grp + 1) * 512],
                                                  start=False, stop=True),
                                         wait={"VE": ve_consts}, inc="TE")
                        grp_last_te = te_g
                        if kind == "k":
                            se_e = p.add("scalar", lambda e, grp=grp, bank=bank, bi=bi:
                                         e.activation(ktok[bi][:, grp * 512:(grp + 1) * 512],
                                                      pabank[bank][:, 0:512], AF.Relu),
                                         wait={"TE": max(te_g, ktok_free[bi])}, inc="SE")
                        elif kind == "v":
                            se_e = p.add("scalar", lambda e, grp=grp, bank=bank, bi=bi:
                                         e.activation(vtok[bi][:, (grp - 2) * 512:(grp - 1) * 512],
                                                      pabank[bank][:, 0:512], AF.Identity),
                                         wait={"TE": max(te_g, vtok_free[bi])}, inc="SE")
                        else:
                            se_e = p.add("scalar", lambda e, grp=grp, bank=bank, bi=bi:
                                         e.activation(qtok[bi][:, (grp - 4) * 512:(grp - 3) * 512],
                                                      pabank[bank][:, 0:512], AF.Relu),
                                         wait={"TE": max(te_g, qtok_free[bi])}, inc="SE")
                        bank_free[bank] = se_e
                    se_kv_ready = max(bank_free)

                    te_kv = None
                    for hh in range(H):
                        pr = hh // 2
                        po = (hh % 2) * 64
                        te_kv = p.add("tensor", lambda e, hh=hh, pr=pr, po=po, bi=bi, tt=tt:
                                      e.matmul(pkv[po:po + 64, pr * 64:(pr + 1) * 64],
                                               ktok[bi][:, hh * 64:(hh + 1) * 64],
                                               vtok[bi][:, hh * 64:(hh + 1) * 64],
                                               start=False,
                                               stop=(tt == NT - 1 and hh == H - 1),
                                               skip_group_check=True),
                                      wait={"SE": se_kv_ready, "VE": v_pkv0}
                                      if hh == 0 else None, inc="TE")
                        te_kv = p.add("tensor", lambda e, hh=hh, po=po, bi=bi, tt=tt:
                                      e.matmul(pks[po:po + 64, hh:hh + 1],
                                               ktok[bi][:, hh * 64:(hh + 1) * 64],
                                               ones_col[:],
                                               start=False,
                                               stop=(tt == NT - 1 and hh == H - 1),
                                               skip_group_check=True),
                                      inc="TE")
                    ktok_free[bi] = te_kv
                    vtok_free[bi] = te_kv
                    pa_state["kv_te_last"] = te_kv

                    te_qt = None
                    for j in range(8):
                        te_qt = p.add("tensor", lambda e, j=j, bi=bi: e.transpose(
                            ptr7[:, j * 128:(j + 1) * 128], qtok[bi][:, j * 128:(j + 1) * 128],
                            ident[:]),
                            wait=merge({"SE": bank_free[1]}, tr7_free[0])
                            if j == 0 else None, inc="TE")
                    qtok_free[bi] = te_qt
                    v_qfm = p.add("vector", lambda e, tt=tt: e.tensor_copy(
                        bigv[:, :, tt * 128:(tt + 1) * 128], ptr7[:, 0:1024]),
                        wait={"TE": te_qt}, inc="VE")
                    tr7_free[0] = {"VE": v_qfm}
                    xnfm_free[bi] = grp_last_te

                    if tt == 0:
                        if "xnfm" in dbg:
                            p.dma("sync", dbg["xnfm"], xnfm0[:], wait={"VE": v_xnfm}, sem="sDBG")
                        if "stat0" in dbg:
                            p.dma("sync", dbg["stat0"], stat[:], wait={"VE": v_xnfm}, sem="sDBG")
                        if "ktok" in dbg:
                            p.dma("sync", dbg["ktok"], ktok0[:], wait={"SE": bank_free[1]}, sem="sDBG")
                        if "vtok" in dbg:
                            p.dma("sync", dbg["vtok"], vtok0[:], wait={"SE": bank_free[3]}, sem="sDBG")
                        if "qtok" in dbg:
                            p.dma("sync", dbg["qtok"], qtok0[:], wait={"SE": bank_free[1]}, sem="sDBG")

                pa_state = {"prev_s_var": 0, "v_xnfm": {}, "kv_te_last": 0, "s_xn": {}}
                pa_front(0)
                pa_frontT(0)
                for tt in range(NT):
                    if tt + 1 < NT:
                        pa_front(tt + 1)
                    want1 = min(len(chunks), 2 * (tt + 1))
                    while cp["s1"] < want1:
                        prep_stage1()
                    want2 = min(cp["s1"], max(0, 2 * tt))
                    while cp["s2"] < want2:
                        prep_stage2(ptr7, tr7_free)
                    pa_back(tt)
                    if tt + 1 < NT:
                        pa_frontT(tt + 1)
                kv_te_last = pa_state["kv_te_last"]
                while cp["s2"] < len(chunks):
                    prep_stage2(ptr7, tr7_free)
                ve_prep_pa = p.cnt["VE"]
                gd_prep_pa = (p.cnt.get("sWSG0", 0), p.cnt.get("sWSG1", 0))
                se_prep_pa = p.cnt["SE"]

                p.add("vector", lambda e: e.memset(kvx_sb[:], 0.0),
                      wait={"TE": kv_te_last}, inc="VE")
                p.add("vector", lambda e: e.memset(ks_sb[:], 0.0), inc="VE")
                for g in range(8):
                    p.add("vector", lambda e, g=g: e.tensor_copy(
                        kvx_sb[0:64, g * 128: g * 128 + 64],
                        pkv[0:64, g * 64:(g + 1) * 64]), inc="VE")
                    p.add("vector", lambda e, g=g: e.tensor_copy(
                        kvx_sb[64:128, g * 128 + 64: g * 128 + 128],
                        pkv[64:128, g * 64:(g + 1) * 64]), inc="VE")
                    p.add("vector", lambda e, g=g: e.tensor_copy(
                        ks_sb[0:64, g * 16 + 2 * g: g * 16 + 2 * g + 1],
                        pks[0:64, 2 * g:2 * g + 1]), inc="VE")
                    p.add("vector", lambda e, g=g: e.tensor_copy(
                        ks_sb[64:128, g * 16 + 2 * g + 1: g * 16 + 2 * g + 2],
                        pks[64:128, 2 * g + 1:2 * g + 2]), inc="VE")
                ve_kvx = p.cnt["VE"]
        esA.close()

        p.fence()

        esB = ExitStack()
        if True:
            WoutB = esB.enter_context(nc.sbuf_tensor("WoutB", [128, CCH * C], BF16))
            Awin0 = esB.enter_context(nc.sbuf_tensor("Awin0", [128, CCH * 512], BF16))
            Awin1 = esB.enter_context(nc.sbuf_tensor("Awin1", [128, CCH * 512], BF16))
            rdb0 = esB.enter_context(nc.sbuf_tensor("rdb0", [16, 512], BF16))
            rdb1 = esB.enter_context(nc.sbuf_tensor("rdb1", [16, 512], BF16))
            bcb0 = esB.enter_context(nc.sbuf_tensor("bcb0", [128, 512], F32))
            bcb1 = esB.enter_context(nc.sbuf_tensor("bcb1", [128, 512], F32))
            pnrm = esB.enter_context(nc.psum_tensor("pnrm", [128, 512], F32))
            pbc0 = esB.enter_context(nc.psum_tensor("pbc0", [128, 512], F32))
            pat0 = esB.enter_context(nc.psum_tensor("pat0", [128, 512], F32))
            pbc1 = esB.enter_context(nc.psum_tensor("pbc1", [128, 512], F32))
            pat1 = esB.enter_context(nc.psum_tensor("pat1", [128, 512], F32))
            po0 = esB.enter_context(nc.psum_tensor("po0", [128, 512], F32))
            po1 = esB.enter_context(nc.psum_tensor("po1", [128, 512], F32))
            ptrB = esB.enter_context(nc.psum_tensor("ptrB", [128, 1024], BF16))
            wfb_d = esB.enter_context(nc.sbuf_tensor("wfb_d", [128, NI], F32))
            wub_d = esB.enter_context(nc.sbuf_tensor("wub_d", [128, NI // 2], F32))
            wt1_d = esB.enter_context(nc.sbuf_tensor("wt1_d", [128, NI // 2], F32))
            what_d0 = esB.enter_context(nc.sbuf_tensor("what_d0", [128, NI], BF16))
            what_d1 = esB.enter_context(nc.sbuf_tensor("what_d1", [128, NI], BF16))
            wstg_d0 = esB.enter_context(nc.sbuf_tensor("wstg_d0", [128, NI], BF16))
            wstg_d1 = esB.enter_context(nc.sbuf_tensor("wstg_d1", [128, NI], BF16))
            wstat_d = esB.enter_context(nc.sbuf_tensor("wstat_d", [128, 4], F32))
            Wout_v = WoutB[:].rearrange("p (c n) -> p c n", n=C)
            for cc in range(CCH):
                p.dma("gpsimd", Wout_v[:, cc, :], d_wout[cc * 128:(cc + 1) * 128, :],
                      sem="sWOUT")
            gd_wout = p.cnt["sWOUT"]

            Awin = [Awin0, Awin1]
            Awin_v = [Awin0[:].rearrange("p (c n) -> p c n", n=512),
                      Awin1[:].rearrange("p (c n) -> p c n", n=512)]
            rd = [rdb0, rdb1]
            bc = [bcb0, bcb1]
            pbc = [pbc0, pbc1]
            pat = [pat0, pat1]
            Awin_free = [0, 0]
            rd_free = [0, 0]
            bc_free = [0, 0]
            pat_free = [0, 0]
            pbc_free = [0, 0]
            po_free = [0, 0]

            pb_state = {"ve_awin": {}, "prev_v_am": 0}

            def pb_front(ns):
                wi = ns % 2
                te_n = None
                for g in range(8):
                    te_n = p.add("tensor", lambda e, g=g, ns=ns: e.matmul(
                        pnrm[0:16, 0:512], ks_sb[:, g * 16:(g + 1) * 16],
                        bigv[:, g, ns * 512:(ns + 1) * 512],
                        start=(g == 0), stop=False),
                        wait={"VE": max(ve_kvx, pb_state.get("pnrm_free", 0))}
                        if g == 0 else None, inc="TE")
                te_n = p.add("tensor", lambda e: e.matmul(
                    pnrm[0:16, 0:512], eps_row[:], ones_row[0:1, 0:512],
                    start=False, stop=True), inc="TE")
                v_rd = p.add("vector", lambda e, wi=wi: e.reciprocal(
                    rd[wi][:], pnrm[0:16, 0:512]),
                    wait={"TE": max(te_n, rd_free[wi])}, inc="VE")
                pb_state["pnrm_free"] = v_rd

                for g in range(8):
                    pi = g % 2
                    te_bc = p.add("tensor", lambda e, g=g, pi=pi, wi=wi: e.matmul(
                        pbc[pi][:, 0:512], ind_sb[:, g * 128:(g + 1) * 128], rd[wi][:],
                        start=True, stop=True),
                        wait={"VE": v_rd, "SE": pbc_free[pi]}, inc="TE")
                    rd_free[wi] = te_bc
                    te_at = p.add("tensor", lambda e, g=g, pi=pi, ns=ns: e.matmul(
                        pat[pi][:, 0:512], kvx_sb[:, g * 128:(g + 1) * 128],
                        bigv[:, g, ns * 512:(ns + 1) * 512],
                        start=True, stop=True),
                        wait={"VE": pat_free[pi]}, inc="TE")
                    se_bc = p.add("scalar", lambda e, pi=pi: e.activation(
                        bc[pi][:], pbc[pi][:, 0:512], AF.Identity),
                        wait={"TE": te_bc, "VE": bc_free[pi]}, inc="SE")
                    pbc_free[pi] = se_bc
                    v_a = p.add("vector", lambda e, g=g, pi=pi, wi=wi: e.tensor_tensor(
                        Awin_v[wi][:, g, :], pat[pi][:, 0:512], bc[pi][:], ALU.mult),
                        wait={"TE": max(te_at, Awin_free[wi]), "SE": se_bc}, inc="VE")
                    pat_free[pi] = v_a
                    bc_free[pi] = v_a
                pb_state["ve_awin"][ns] = p.cnt["VE"]

            def pb_back(ns):
                wi = ns % 2
                ve_awin = pb_state["ve_awin"][ns]
                for tl in range(4):
                    tt = ns * 4 + tl
                    bi = tt % 2
                    sdx = p.dma("sync", xb[bi][:], d_x[tt * 128:(tt + 1) * 128, :],
                                wait=xb_free[bi], sem=f"sXB{bi}")
                    te_o = None
                    for c in range(CCH):
                        p.add("tensor", lambda e, c=c, tl=tl, wi=wi: e.matmul(
                            po0[:, 0:512], Awin_v[wi][:, c, tl * 128:(tl + 1) * 128],
                            Wout_v[:, c, 0:512], start=(c == 0), stop=False),
                            wait={"VE": max(ve_awin, po_free[0]), "sWOUT": gd_wout}
                            if c == 0 else None, inc="TE")
                        te_o = p.add("tensor", lambda e, c=c, tl=tl, wi=wi: e.matmul(
                            po1[:, 0:512], Awin_v[wi][:, c, tl * 128:(tl + 1) * 128],
                            Wout_v[:, c, 512:1024], start=(c == 0), stop=False),
                            wait={"VE": po_free[1]} if c == 0 else None, inc="TE")
                    p.add("tensor", lambda e: e.matmul(
                        po0[:, 0:512], ones_row[0:1, 0:128], bout_row[0:1, 0:512],
                        start=False, stop=True), wait={"sCONST2": gd_consts}, inc="TE")
                    te_o = p.add("tensor", lambda e: e.matmul(
                        po1[:, 0:512], ones_row[0:1, 0:128], bout_row[0:1, 512:1024],
                        start=False, stop=True), inc="TE")
                    if tl == 3:
                        Awin_free[wi] = te_o
                    v_x1a = p.add("vector", lambda e, bi=bi: e.scalar_tensor_tensor(
                        x1b[bi][:, 0:512], po0[:, 0:512], 1.0, xb[bi][:, 0:512],
                        ALU.mult, ALU.add),
                        wait=merge({"TE": te_o, f"sXB{bi}": sdx}, x1b_free[bi]), inc="VE")
                    v_x1b = p.add("vector", lambda e, bi=bi: e.scalar_tensor_tensor(
                        x1b[bi][:, 512:1024], po1[:, 0:512], 1.0, xb[bi][:, 512:1024],
                        ALU.mult, ALU.add), inc="VE")
                    po_free[0] = v_x1b
                    po_free[1] = v_x1b
                    xb_free[bi] = {"VE": v_x1b}
                    sd_x1 = p.dma("sync", d_x1[tt * 128:(tt + 1) * 128, :], x1b[bi][:],
                                  wait={"VE": v_x1b}, sem=f"sX1W{bi}")
                    sd_x1_pb[tt] = {f"sX1W{bi}": sd_x1}
                    x1b_free[bi] = {f"sX1W{bi}": sd_x1}
                    s_sum = p.add("scalar", lambda e, bi=bi: e.activation(
                        sq[:], x1b[bi][:], AF.Identity, accum_out=stat[:, 0:1]),
                        wait={"VE": max(v_x1b, pb_state["prev_v_am"])}, inc="SE")
                    s_sq2 = p.add("scalar", lambda e, bi=bi: e.activation(
                        sq[:], x1b[bi][:], AF.Square, accum_out=stat[:, 1:2]), inc="SE")
                    s_mean = p.add("scalar", lambda e: e.activation(
                        stat[:, 2:3], stat[:, 0:1], AF.Identity, scale=1.0 / C),
                        wait={"SE": s_sq2}, keep_self=True, inc="SE")
                    s_m2 = p.add("scalar", lambda e: e.activation(
                        stat[:, 3:4], stat[:, 2:3], AF.Square),
                        wait={"SE": s_mean}, keep_self=True, inc="SE")
                    s_nm2 = p.add("scalar", lambda e: e.activation(
                        stat[:, 4:5], stat[:, 3:4], AF.Identity, scale=-1.0),
                        wait={"SE": s_m2}, keep_self=True, inc="SE")
                    s_var = p.add("scalar", lambda e: e.activation(
                        stat[:, 5:6], stat[:, 1:2], AF.Identity, scale=1.0 / C,
                        bias=stat[:, 4:5]),
                        wait={"SE": s_nm2}, keep_self=True, inc="SE")
                    s_sd2 = p.add("scalar", lambda e: e.activation(
                        stat[:, 6:7], stat[:, 5:6], AF.Sqrt, bias=constf[:, 0:1]),
                        wait={"SE": s_var}, keep_self=True, inc="SE")
                    v_r2 = p.add("vector", lambda e, tt=tt: e.reciprocal(
                        r2_all[:, tt:tt + 1], stat[:, 6:7]), wait={"SE": s_sd2}, inc="VE")
                    v_nmr2 = p.add("vector", lambda e, tt=tt: e.tensor_scalar(
                        nmr2_all[:, tt:tt + 1], stat[:, 2:3], r2_all[:, tt:tt + 1], -1.0,
                        ALU.mult, ALU.mult),
                        wait={"VE": v_r2}, keep_self=True, inc="VE")
                    s_u = p.add("scalar", lambda e, bi=bi, tt=tt: e.activation(
                        sq[:], x1b[bi][:], AF.Identity, bias=nmr2_all[:, tt:tt + 1],
                        scale=r2_all[:, tt:tt + 1]),
                        wait={"VE": v_nmr2}, inc="SE")
                    v_am = p.add("vector", lambda e: e.reduce_max(
                        stat[:, 8:9], sq[:], AX.X, apply_absolute_value=True),
                        wait={"SE": s_u}, inc="VE")
                    amo, amn = 10 + (tt % 2), 10 + ((tt + 1) % 2)
                    pb_state["prev_v_am"] = p.add(
                        "vector", lambda e, amo=amo, amn=amn: e.tensor_scalar(
                            stat[:, amn:amn + 1], stat[:, 8:9], stat[:, amo:amo + 1], None,
                            ALU.max), wait={"VE": v_am}, keep_self=True, inc="VE")

            what_d = [what_d0, what_d1]
            wstg_d = [wstg_d0, wstg_d1]
            dp = {"s1": 0, "s2": 0, "wfb_free": {}, "wub_free": {}, "wstat_free": {},
                  "what_free": [{}, {}], "wstg_free": [{}, {}], "v6": {},
                  "trB_free": {}}

            def dprep_stage1():
                if dp["s1"] >= CCH:
                    return
                j = dp["s1"]
                dp["s1"] += 1
                kb = j % 2
                gdv = p.dma("gpsimd", wfb_d[:], d_wd[j * 128:(j + 1) * 128, :],
                            wait=dp["wfb_free"], sem="sDL")
                v1 = p.add("vector", lambda e: e.reduce_max(
                    wstat_d[:, 0:1], wfb_d[:], AX.X, apply_absolute_value=True),
                    wait=merge({"sDL": gdv}, dp["wstat_free"]), inc="VE")
                v2 = p.add("vector", lambda e: e.tensor_scalar(
                    wstat_d[:, 1:2], wstat_d[:, 0:1], 1.0 / 127.0, None, ALU.mult),
                    wait={"VE": v1}, keep_self=True, inc="VE")
                v3 = p.add("vector", lambda e: e.reciprocal(
                    wstat_d[:, 2:3], wstat_d[:, 1:2]),
                    wait={"VE": v2}, keep_self=True, inc="VE")
                s1v = None
                v6 = None
                for p0 in range(0, NI, NI // 2):
                    w = NI // 2
                    s1v = p.add("vector", lambda e, p0=p0, w=w: e.tensor_scalar(
                        wub_d[:, 0:w], wfb_d[:, p0:p0 + w], wstat_d[:, 2:3], MAGIC,
                        ALU.mult, ALU.add),
                        wait=merge({"VE": v3}, dp["wub_free"]), keep_self=True, inc="VE")
                    v4 = p.add("vector", lambda e, w=w: e.tensor_scalar(
                        wt1_d[:, 0:w], wub_d[:, 0:w], MAGIC, 127.0,
                        ALU.subtract, ALU.min), inc="VE")
                    p.add("vector", lambda e, w=w: e.tensor_scalar(
                        wub_d[:, 0:w], wt1_d[:, 0:w], -128.0, None, ALU.max), inc="VE")
                    v6 = p.add("vector", lambda e, kb=kb, p0=p0, w=w: e.tensor_scalar(
                        what_d[kb][:, p0:p0 + w], wub_d[:, 0:w], wstat_d[:, 1:2], None,
                        ALU.mult), wait=dp["what_free"][kb], inc="VE")
                    dp["wub_free"] = {"VE": v6}
                dp["wfb_free"] = {"VE": s1v}
                dp["wstat_free"] = {"VE": s1v}
                dp["v6"][j] = v6

            def dprep_stage2():
                if dp["s2"] >= dp["s1"]:
                    return
                j = dp["s2"]
                dp["s2"] += 1
                kb = j % 2
                v6 = dp["v6"].pop(j)
                te_last = None
                for r0 in range(0, NIC, 8):
                    rblks = min(8, NIC - r0)
                    for jj in range(rblks):
                        te_last = p.add("tensor", lambda e, r0=r0, jj=jj, kb=kb:
                                        e.transpose(ptrB[:, jj * 128:(jj + 1) * 128],
                                                    what_d[kb][:, (r0 + jj) * 128:(r0 + jj + 1) * 128],
                                                    ident[:]),
                                        wait=merge({"VE": v6}, dp["trB_free"])
                                        if jj == 0 else None, inc="TE")
                    se_ev = p.add("scalar", lambda e, r0=r0, rblks=rblks, kb=kb:
                                  e.activation(wstg_d[kb][:, r0 * 128:(r0 + rblks) * 128],
                                               ptrB[:, 0:rblks * 128], AF.Identity),
                                  wait=merge({"TE": te_last}, dp["wstg_free"][kb]),
                                  inc="SE")
                    dp["trB_free"] = {"SE": se_ev}
                dp["what_free"][kb] = {"TE": te_last}
                gdo = p.dma("gpsimd", d_wdT[:, j].transpose([1, 0, 2]),
                            wstg_d[kb][:].rearrange("p (i n) -> p i n", n=128),
                            wait=dp["trB_free"], sem=f"sDS{kb}")
                dp["wstg_free"][kb] = {f"sDS{kb}": gdo}

            pb_front(0)
            for ns in range(8):
                if ns + 1 < 8:
                    pb_front(ns + 1)
                dprep_stage1()
                pb_back(ns)
                dprep_stage2()
            while dp["s2"] < CCH:
                dprep_stage2()
            gd_prep_pb = (p.cnt.get("sDS0", 0), p.cnt.get("sDS1", 0))
            ve_pb = p.cnt["VE"]
            sd_pb = dict(p.cnt)

            if "x1" in dbg:
                for t2 in range(NT):
                    p.dma("sync", dbg["x1"][t2 * 128:(t2 + 1) * 128, :],
                          d_x1[t2 * 128:(t2 + 1) * 128, :],
                          wait={"sX1W0": p.cnt.get("sX1W0", 0),
                                "sX1W1": p.cnt.get("sX1W1", 0)}, sem="sDBG")
            if "q_fm" in dbg:
                p.dma("sync", dbg["q_fm"], big[:], wait={"VE": ve_pb}, sem="sDBG")
            if "kvx" in dbg:
                p.dma("sync", dbg["kvx"], kvx_sb[:], wait={"VE": ve_pb}, sem="sDBG")
            if "ks" in dbg:
                p.dma("sync", dbg["ks"], ks_sb[:], wait={"VE": ve_pb}, sem="sDBG")
        esB.close()

        def barrier(acc_ap, cc_in, cc_out, scr, s_col, rs_col, clip_sb, wait_d):
            sd1 = p.dma("sync", cc_in, acc_ap, wait=wait_d, sem="sBAR")
            ge = p.add("gpsimd", lambda e, ci=cc_in, co=cc_out: e.collective_compute(
                "AllReduce", ALU.max, replica_groups=[list(range(N_CORES))],
                ins=[ci], outs=[co]),
                wait={"sBAR": sd1}, inc="GE")
            sd2 = p.dma("sync", grow[:], cc_out.rearrange("a one -> one a"),
                        wait={"GE": ge}, sem="sBAR")
            v1 = p.add("vector", lambda e: e.reduce_max(gcol[:], grow[:], AX.X),
                       wait={"sBAR": sd2}, inc="VE")
            v2 = p.add("vector", lambda e: e.tensor_scalar(
                grow[:], ones_f32[:], gcol[0:1, 0:1], None, ALU.mult),
                wait={"VE": v1}, keep_self=True, inc="VE")
            sd3 = p.dma("sync", scr.rearrange("a one -> one a"), grow[:],
                        wait={"VE": v2}, sem="sBAR")
            sd4 = p.dma("sync", gbc[:], scr, wait={"sBAR": sd3}, sem="sBAR")
            s_sig = p.add("scalar", lambda e, cs=clip_sb: e.activation(
                sig_col[:], cs[:], AF.Sigmoid),
                wait={"sBAR": sd4, "sCONST": sd_consts}, inc="SE")
            v3 = p.add("vector", lambda e, s=s_col: e.scalar_tensor_tensor(
                s[:], sig_col[:], 1.0 / 127.0, gbc[:], ALU.mult, ALU.mult),
                wait={"SE": s_sig, "sBAR": sd4}, inc="VE")
            v4 = p.add("vector", lambda e, s=s_col, r=rs_col: e.reciprocal(r[:], s[:]),
                       wait={"VE": v3}, keep_self=True, inc="VE")
            return v4

        v_b1 = barrier(stat[:, 10:11], d_cc1_in, d_cc1_out, d_s128a, s1_col, rs1_col,
                       clipx_sb, {"VE": ve_pb})
        p.add("vector", lambda e: e.tensor_scalar(
            uscale_all[:], r2_all[:], rs1_col[:], None, ALU.mult),
            wait={"VE": v_b1}, keep_self=True, inc="VE")
        p.add("vector", lambda e: e.tensor_scalar(
            ubias_all[:], nmr2_all[:], rs1_col[:], None, ALU.mult), inc="VE")
        p.add("vector", lambda e: e.tensor_scalar(
            s1sg_all[:], sg_all[:], s1_col[:], None, ALU.mult),
            wait={"VE": ve_prep_pa}, inc="VE")
        v_b1e = p.add("vector", lambda e: e.tensor_scalar(
            s1su_all[:], su_all[:], s1_col[:], None, ALU.mult), inc="VE")
        if "s1" in dbg:
            p.dma("sync", dbg["s1"], s1_col[:], wait={"VE": v_b1e}, sem="sDBG")

        p.fence()

        esC = ExitStack()
        if True:
            wqTsb = esC.enter_context(nc.sbuf_tensor("wqTsb", [128, 2 * NIC * C], BF16))
            ubuf = esC.enter_context(nc.sbuf_tensor("ubuf", [128, C], F32))
            t1buf = esC.enter_context(nc.sbuf_tensor("t1buf", [128, C], F32))
            xqtok = esC.enter_context(nc.sbuf_tensor("xqtok", [128, C], BF16))
            sgb0 = esC.enter_context(nc.sbuf_tensor("sgb0", [128, 512], BF16))
            sgb1 = esC.enter_context(nc.sbuf_tensor("sgb1", [128, 512], BF16))
            hb0 = esC.enter_context(nc.sbuf_tensor("hb0", [128, 512], F32))
            hb1 = esC.enter_context(nc.sbuf_tensor("hb1", [128, 512], F32))
            hb2 = esC.enter_context(nc.sbuf_tensor("hb2", [128, 512], F32))
            pg0 = esC.enter_context(nc.psum_tensor("pg0", [128, 512], F32))
            pu0 = esC.enter_context(nc.psum_tensor("pu0", [128, 512], F32))
            pg1 = esC.enter_context(nc.psum_tensor("pg1", [128, 512], F32))
            pu1 = esC.enter_context(nc.psum_tensor("pu1", [128, 512], F32))
            pg2 = esC.enter_context(nc.psum_tensor("pg2", [128, 512], F32))
            pu2 = esC.enter_context(nc.psum_tensor("pu2", [128, 512], F32))
            ptrC = esC.enter_context(nc.psum_tensor("ptrC", [128, 1024], BF16))
            wqT_v = wqTsb[:].rearrange("p (m i c) -> p m i c", m=2, i=NIC)
            for mi in range(2):
                for ii in range(NIC):
                    p.dma("gpsimd", wqT_v[:, mi, ii, :], d_wqT[mi, ii],
                          wait={"sWSG0": gd_prep_pa[0], "sWSG1": gd_prep_pa[1]},
                          sem="sWQT")
            gd_wqT = p.cnt["sWQT"]

            xq_evac_ve = [0] * NT
            x1_free = [dict(x1b_free[0]), dict(x1b_free[1])]
            trC_free = [0]
            ubuf_free = [0]
            xqtok_free2 = [0]

            def emit_xq_tile(tt):
                bi = tt % 2
                sdx = p.dma("sync", x1b[bi][:], d_x1[tt * 128:(tt + 1) * 128, :],
                            wait=merge(x1_free[bi], sd_x1_pb[tt]), sem=f"sX1R{bi}")
                s_u = p.add("scalar", lambda e, bi=bi, tt=tt: e.activation(
                    ubuf[:], x1b[bi][:], AF.Identity, bias=ubias_all[:, tt:tt + 1],
                    scale=uscale_all[:, tt:tt + 1]),
                    wait={f"sX1R{bi}": sdx, "VE": max(v_b1e, ubuf_free[0])}, inc="SE")
                v_t1 = p.add("vector", lambda e: e.tensor_scalar(
                    t1buf[:], ubuf[:], MAGIC, MAGIC, ALU.add, ALU.subtract),
                    wait={"SE": s_u}, inc="VE")
                ubuf_free[0] = v_t1
                v_xq = p.add("vector", lambda e: e.tensor_scalar(
                    xqtok[:], t1buf[:], 127.0, -128.0, ALU.min, ALU.max),
                    wait={"TE": xqtok_free2[0]}, inc="VE")
                x1_free[bi] = {"VE": v_xq}
                te_tr = None
                for j in range(8):
                    te_tr = p.add("tensor", lambda e, j=j: e.transpose(
                        ptrC[:, j * 128:(j + 1) * 128], xqtok[:, j * 128:(j + 1) * 128],
                        ident[:]),
                        wait={"VE": max(v_xq, trC_free[0])} if j == 0 else None, inc="TE")
                xqtok_free2[0] = te_tr
                v_ev = p.add("vector", lambda e, tt=tt: e.tensor_copy(
                    bigv[:, :, tt * 128:(tt + 1) * 128], ptrC[:, 0:1024]),
                    wait={"TE": te_tr}, inc="VE")
                trC_free[0] = v_ev
                xq_evac_ve[tt] = v_ev

            for tt in range(8):
                emit_xq_tile(tt)

            pg = [pg0, pg1, pg2]
            pu = [pu0, pu1, pu2]
            sgb = [sgb0, sgb1]
            hb = [hb0, hb1, hb2]
            pg_free = [0, 0, 0]
            pu_free = [0, 0, 0]
            sg_free = [0, 0]
            hb_free = [{}, {}, {}]
            prev_hm = 0
            hi = 0

            for ns in range(8):
                for t3 in range(8 + ns * 3, min(NT, 8 + (ns + 1) * 3)):
                    emit_xq_tile(t3)
                need_ve = xq_evac_ve[4 * ns + 3]
                for ii in range(NIC):
                    pi = (ns * NIC + ii) % 3
                    te_g = None
                    for hbk in range(CCH):
                        te_g = p.add("tensor", lambda e, ii=ii, hbk=hbk, ns=ns, pi=pi:
                                     e.matmul(pg[pi][:, 0:512],
                                              wqT_v[:, 0, ii, hbk * 128:(hbk + 1) * 128],
                                              bigv[:, hbk, ns * 512:(ns + 1) * 512],
                                              start=(hbk == 0), stop=(hbk == CCH - 1)),
                                     wait={"VE": need_ve, "sWQT": gd_wqT, "SE": pg_free[pi]}
                                     if hbk == 0 else None, inc="TE")
                    te_u = None
                    for hbk in range(CCH):
                        te_u = p.add("tensor", lambda e, ii=ii, hbk=hbk, ns=ns, pi=pi:
                                     e.matmul(pu[pi][:, 0:512],
                                              wqT_v[:, 1, ii, hbk * 128:(hbk + 1) * 128],
                                              bigv[:, hbk, ns * 512:(ns + 1) * 512],
                                              start=(hbk == 0), stop=(hbk == CCH - 1)),
                                     wait={"VE": pu_free[pi]} if hbk == 0 else None,
                                     inc="TE")
                    si = (ns * NIC + ii) % 2
                    s_sg = p.add("scalar", lambda e, ii=ii, pi=pi, si=si: e.activation(
                        sgb[si][:], pg[pi][:, 0:512], AF.Silu,
                        scale=s1sg_all[:, ii:ii + 1]),
                        wait={"TE": te_g, "VE": max(v_b1e, sg_free[si])}, inc="SE")
                    pg_free[pi] = s_sg
                    hbi = hi % 3
                    hi += 1
                    v_h = p.add("vector", lambda e, ii=ii, pi=pi, si=si, hbi=hbi:
                                e.scalar_tensor_tensor(hb[hbi][:], pu[pi][:, 0:512],
                                                       s1su_all[:, ii:ii + 1], sgb[si][:],
                                                       ALU.mult, ALU.mult),
                                wait=merge({"TE": te_u, "SE": s_sg}, hb_free[hbi]),
                                inc="VE")
                    pu_free[pi] = v_h
                    sg_free[si] = v_h
                    v_hm = p.add("vector", lambda e, hbi=hbi: e.reduce_max(
                        stat[:, 9:10], hb[hbi][:], AX.X, apply_absolute_value=True),
                        wait={"VE": prev_hm}, inc="VE")
                    kk = ns * NIC + ii
                    hmo, hmn = 12 + (kk % 2), 12 + ((kk + 1) % 2)
                    prev_hm = p.add("vector", lambda e, hmo=hmo, hmn=hmn: e.tensor_scalar(
                        stat[:, hmn:hmn + 1], stat[:, 9:10], stat[:, hmo:hmo + 1], None,
                        ALU.max), wait={"VE": v_hm}, keep_self=True, inc="VE")
                    sd_h = p.dma("sync", d_h[ii][:, ns * 512:(ns + 1) * 512], hb[hbi][:],
                                 wait={"VE": v_h}, sem=f"sHW{hbi}")
                    hb_free[hbi] = {f"sHW{hbi}": sd_h}
            ve_pc = p.cnt["VE"]
            sd_pc = {f"sHW{k}": p.cnt.get(f"sHW{k}", 0) for k in range(3)}
            if "xq" in dbg:
                p.dma("sync", dbg["xq"], big[:], wait={"VE": ve_pc}, sem="sDBG")
            if "h0" in dbg:
                p.dma("sync", dbg["h0"], d_h[0], wait=sd_pc, sem="sDBG")
            te_pc = p.cnt["TE"]
        esC.close()

        if "hmax" in dbg:
            p.dma("sync", dbg["hmax"], stat[:, 12:13], wait={"VE": ve_pc}, sem="sDBG")
        if "hall" in dbg:
            for ci2 in range(NIC):
                p.dma("sync", dbg["hall"][ci2 * 128:(ci2 + 1) * 128, :], d_h[ci2],
                      wait=sd_pc, sem="sDBG")
        v_b2 = barrier(stat[:, 12:13], d_cc2_in, d_cc2_out, d_s128b, s2_col, rs2_col,
                       clipd_sb, {"VE": ve_pc})
        if "s2" in dbg:
            p.dma("sync", dbg["s2"], s2_col[:], wait={"VE": v_b2}, sem="sDBG")

        esBig.close()
        p.fence()

        esD = ExitStack()
        if True:
            wdTsb = esD.enter_context(nc.sbuf_tensor("wdTsb", [128, NIC * C], BF16))
            hqs0 = esD.enter_context(nc.sbuf_tensor("hqs0", [128, NIC * 512], BF16))
            hqs1 = esD.enter_context(nc.sbuf_tensor("hqs1", [128, NIC * 512], BF16))
            hch0 = esD.enter_context(nc.sbuf_tensor("hch0", [128, 512], F32))
            hch1 = esD.enter_context(nc.sbuf_tensor("hch1", [128, 512], F32))
            hch2 = esD.enter_context(nc.sbuf_tensor("hch2", [128, 512], F32))
            uch0 = esD.enter_context(nc.sbuf_tensor("uch0", [128, 512], F32))
            uch1 = esD.enter_context(nc.sbuf_tensor("uch1", [128, 512], F32))
            tch0 = esD.enter_context(nc.sbuf_tensor("tch0", [128, 512], F32))
            tch1 = esD.enter_context(nc.sbuf_tensor("tch1", [128, 512], F32))
            outb0 = esD.enter_context(nc.sbuf_tensor("outb0", [128, C], F32))
            outb1 = esD.enter_context(nc.sbuf_tensor("outb1", [128, C], F32))
            pdbank = [esD.enter_context(nc.psum_tensor(f"pdb{i}", [128, 512], F32))
                      for i in range(8)]

            wdT_v = wdTsb[:].rearrange("p (i c) -> p i c", i=NIC)
            wdT_ld = []
            for ci in range(NIC):
                eng2 = "sync" if ci % 2 == 0 else "scalar"
                v = p.dma(eng2, wdT_v[:, ci, :].rearrange("p (c b) -> p c b", b=128),
                          d_wdT[ci].transpose([1, 0, 2]),
                          wait={"sDS0": gd_prep_pb[0], "sDS1": gd_prep_pb[1],
                                "TE": te_pc},
                          sem=f"sWDT{ci}")
                wdT_ld.append(v)

            v_b2 = barrier(stat[:, 12:13], d_cc2_in, d_cc2_out, d_s128b, s2_col,
                           rs2_col, clipd_sb, {"VE": ve_pc})
            if "s2" in dbg:
                p.dma("sync", dbg["s2"], s2_col[:], wait={"VE": v_b2}, sem="sDBG")
            p.fence()

            hqs = [hqs0, hqs1]
            hqsv = [h[:].rearrange("p (i n) -> p i n", n=512) for h in hqs]
            hch = [hch0, hch1, hch2]
            uch = [uch0, uch1]
            tch = [tch0, tch1]
            outb = [outb0, outb1]
            pdk = [(pdbank[0], pdbank[1]), (pdbank[2], pdbank[3]),
                   (pdbank[4], pdbank[5]), (pdbank[6], pdbank[7])]
            hch_free = [{}, {}, {}]
            uch_free = [{}, {}]
            tch_free = [{}, {}]
            hqs_free = [{}, {}]
            pd_free = [0, 0, 0, 0]
            outb_free = [0, 0]
            sd_out_last = None
            v_hq_last_arr = [0] * 8

            def pd_quant(ns):
                si = ns % 2
                v_hq_last = None
                for ci in range(NIC):
                    k = ns * NIC + ci
                    h3 = k % 3
                    u2 = k % 2
                    sdh = p.dma("sync", hch[h3][:],
                                d_h[ci][:, ns * 512:(ns + 1) * 512],
                                wait=merge(hch_free[h3], sd_pc if k < 3 else None),
                                sem=f"sHR{h3}")
                    if k == 0 and "pdu" in dbg:
                        sd_d1 = p.dma("sync", dbg["pdu"][:, 0:512], hch[h3][:],
                                      wait={f"sHR{h3}": sdh}, sem="sDBG")
                        hch_free[h3] = merge(hch_free[h3], {"sDBG": sd_d1})
                    s_u = p.add("scalar", lambda e, h3=h3, u2=u2: e.activation(
                        uch[u2][:], hch[h3][:], AF.Identity, bias=constf[:, 1:2],
                        scale=rs2_col[:]),
                        wait=merge({f"sHR{h3}": sdh, "VE": v_b2}, uch_free[u2]), inc="SE")
                    hch_free[h3] = merge(hch_free[h3], {"SE": s_u})
                    if k == 0 and "pdu" in dbg:
                        sd_d2 = p.dma("sync", dbg["pdu"][:, 512:1024], uch[u2][:],
                                      wait={"SE": s_u}, sem="sDBG")
                        uch_free[u2] = merge(uch_free[u2], {"sDBG": sd_d2})
                    v_t1 = p.add("vector", lambda e, u2=u2: e.tensor_scalar(
                        tch[u2][:], uch[u2][:], MAGIC, 127.0, ALU.subtract, ALU.min),
                        wait=merge({"SE": s_u}, tch_free[u2]), inc="VE")
                    uch_free[u2] = {"VE": v_t1}
                    if k == 0 and "pdu" in dbg:
                        sd_d3 = p.dma("sync", dbg["pdu"][:, 1024:1536], tch[u2][:],
                                      wait={"VE": v_t1}, sem="sDBG")
                        tch_free[u2] = merge(tch_free[u2], {"sDBG": sd_d3})
                    v_hq = p.add("vector", lambda e, si=si, ci=ci, u2=u2:
                        e.tensor_scalar(hqsv[si][:, ci, :], tch[u2][:], -128.0, None,
                                        ALU.max),
                        wait=merge({"VE": v_t1}, hqs_free[si] if ci == 0 else None),
                        keep_self=True, inc="VE")
                    tch_free[u2] = {"VE": v_hq}
                    v_hq_last = v_hq
                v_hq_last_arr[ns] = v_hq_last
                if ns == 0 and "hqs" in dbg:
                    sd_hqd = p.dma("sync", dbg["hqs"], hqs0[:], wait={"VE": v_hq_last},
                                   sem="sDBG")
                    hqs_free[0] = merge(hqs_free[0], {"sDBG": sd_hqd})

            def pd_mms(ns):
                si = ns % 2
                v_hq_last = v_hq_last_arr[ns]
                if ns == 0 and "wdts" in dbg:
                    p.dma("sync", dbg["wdts"], wdTsb[:],
                          wait={f"sWDT{c2}": wdT_ld[c2] for c2 in range(NIC)}, sem="sDBG")
                for tl in range(4):
                    tt = ns * 4 + tl
                    bi = tt % 2
                    sdx = p.dma("gpsimd", x1b[bi][:], d_x1[tt * 128:(tt + 1) * 128, :],
                                wait=merge(x1_free[bi], sd_x1_pb[tt]), sem=f"sX1R{bi}")
                    pdi = tt % 4
                    pda, pdb = pdk[pdi]
                    te_d = None
                    for ci in range(NIC):
                        p.add("tensor", lambda e, ci=ci, si=si, tl=tl, pda=pda: e.matmul(
                            pda[:, 0:512], hqsv[si][:, ci, tl * 128:(tl + 1) * 128],
                            wdT_v[:, ci, 0:512], start=(ci == 0), stop=(ci == NIC - 1)),
                            wait=merge({"VE": max(v_hq_last, pd_free[pdi])}
                                       if ci == 0 else None,
                                       {f"sWDT{ci}": wdT_ld[ci]}), inc="TE")
                        te_d = p.add("tensor", lambda e, ci=ci, si=si, tl=tl, pdb=pdb: e.matmul(
                            pdb[:, 0:512], hqsv[si][:, ci, tl * 128:(tl + 1) * 128],
                            wdT_v[:, ci, 512:1024], start=(ci == 0), stop=(ci == NIC - 1)),
                            inc="TE")
                    if tl == 3:
                        hqs_free[si] = merge({}, {"TE": te_d})
                    v_o1 = p.add("vector", lambda e, bi=bi, pda=pda: e.scalar_tensor_tensor(
                        outb[bi][:, 0:512], pda[:, 0:512], s2_col[:], x1b[bi][:, 0:512],
                        ALU.mult, ALU.add),
                        wait=merge({"TE": te_d, f"sX1R{bi}": sdx},
                                   {f"sOUT{bi}": outb_free[bi]}), inc="VE")
                    v_o2 = p.add("vector", lambda e, bi=bi, pdb=pdb: e.scalar_tensor_tensor(
                        outb[bi][:, 512:1024], pdb[:, 0:512], x1b[bi]
                        if False else s2_col[:], x1b[bi][:, 512:1024],
                        ALU.mult, ALU.add), inc="VE")
                    pd_free[pdi] = v_o2
                    x1_free[bi] = {"VE": v_o2}
                    sd_o = p.dma("sync", d_out[tt * 128:(tt + 1) * 128, :], outb[bi][:],
                                 wait={"VE": v_o2}, sem=f"sOUT{bi}")
                    outb_free[bi] = sd_o

            pd_quant(0)
            for ns in range(8):
                if ns + 1 < 8:
                    pd_quant(ns + 1)
                pd_mms(ns)

            p.add("sync", lambda e: None,
                  wait={"sOUT0": p.cnt["sOUT0"], "sOUT1": p.cnt["sOUT1"]})
        esD.close()

        with nc.allow_low_precision(reason="bf16 matmul operands / quantized codes by design; fp32 accumulation everywhere"):
            p.flush(blk)
    return nc


_NC_CACHE = {}


def _consts():
    ident = np.eye(128, dtype=np.float32).astype(BF)
    ones_col = np.ones((128, 1), np.float32).astype(BF)
    ones_row = np.ones((1, 1024), np.float32).astype(BF)
    ones_f32 = np.ones((1, 128), np.float32)
    eps_row = np.full((1, 16), 1e-6, np.float32).astype(BF)
    ind = np.zeros((8, 16, 128), np.float32)
    for g in range(8):
        ind[g, 2 * g, 0:64] = 1.0
        ind[g, 2 * g + 1, 64:128] = 1.0
    return ident, ones_col, ones_row, ones_f32, eps_row, ind.astype(BF)


def kernel(L0_attn_x, L0_attn_W_qkv, L0_attn_W_f, L0_attn_b_f, L0_attn_W_out,
           L0_attn_b_out, L0_ffn_clip_x, L0_ffn_clip_down, L0_ffn_w_gate,
           L0_ffn_w_up, L0_ffn_w_down):
    x = np.ascontiguousarray(np.asarray(L0_attn_x, dtype=np.float32))
    ident, ones_col, ones_row, ones_f32, eps_row, ind = _consts()
    shared = {
        "wqkv": np.ascontiguousarray(np.asarray(L0_attn_W_qkv, np.float32)),
        "wf": np.ascontiguousarray(np.asarray(L0_attn_W_f, np.float32)),
        "bfv": np.ascontiguousarray(np.asarray(L0_attn_b_f, np.float32).reshape(1, DH)),
        "wout": np.ascontiguousarray(np.asarray(L0_attn_W_out, np.float32)),
        "bout": np.ascontiguousarray(np.asarray(L0_attn_b_out, np.float32).reshape(1, C)),
        "clipx": np.ascontiguousarray(np.broadcast_to(
            np.asarray(L0_ffn_clip_x, np.float32).reshape(1, 1), (128, 1))),
        "clipd": np.ascontiguousarray(np.broadcast_to(
            np.asarray(L0_ffn_clip_down, np.float32).reshape(1, 1), (128, 1))),
        "wg": np.ascontiguousarray(np.asarray(L0_ffn_w_gate, np.float32)),
        "wu": np.ascontiguousarray(np.asarray(L0_ffn_w_up, np.float32)),
        "wd": np.ascontiguousarray(np.asarray(L0_ffn_w_down, np.float32)),
        "ident": ident, "ones_col": ones_col, "ones_row": ones_row,
        "ones_f32": ones_f32, "eps_row": eps_row, "ind": ind,
        "constf": np.ascontiguousarray(np.broadcast_to(
            np.array([[EPS, MAGIC]], np.float32), (128, 2))),
    }
    if "nc" not in _NC_CACHE:
        _NC_CACHE["nc"] = build()
    nc = _NC_CACHE["nc"]
    in_maps = [dict(shared, x=np.ascontiguousarray(x[b])) for b in range(N_CORES)]
    res = run_bass_kernel_spmd(nc, in_maps, core_ids=list(range(N_CORES)))
    out = np.stack([np.asarray(res.results[b]["out"]) for b in range(N_CORES)], axis=0)
    return out
